# version 17
# baseline (speedup 1.0000x reference)
import numpy as np
import ml_dtypes
from contextlib import ExitStack

import concourse.bass as bass
import concourse.mybir as mybir
from concourse.bass_utils import run_bass_kernel_spmd

F32 = mybir.dt.float32
BF16 = mybir.dt.bfloat16
I32 = mybir.dt.int32
AF = mybir.ActivationFunctionType
ALU = mybir.AluOpType
AX = mybir.AxisListType

T = 2048
D = 1024
H = 16
HD = 64
CH = 128
NCH = T // CH
DHID = 4096
NTT = T // 128
TG = 512
NG = T // TG


class Sem:
    __slots__ = ("h", "n", "fn")

    def __init__(self, es, nc, name):
        self.h = es.enter_context(nc.semaphore(name))
        self.n = 0
        self.fn = 0


class KB:
    def __init__(self, n_sub=4, stage=99):
        self.stage = stage
        self.n_sub = n_sub
        self.nc = bass.Bass("TRN2", target_bir_lowering=False)
        self.es = ExitStack()
        self.uid = 0
        self._cms = {}
        self._seq = 0
        self._free_toks = {}
        self.waited = {e: {} for e in ("sync", "tensor", "vector", "scalar", "gpsimd")}

    def _nm(self, p):
        self.uid += 1
        return f"{p}{self.uid}"

    def sb(self, p, shape, dtype, side=None):
        cm = self.nc.sbuf_tensor(self._nm(p), shape, dtype, side=side)
        t = cm.__enter__()
        self._seq += 1
        self._cms[id(t)] = (self._seq, cm)
        return t

    def ps(self, p, shape, dtype=F32):
        cm = self.nc.psum_tensor(self._nm(p), shape, dtype)
        t = cm.__enter__()
        self._seq += 1
        self._cms[id(t)] = (self._seq, cm)
        return t

    def free(self, *ts, after=()):
        self.flush()
        for tok in after:
            if tok is None:
                continue
            sem, v = tok
            cur = self._free_toks.get(id(sem))
            if cur is None or cur[1] < v:
                self._free_toks[id(sem)] = (sem, v)
        items = sorted((self._cms.pop(id(t)) for t in ts), key=lambda x: -x[0])
        for _, cm in items:
            cm.__exit__(None, None, None)

    def dram(self, name, shape, dtype, kind=None):
        if kind is None:
            return self.nc.dram_tensor(name, shape, dtype).ap()
        return self.nc.dram_tensor(name, shape, dtype, kind=kind).ap()

    def _resolve(self, eng, after):
        own = self.esem[eng]
        out = []
        w = self.waited[eng]
        for tok in after:
            if tok is None:
                continue
            sem, v = tok
            if sem is own or v is None:
                continue
            if w.get(id(sem), -1) >= v:
                continue
            out.append((sem, v))
            w[id(sem)] = v
        return out

    _CHAIN_OPS = frozenset((
        "tensor_tensor", "tensor_scalar", "tensor_copy", "tensor_reduce",
        "reduce_max", "reduce_sum", "reciprocal", "memset", "select",
        "scalar_tensor_tensor", "tensor_tensor_reduce", "affine_select",
        "iota", "tensor_add", "tensor_mul", "tensor_sub", "tensor_max",
        "bn_stats", "bn_aggr", "pool_max", "pool_avg", "transpose",
        "tensor_scalar_add", "tensor_scalar_mul", "tensor_scalar_max",
        "tensor_scalar_min", "copy_predicated", "max",
        "activation", "copy", "add", "mul", "sqrt", "square", "sign"))

    class _CountEng:

        class _Fake:
            def then_inc(self, *a):
                return self

        def __init__(self):
            self.n_chain = 0

        def __getattr__(self, name):
            if name in KB._CHAIN_OPS:
                def f(*a, **kw):
                    self.n_chain += 1
                    return KB._CountEng._Fake()
                return f
            return lambda *a, **kw: KB._CountEng._Fake()

    class _ChainEng:

        def __init__(self, kb, eng, e):
            self._kb, self._eng, self._e = kb, eng, e

        def __getattr__(self, name):
            f = getattr(self._e, name)
            if name not in KB._CHAIN_OPS:
                return f
            kb, eng, e = self._kb, self._eng, self._e

            def wrapped(*a, **kw):
                sem = kb.esem[eng]
                w = kb.fwaited[eng]
                if sem.fn > 0 and w.get(id(sem), -1) < sem.fn:
                    e.wait_ge(sem.h, sem.fn)
                    w[id(sem)] = sem.fn
                ins = f(*a, **kw)
                ins.then_inc(sem.h, 1)
                sem.fn += 1
                return ins
            return wrapped

    def cstep(self, eng, after, fn, tag=""):
        sem = self.esem[eng]
        chained = eng in ("vector", "gpsimd", "scalar")
        waits = self._resolve(eng, list(self._free_toks.values()) + list(after))
        if chained:
            ce = KB._CountEng()
            fn(ce)
            assert ce.n_chain > 0, f"cstep {tag}: no chained ops"
            sem.n += ce.n_chain
            self.pending[eng].append(("c", waits, fn, ce.n_chain, tag))
        else:
            sem.n += 1
            self.pending[eng].append(("c1", waits, fn, 1, tag))
        return (sem, sem.n)

    def dma_alldone(self, eng):
        return [(ln, ln.n) for ln in self.dlanes[eng]]

    def tstep(self, after, fn, tag=""):
        return self.dstep("sync", list(after) + self.dma_alldone("gpsimd"),
                          fn, tag)

    def dstep(self, eng, after, fn, tag=""):
        self.dturn[eng] += 1
        lane = self.dlanes[eng][self.dturn[eng] % len(self.dlanes[eng])]
        waits = self._resolve(eng, list(self._free_toks.values()) + list(after)
                              + [(lane, lane.n)])
        ce = KB._CountEng()
        ndma = [0]

        def cdma(**kw):
            ndma[0] += 1
        fn(ce, cdma)
        lane.n += 16 * ndma[0]
        self.pending[eng].append(("d", waits, fn, (lane, ndma[0]), tag))
        return (lane, lane.n)

    def rawstep(self, eng, fn, tag=""):
        self.pending[eng].append(("r", [], fn, None, tag))

    def flush(self):
        for eng in ("sync", "tensor", "vector", "scalar", "gpsimd"):
            plist = self.pending[eng]
            if not plist:
                continue

            def body(e, eng=eng, plist=plist):
                sem = self.esem[eng]
                for kind, waits, fn, info, tag in plist:
                    for (s_, v) in waits:
                        e.wait_ge(s_.h, v)
                    if kind == "c":
                        n0 = sem.fn
                        fn(KB._ChainEng(self, eng, e))
                        assert sem.fn == n0 + info, \
                            f"{tag}: {sem.fn - n0} ops, predicted {info}"
                    elif kind == "c1":
                        last = fn(e)
                        assert last is not None, tag
                        last.then_inc(sem.h, 1)
                        sem.fn += 1
                    elif kind == "d":
                        lane, nd = info
                        cnt = [0]

                        def dma(**kw):
                            e.dma_start(**kw).then_inc(lane.h, 16)
                            cnt[0] += 1
                        fn(e, dma)
                        assert cnt[0] == nd, f"{tag}: dma count {cnt[0]} != {nd}"
                    else:
                        fn(e)
            getattr(self.blk, eng)(body)
            self.pending[eng] = []

    def build(self):
        nc = self.nc
        es = self.es
        self.x_h = self.dram("x", [T, D], F32, kind="ExternalInput")
        w = {}
        for n, shp in [("aWq", [D, D]), ("aWk", [D, D]), ("aWv", [D, D]),
                       ("aWo", [D, D]), ("bWqkv", [3 * D, D]), ("bWout", [D, D]),
                       ("fw1", [DHID, D]), ("fw2", [D, DHID])]:
            w[n] = self.dram(n, shp, F32, kind="ExternalInput")
        for n, ln in [("abq", D), ("abk", D), ("abv", D), ("abo", D),
                      ("bbqkv", 3 * D), ("bbout", D), ("fcw", 3 * DHID), ("fcb", DHID)]:
            w[n] = self.dram(n, [ln], F32, kind="ExternalInput")
        self.w = w
        self.pmask_h = self.dram("pmask", [128, 8], F32, kind="ExternalInput")
        self.hsel_h = self.dram("hsel", [16, 2], F32, kind="ExternalInput")
        self.y_h = self.dram("y", [T, D], F32, kind="ExternalOutput")

        self.ln_sp = self.dram("ln_sp", [T, D], BF16)
        self.attn_sp = self.dram("attn_sp", [T, D], BF16)
        self.k_sp = self.dram("k_sp", [D, T], BF16)
        self.spref_sp = self.dram("spref_sp", [NCH * 128, 512], F32)
        self.zpref_sp = self.dram("zpref_sp", [NCH * 128, 128], F32)
        self.cc_in = self.dram("cc_in", [128, 640], F32)
        self.cc_out = self.dram("cc_out", [8 * 128, 640], F32)
        self.cc2_in = self.dram("cc2_in", [2, D], F32)
        self.cc2_out = self.dram("cc2_out", [16, D], F32)
        self.cc3_in = self.dram("cc3_in", [2, D], F32)
        self.cc3_out = self.dram("cc3_out", [16, D], F32)
        self.wbf = {n: self.dram(n + "_bf", list(self.w[n].shape), BF16)
                    for n in ["aWq", "aWk", "aWv", "aWo", "bWqkv", "bWout"]}
        self.w1i_sp = self.dram("w1i_sp", [DHID, D], BF16)
        self.w2i_sp = self.dram("w2i_sp", [D, DHID], BF16)
        self.xq_sp = self.dram("xq_sp", [2064, D], BF16)
        self.h_sp = self.dram("h_sp", [DHID, T], BF16)
        self.hq_sp = self.dram("hq_sp", [T, DHID], BF16)
        self.x_sp = self.dram("x_sp", [T, D], F32)

        self.esem = {e: Sem(es, nc, "es_" + e)
                     for e in ("sync", "tensor", "vector", "scalar", "gpsimd")}
        self.pending = {e: [] for e in ("sync", "tensor", "vector", "scalar",
                                        "gpsimd")}
        self.fwaited = {e: {} for e in ("sync", "tensor", "vector", "scalar",
                                        "gpsimd")}
        self.dlanes = {"sync": [Sem(es, nc, f"dsy{i}") for i in range(6)],
                       "gpsimd": [Sem(es, nc, f"dgp{i}") for i in range(3)]}
        self.dturn = {"sync": 0, "gpsimd": 0}
        self.ccsem = Sem(es, nc, "ccs")

        self.ones_bf = self.sb("ones_bf", [128, 128], BF16)
        self.onesr_bf = self.sb("onesr_bf", [1, 128], BF16)
        self.zrow_bf = self.sb("zrow_bf", [1, 512], BF16)
        self.one_f32 = self.sb("one_f32", [1, 128], F32)
        self.ident_bf = self.sb("ident_bf", [128, 128], BF16)
        self.ident64 = self.sb("ident64", [128, 64], BF16)
        self.zbig = self.sb("zbig", [128, 512], BF16)
        self.mask4 = self.sb("mask4", [128, 512], F32)
        self.zero16 = self.sb("zero16", [16, D], BF16)
        self.rows_bf = self.sb("rows_bf", [1, 5 * D], BF16)
        self.rows2_bf = self.sb("rows2_bf", [1, 3 * D], BF16)
        self.convc = self.sb("convc", [128, 96], F32)
        self.convb = self.sb("convb", [128, 32], F32)
        self.pmask = self.sb("pmask", [128, 8], F32)
        self.hsel = self.sb("hsel", [16, 2], F32)
        self.cols = self.sb("cols", [128, 40], F32)

        self.stream = [None] * NTT

        with nc.Block("main") as blk:
            self.blk = blk
            self.phase0()
            if self.stage >= 2:
                self.sl1()
            if self.n_sub >= 2:
                self.sl_ffn(first=True)
            if self.n_sub >= 3:
                self.sl3()
            if self.n_sub >= 4:
                self.sl_ffn(first=False)
            self.finish()
            self.flush()
        self.es.close()
        return nc

    def phase0(self):
        rows = self.sb("rows", [1, 5 * D], F32)
        rows2 = self.sb("rows2", [1, 3 * D], F32)
        rowc = self.sb("rowc", [1, 3 * DHID], F32)
        rowcb = self.sb("rowcb", [1, DHID], F32)
        for i in range(NTT):
            self.stream[i] = self.sb("x0", [128, D], F32, side="right")

        def ld(e, dma):
            for i in range(NTT):
                dma(out=self.stream[i][:], in_=self.x_h[128 * i:128 * (i + 1), :])
            dma(out=self.pmask[:], in_=self.pmask_h)
            dma(out=self.hsel[:], in_=self.hsel_h)
            for nm, off in [("abq", 0), ("abk", D), ("abv", 2 * D),
                            ("abo", 3 * D), ("bbout", 4 * D)]:
                dma(out=rows[0:1, off:off + D],
                    in_=self.w[nm].rearrange("(a b) -> a b", a=1))
            dma(out=rows2[0:1, :],
                in_=self.w["bbqkv"].rearrange("(a b) -> a b", a=1))
            dma(out=rowc[0:1, :],
                in_=self.w["fcw"].rearrange("(a b) -> a b", a=1))
            dma(out=rowcb[0:1, :],
                in_=self.w["fcb"].rearrange("(a b) -> a b", a=1))
        self.t_x = self.dstep("sync", [], ld, "p0load")

        def consts(e):
            e.memset(self.ones_bf[:], 1.0)
            e.memset(self.onesr_bf[:], 1.0)
            e.memset(self.zrow_bf[:], 0.0)
            e.memset(self.zbig[:], 0.0)
            e.memset(self.one_f32[:], 1.0)
            e.memset(self.zero16[:], 0.0)
            e.memset(self.ident_bf[:], 1.0)
            e.affine_select(self.ident_bf[:], self.ident_bf[:], [[1, 128]],
                            ALU.is_equal, 0.0, base=0, channel_multiplier=-1)
            e.memset(self.mask4[:], 1.0)
            return e.affine_select(self.mask4[:], self.mask4[:], [[0, 4], [1, 128]],
                                   ALU.is_ge, 0.0, base=0, channel_multiplier=-1)
        self.t_const = self.cstep("gpsimd", [], consts, "consts")

        def casts(e, dma):
            for n in ["aWq", "aWk", "aWv", "aWo"]:
                dma(out=self.wbf[n], in_=self.w[n])
        self.t_acast = self.dstep("gpsimd", [], casts, "acast")

        pc = self.ps("pcols", [128, 32])

        def mkcols(e):
            last = None
            for j in range(8):
                last = e.matmul(pc[:, j:j + 1], rows[0:1, 128 * j:128 * (j + 1)],
                                self.one_f32[0:1, 0:1], start=(j == 0), stop=False)
            for j in range(8):
                last = e.matmul(pc[:, 8 + j:9 + j],
                                rows[0:1, D + 128 * j:D + 128 * (j + 1)],
                                self.one_f32[0:1, 0:1], start=False, stop=False)
            for j in range(16):
                last = e.matmul(pc[:, 16 + j:17 + j],
                                rows2[0:1, 128 * j:128 * (j + 1)],
                                self.one_f32[0:1, 0:1], start=False, stop=(j == 15))
            return last
        tcm = self.cstep("tensor", [self.t_x, self.t_const], mkcols, "mkcols")

        def evcols(e):
            return e.tensor_copy(self.cols[:, 0:32], pc[:])
        self.t_cols = self.cstep("vector", [tcm], evcols, "evcols")

        def bfrows(e):
            e.tensor_copy(self.rows_bf[:], rows[:])
            return e.tensor_copy(self.rows2_bf[:], rows2[:])
        self.t_rowsbf = self.cstep("vector", [self.t_x], bfrows, "bfrows")

        def mkid64(e):
            e.tensor_copy(self.ident64[0:64, :], self.ident_bf[0:64, 0:64])
            e.tensor_copy(self.ident64[64:128, :], self.ident_bf[0:64, 0:64])
        self.t_id64 = self.cstep("vector", [self.t_const], mkid64, "id64")

        self.free(pc, after=[self.t_cols])
        pc2 = self.ps("pconv", [128, 128])
        rcv = rowc[0:1, :].rearrange("a (c p k) -> a c p k", p=128, k=3)

        def mkconv(e):
            last = None
            for cc in range(32):
                for kk in range(3):
                    last = e.matmul(pc2[:, 3 * cc + kk:3 * cc + kk + 1],
                                    rcv[0:1, cc, :, kk],
                                    self.one_f32[0:1, 0:1],
                                    start=(cc == 0 and kk == 0), stop=False)
            for cc in range(32):
                last = e.matmul(pc2[:, 96 + cc:97 + cc],
                                rowcb[0:1, 128 * cc:128 * (cc + 1)],
                                self.one_f32[0:1, 0:1],
                                start=False, stop=(cc == 31))
            return last
        tcv = self.cstep("tensor", [self.t_x, self.t_cols], mkconv, "mkconv")

        def evconv(e):
            e.tensor_copy(self.convc[:], pc2[:, 0:96])
            return e.tensor_copy(self.convb[:], pc2[:, 96:128])
        self.t_conv = self.cstep("vector", [tcv], evconv, "evconv")
        self.free(pc2, rowc, rowcb, rows, rows2,
                  after=[self.t_conv, self.t_rowsbf, self.t_cols])

    def ln_pass(self, dst_sp, spill_x, tag):
        stats = self.sb(tag + "st", [128, 4 * NTT], F32)
        sq = self.sb(tag + "sq", [128, D], F32)
        lnt = [self.sb(tag + "ln", [128, D], BF16) for _ in range(2)]
        t_stats = []
        for t in range(NTT):
            xt = self.stream[t]
            tv = self.cstep("vector", [self.t_x], lambda e, xt=xt, t=t: e.tensor_reduce(
                stats[:, t:t + 1], xt[:], AX.X, ALU.add), "lnred")
            ts = self.cstep("scalar", [self.t_x], lambda e, xt=xt, t=t: e.activation(
                sq[:], xt[:], AF.Square,
                accum_out=stats[:, NTT + t:NTT + t + 1]), "lnsq")
            t_stats += [tv, ts]

        def colmath(e):
            M = stats[:, 0:NTT]
            S2 = stats[:, NTT:2 * NTT]
            V = stats[:, 2 * NTT:3 * NTT]
            e.tensor_scalar(M, M, 1.0 / D, None, ALU.mult)
            e.tensor_scalar(S2, S2, 1.0 / D, None, ALU.mult)
            e.tensor_tensor(V, M, M, ALU.mult)
            e.tensor_tensor(V, S2, V, ALU.subtract)
            return e.tensor_scalar(V, V, 1e-5, None, ALU.add)
        t1 = self.cstep("vector", t_stats, colmath, "lncol")
        t2 = self.cstep("scalar", [t1], lambda e: e.activation(
            stats[:, 2 * NTT:3 * NTT], stats[:, 2 * NTT:3 * NTT], AF.Sqrt), "lnsqrt")

        def recip(e):
            V = stats[:, 2 * NTT:3 * NTT]
            R = stats[:, 3 * NTT:4 * NTT]
            e.reciprocal(R, V)
            e.tensor_tensor(V, stats[:, 0:NTT], R, ALU.mult)
            return e.tensor_scalar(V, V, -1.0, None, ALU.mult)
        t3 = self.cstep("vector", [t2], recip, "lnrec")

        sp_tok = None
        sp_toks = []
        prev_spill = [None, None]
        for t in range(NTT):
            xt = self.stream[t]
            R = stats[:, 3 * NTT + t:3 * NTT + t + 1]
            NMR = stats[:, 2 * NTT + t:2 * NTT + t + 1]
            buf = lnt[t % 2]
            ta = self.cstep("scalar", [t3, prev_spill[t % 2]],
                            lambda e, xt=xt, buf=buf, R=R, NMR=NMR: e.activation(
                                buf[:], xt[:], AF.Identity, scale=R, bias=NMR), "lnap")

            def spl(e, dma, t=t, xt=xt, buf=buf):
                dma(out=dst_sp[128 * t:128 * (t + 1), :], in_=buf[:])
                if spill_x:
                    dma(out=self.x_sp[128 * t:128 * (t + 1), :], in_=xt[:])
            sp_tok = self.dstep("sync", [ta], spl, "lnspl")
            prev_spill[t % 2] = sp_tok
            sp_toks.append(sp_tok)
        self.free(stats, sq, *lnt, after=sp_toks)
        if spill_x:
            self._stream_spilled_toks = sp_toks
            self.free(*[self.stream[t] for t in range(NTT)], after=sp_toks)
            for t in range(NTT):
                self.stream[t] = None
        return sp_toks

    def sl1(self):
        nc = self.nc
        if self.stage < 3:
            self._stage_toks = []
            return
        t_lns = self.ln_pass(self.ln_sp, spill_x=True, tag="l1")
        if self.stage < 4:
            self._stage_toks = t_lns
            return
        self.q_fm = self.sb("q_fm", [128, 8 * T], BF16)
        self.k_fm = self.sb("k_fm", [128, 8 * T], BF16)
        self.v_tm = self.sb("v_tm", [128, NTT * D], BF16)
        self.WqT = self.sb("WqT", [128, 8 * D], BF16)
        self.WkT = self.sb("WkT", [128, 8 * D], BF16)
        self.WvT = self.sb("WvT", [128, 8 * D], BF16)

        def wload(e, dma):
            for dst, src in [(self.WqT, self.wbf["aWq"]), (self.WkT, self.wbf["aWk"]),
                             (self.WvT, self.wbf["aWv"])]:
                dma(out=dst[:].rearrange("p (c n) -> p c n", c=8),
                    in_=src, transpose=True)
        t_w = self.tstep([self.t_acast], wload, "s1wT")

        lnb = [self.sb("lnb", [128, 8 * TG], BF16) for _ in range(2)]
        pp = [self.ps("pp", [128, 512]) for _ in range(2)]
        e2b = [self.sb("e2b", [128, 512], F32) for _ in range(2)]
        rb = [self.sb("rb", [128, 512], F32) for _ in range(2)]
        prev_p = {}
        t_lnld = {}
        ti = 0
        for tb in range(4):
            def lnload(e, dma, tb=tb):
                dma(out=lnb[tb % 2][:].rearrange("p (c n) -> p c n", c=8),
                    in_=self.ln_sp[TG * tb:TG * (tb + 1), :], transpose=True)
            t_lnld[tb] = self.tstep(t_lns[4 * tb:4 * tb + 4]
                                    + [prev_p.get((tb - 2, "last"))],
                                    lnload, "s1lnld")
            lnc = lnb[tb % 2]
            for kind in ("q", "k", "v"):
                WT = {"q": self.WqT, "k": self.WkT, "v": self.WvT}[kind]
                if kind != "v":
                    dst = self.q_fm if kind == "q" else self.k_fm
                    boff = 0 if kind == "q" else 8
                    for oc in range(8):
                        par = ti % 2
                        ti += 1

                        def mm(e, WT=WT, oc=oc, par=par, lnc=lnc):
                            last = None
                            for kc in range(8):
                                last = e.matmul(
                                    pp[par][:],
                                    WT[:, kc * D + 128 * oc:kc * D + 128 * (oc + 1)],
                                    lnc[:, kc * TG:(kc + 1) * TG],
                                    start=(kc == 0), stop=(kc == 7))
                            return last
                        tm = self.cstep("tensor",
                                        [t_lnld[tb], t_w, prev_p.get(par),
                                         self.t_cols], mm, "s1proj")
                        bcol = self.cols[:, boff + oc:boff + oc + 1]
                        te = self.cstep("scalar", [tm], lambda e, par=par, bcol=bcol:
                                        e.activation(e2b[par][:], pp[par][:], AF.Exp,
                                                     bias=bcol), "s1exp")
                        dslice_off = oc * T + TG * tb

                        def stt(e, par=par, bcol=bcol, dst=dst, off=dslice_off):
                            e.tensor_scalar(rb[par][:], pp[par][:], bcol,
                                            0.0, ALU.add, ALU.max)
                            e.scalar_tensor_tensor(
                                dst[:, off:off + TG], e2b[par][:], 1.0, rb[par][:],
                                ALU.min, ALU.add)
                        tg = self.cstep("vector", [te], stt, "s1phi")
                        prev_p[par] = tg
                        prev_p[(tb, "last")] = tg
                else:
                    for tt4 in range(4):
                        tglob = 4 * tb + tt4
                        for half in range(2):
                            par = ti % 2
                            ti += 1

                            def mmv(e, tt4=tt4, half=half, par=par, lnc=lnc):
                                last = None
                                for kc in range(8):
                                    last = e.matmul(
                                        pp[par][:],
                                        lnc[:, kc * TG + 128 * tt4:kc * TG + 128 * (tt4 + 1)],
                                        WT[:, kc * D + 512 * half:kc * D + 512 * (half + 1)],
                                        start=(kc == 0), stop=False)
                                return e.matmul(
                                    pp[par][:], self.onesr_bf[0:1, :],
                                    self.rows_bf[0:1, 2 * D + 512 * half:2 * D + 512 * (half + 1)],
                                    start=False, stop=True)
                            tm = self.cstep("tensor",
                                            [t_lnld[tb], t_w, prev_p.get(par),
                                             self.t_rowsbf, self.t_const],
                                            mmv, "s1vproj")
                            voff = tglob * D + 512 * half
                            tv = self.cstep("vector", [tm], lambda e, par=par, voff=voff:
                                            e.tensor_copy(self.v_tm[:, voff:voff + 512],
                                                          pp[par][:]), "s1vev")
                            prev_p[par] = tv
                            prev_p[(tb, "last")] = tv
        t_proj_done = [prev_p[0], prev_p[1]]
        self.free(*lnb, *pp, *e2b, *rb)

        if self.stage < 5:
            self._stage_toks = t_proj_done
            return
        def kspill(e, dma):
            dma(out=self.k_sp.rearrange("(c p) t -> p c t", p=128),
                in_=self.k_fm[:].rearrange("p (c t) -> p c t", c=8))
        t_ks = self.dstep("sync", t_proj_done, kspill, "kspill")
        self.free(self.WqT, self.WkT, self.WvT, after=t_proj_done)
        self.k_tm = self.sb("k_tm", [128, NTT * D], BF16)

        def ktload(e, dma):
            dma(out=self.k_tm[:].rearrange("p (c n) -> p c n", c=NTT),
                in_=self.k_sp, transpose=True)
        t_kt = self.tstep([t_ks], ktload, "ktload")

        psd = [self.ps("psd", [128, 512]) for _ in range(2)]
        pzd = [self.ps("pzd", [128, 128]) for _ in range(2)]
        spref = [self.sb("spref", [128, 512], F32) for _ in range(2)]
        zpref = [self.sb("zpref", [128, 128], F32) for _ in range(2)]
        t_pref = {}
        t_prefdma = {}
        prev_ev = {}
        for ci in range(NCH):
            par = ci % 2

            def dmm(e, ci=ci, par=par):
                ko = ci * D
                z = self.zrow_bf
                e.matmul(pzd[par][:], z[0:1, 0:128], z[0:1, 0:128],
                         start=True, stop=False)
                e.matmul(psd[par][:], z[0:1, 0:128], z[0:1, 0:512],
                         start=True, stop=False)
                for h in range(H):
                    pb = 64 * (h % 2)
                    e.matmul(
                        psd[par][pb:pb + 64, 64 * (h // 2):64 * (h // 2) + 64],
                        self.k_tm[:, ko + 64 * h:ko + 64 * (h + 1)],
                        self.v_tm[:, ko + 64 * h:ko + 64 * (h + 1)],
                        start=False, stop=False)
                for h in range(H):
                    fc = h // 2
                    pb = 64 * (h % 2)
                    col = 16 * fc + h
                    e.matmul(pzd[par][pb:pb + 64, col:col + 1],
                             self.k_tm[:, ci * D + 64 * h:ci * D + 64 * (h + 1)],
                             self.ones_bf[:, 0:1],
                             start=False, stop=False)
                e.matmul(pzd[par][:], z[0:1, 0:128], z[0:1, 0:128],
                         start=False, stop=True)
                return e.matmul(psd[par][:], z[0:1, 0:128], z[0:1, 0:512],
                                start=False, stop=True)
            tm = self.cstep("tensor", [t_kt, prev_ev.get(par), self.t_const], dmm, "s1dS")

            def pref(e, ci=ci, par=par):
                if ci == 0:
                    e.tensor_copy(spref[par][:], psd[par][:])
                    return e.tensor_copy(zpref[par][:], pzd[par][:])
                e.tensor_tensor(spref[par][:], psd[par][:], spref[1 - par][:], ALU.add)
                return e.tensor_tensor(zpref[par][:], pzd[par][:], zpref[1 - par][:],
                                       ALU.add)
            deps = [tm] if ci == 0 else [tm, t_pref[ci - 1],
                                         t_prefdma.get(ci - 2)]
            t_pref[ci] = self.cstep("vector", deps, pref, "s1pref")
            prev_ev[par] = t_pref[ci]

            def prefspl(e, dma, ci=ci, par=par):
                dma(out=self.spref_sp[128 * ci:128 * (ci + 1), :], in_=spref[par][:])
                dma(out=self.zpref_sp[128 * ci:128 * (ci + 1), :], in_=zpref[par][:])
            t_prefdma[ci] = self.dstep("sync", [t_pref[ci]], prefspl, "s1prefspl")

        if self.stage < 6:
            self._stage_toks = list(t_prefdma.values())
            return
        def ccout(e, dma, par=(NCH - 1) % 2):
            dma(out=self.cc_in[:, 0:512], in_=spref[par][:])
            dma(out=self.cc_in[:, 512:640], in_=zpref[par][:])
        t_cco = self.dstep("gpsimd", [t_pref[NCH - 1]], ccout, "s1cco")

        ccw = self._resolve("gpsimd", [t_cco])
        self.ccsem.n += 1
        ccv = self.ccsem.n

        def docc(e):
            for (s_, v) in ccw:
                e.wait_ge(s_.h, v)
            e.collective_compute(
                "AllGather", ALU.bypass,
                replica_groups=[[0, 1, 2, 3, 4, 5, 6, 7]],
                ins=[self.cc_in], outs=[self.cc_out],
            ).then_inc(self.ccsem.h, 1)
        self.rawstep("gpsimd", docc, "s1cc")
        t_cc = (self.ccsem, ccv)

        self.free(*zpref, *spref, *pzd, *psd, self.k_tm,
                  after=[t_cco, t_pref[NCH - 1]] + list(t_prefdma.values()))
        gath = self.sb("gath", [128, 8 * 640], F32)

        def ccin(e, dma):
            for r in range(8):
                dma(out=gath[:, 640 * r:640 * (r + 1)],
                    in_=self.cc_out[128 * r:128 * (r + 1), :])
        t_gin = self.dstep("gpsimd", [t_cc], ccin, "s1gin")

        sacc = self.sb("sacc", [128, 640], F32)
        stmp = self.sb("stmp", [128, 640], F32)

        def prefsum(e):
            e.tensor_scalar(sacc[:], gath[:, 0:640], self.pmask[:, 0:1], None, ALU.mult)
            last = None
            for r in range(1, 8):
                e.tensor_scalar(stmp[:], gath[:, 640 * r:640 * (r + 1)],
                                self.pmask[:, r:r + 1], None, ALU.mult)
                last = e.tensor_tensor(sacc[:], sacc[:], stmp[:], ALU.add)
            return last
        t_sin = self.cstep("vector", [t_gin, self.t_x], prefsum, "s1prefsum")
        S_in = sacc[:, 0:512]
        z_in = sacc[:, 512:640]

        if self.stage < 7:
            self._stage_toks = [t_sin]
            return
        sprefB = [self.sb("sprefB", [128, 512], F32) for _ in range(2)]
        zprefB = [self.sb("zprefB", [128, 128], F32) for _ in range(2)]
        seff = [self.sb("seff", [128, 512], BF16) for _ in range(2)]
        zeff = self.sb("zeff", [128, 128], F32)
        zhi = [self.sb("zhi", [128, 128], BF16) for _ in range(2)]
        zlo = [self.sb("zlo", [128, 128], BF16) for _ in range(2)]
        msc = [self.sb("msc", [128, 2048], BF16) for _ in range(2)]
        attn_c = [self.sb("attn_c", [128, D], BF16) for _ in range(2)]
        psc = [self.ps("psc", [128, 512]) for _ in range(2)]
        pnum = [self.ps("pnum", [128, 512]) for _ in range(4)]
        pden = [self.ps("pden", [128, 16]) for _ in range(2)]
        rden = self.sb("rden", [128, 16], F32)
        t_attnspl = {}
        t_evnum = None
        t_eff_prev = None
        t_numprev = {}
        t_evprev = {}
        t_mev = {}
        for ci in range(NCH):
            if ci > 0:
                def prefld(e, dma, ci=ci):
                    dma(out=sprefB[ci % 2][:],
                        in_=self.spref_sp[128 * (ci - 1):128 * ci, :])
                    dma(out=zprefB[ci % 2][:],
                        in_=self.zpref_sp[128 * (ci - 1):128 * ci, :])
                t_pl = self.dstep("sync", [t_prefdma[ci - 1], t_eff_prev], prefld,
                                  "s1Bpl")
            else:
                t_pl = None

            def eff(e, ci=ci, cp=ci % 2):
                if ci == 0:
                    e.tensor_copy(seff[cp][:], S_in)
                    e.tensor_copy(zeff[:], z_in)
                else:
                    e.tensor_tensor(seff[cp][:], S_in, sprefB[ci % 2][:], ALU.add)
                    e.tensor_tensor(zeff[:], z_in, zprefB[ci % 2][:], ALU.add)
                e.tensor_copy(zhi[cp][:], zeff[:])
                return e.tensor_tensor(zlo[cp][:], zeff[:], zhi[cp][:],
                                       ALU.subtract)
            t_eff = self.cstep("gpsimd", [t_pl, t_sin, t_numprev.get(ci % 2)],
                               eff, "s1eff")
            t_eff_prev = t_eff
            if self.stage < 72:
                self._stage_toks = [t_eff]
                continue

            mscv = msc[ci % 2][:].rearrange("p (u two c) -> p two u c",
                                            two=2, c=128)
            for hg in range(4):
                par = hg % 2
                heads = [(hg % 2) + 8 * (hg // 2) + 2 * j for j in range(4)]

                def smm(e, ci=ci, heads=heads, par=par):
                    last = None
                    pb = 64 * (heads[0] % 2)
                    for j, h in enumerate(heads):
                        fc = h // 2
                        last = e.matmul(
                            psc[par][:, 128 * j:128 * (j + 1)],
                            self.k_fm[pb:pb + 64, fc * T + CH * ci:fc * T + CH * (ci + 1)],
                            self.q_fm[pb:pb + 64, fc * T + CH * ci:fc * T + CH * (ci + 1)],
                            start=(j == 0), stop=(j == 3))
                    return last
                tsc = self.cstep("tensor", [t_mev.get(par)] + t_proj_done, smm,
                                 "s1sc")

                def mev(e, hg=hg, par=par, mscv=mscv):
                    u0 = 4 * (hg // 2)
                    return e.tensor_tensor(
                        mscv[:, hg % 2, u0:u0 + 4, :],
                        psc[par][:].rearrange("p (a c) -> p a c", c=128),
                        self.mask4[:].rearrange("p (a c) -> p a c", c=128),
                        ALU.mult)
                t_mev[par] = self.cstep("vector", [tsc, self.t_const], mev, "s1mev")
            if self.stage < 73:
                self._stage_toks = [t_mev[0], t_mev[1]]
                continue

            def nummm(e, ci=ci, cp=ci % 2):
                last = None
                for h in range(H):
                    bk = 2 * cp + h // 8
                    first = (h % 8 == 0)
                    lasth = (h % 8 == 7)
                    hcol = 64 * (h % 8)
                    last = e.matmul(
                        pnum[bk][:, hcol:hcol + 64],
                        msc[cp][:, 128 * h:128 * (h + 1)],
                        self.v_tm[:, ci * D + 64 * h:ci * D + 64 * (h + 1)],
                        start=first, stop=False)
                    fc = h // 2
                    pb = 64 * (h % 2)
                    last = e.matmul(
                        pnum[bk][:, hcol:hcol + 64],
                        self.q_fm[pb:pb + 64, fc * T + CH * ci:fc * T + CH * (ci + 1)],
                        seff[cp][pb:pb + 64, 64 * (h // 2):64 * (h // 2) + 64],
                        start=False, stop=lasth)
                for h in range(H):
                    last = e.matmul(pden[cp][:, h:h + 1],
                                    msc[cp][:, 128 * h:128 * (h + 1)],
                                    self.ones_bf[:, 0:1], start=(h == 0),
                                    stop=False)
                for fc in range(8):
                    last = e.matmul(pden[cp][:, 0:16],
                                    self.q_fm[:, fc * T + CH * ci:fc * T + CH * (ci + 1)],
                                    zhi[cp][:, 16 * fc:16 * (fc + 1)],
                                    start=False, stop=False)
                    last = e.matmul(pden[cp][:, 0:16],
                                    self.q_fm[:, fc * T + CH * ci:fc * T + CH * (ci + 1)],
                                    zlo[cp][:, 16 * fc:16 * (fc + 1)],
                                    start=False, stop=(fc == 7))
                return last
            t_num = self.cstep("tensor", [t_mev[0], t_mev[1], t_eff, self.t_const,
                                          t_evprev.get(ci % 2),
                                          t_attnspl.get(ci - 2)], nummm, "s1num")
            t_numprev[ci % 2] = t_num

            def evnum(e, ci=ci, cp=ci % 2):
                e.reciprocal(rden[:], pden[cp][:])
                buf = attn_c[ci % 2]
                rv = rden[:].unsqueeze(2).broadcast_to((128, 16, 64))
                e.tensor_tensor(buf[:, 0:512].rearrange("p (h e) -> p h e", h=8),
                                pnum[2 * cp][:].rearrange("p (h e) -> p h e", h=8),
                                rv[:, 0:8, :], ALU.mult)
                return e.tensor_tensor(
                    buf[:, 512:1024].rearrange("p (h e) -> p h e", h=8),
                    pnum[2 * cp + 1][:].rearrange("p (h e) -> p h e", h=8),
                    rv[:, 8:16, :], ALU.mult)
            t_evnum = self.cstep("vector", [t_num], evnum, "s1evnum")
            t_evprev[ci % 2] = t_evnum

            def aspl(e, dma, ci=ci):
                dma(out=self.attn_sp[128 * ci:128 * (ci + 1), :],
                    in_=attn_c[ci % 2][:])
            t_attnspl[ci] = self.dstep("sync", [t_evnum], aspl, "s1aspl")

        if self.stage < 80:
            self._stage_toks = list(t_attnspl.values()) or \
                list(getattr(self, "_stage_toks", []))
            return
        self.free(self.q_fm, self.k_fm, self.v_tm,
                  *sprefB, *zprefB, *seff, zeff, *zhi, *zlo, *msc, *attn_c,
                  *psc, *pnum, *pden, rden, gath, sacc, stmp,
                  after=[t_attnspl[NCH - 1], t_evnum])

        self.WoT = self.sb("WoT", [128, 8 * D], BF16)
        self.attn_fm = self.sb("attn_fm", [128, 8 * T], BF16)

        def wo_ld(e, dma):
            dma(out=self.WoT[:].rearrange("p (c n) -> p c n", c=8),
                in_=self.wbf["aWo"], transpose=True)
        t_wo = self.tstep([self.t_acast], wo_ld, "s1woT")

        afmv = self.attn_fm[:].rearrange("p (c n) -> p c n", c=8)
        t_afm_h = []
        for hf in range(2):
            t_afm_h.append(self.tstep(
                [t_attnspl[ci] for ci in range(8 * hf, 8 * hf + 8)],
                lambda e, dma, hf=hf: dma(
                    out=afmv[:, :, 1024 * hf:1024 * (hf + 1)],
                    in_=self.attn_sp[1024 * hf:1024 * (hf + 1), :],
                    transpose=True), "s1afm"))
        self.proj_tm_residual(self.attn_fm, t_afm_h, self.WoT, t_wo,
                              self.rows_bf, 3 * D, "s1o", from_dram=True)
        self.free(self.WoT, self.attn_fm, after=self.t_stream_done)

    def proj_tm_residual(self, src_fm, t_src, WT, t_wt, bias_rows, bias_off, tag,
                         from_dram):
        pr = [self.ps(tag + "pr", [128, 512]) for _ in range(4)]
        xload = [self.sb(tag + "xl", [128, D], F32) for _ in range(2)] \
            if from_dram else None
        prev = {}
        for t in range(NTT):
            if from_dram:
                def xld(e, dma, t=t):
                    dma(out=xload[t % 2][:], in_=self.x_sp[128 * t:128 * (t + 1), :])
                t_xl = self.dstep("sync", [prev.get((t - 2, "res"))], xld, tag + "xld")
            newt = self.sb(tag + "nx", [128, D], F32, side="right")
            for half in range(2):
                par = (2 * t + half) % 4

                def mm(e, t=t, half=half, par=par):
                    last = None
                    for kc in range(8):
                        last = e.matmul(
                            pr[par][:],
                            src_fm[:, kc * T + 128 * t:kc * T + 128 * (t + 1)],
                            WT[:, kc * D + 512 * half:kc * D + 512 * (half + 1)],
                            start=(kc == 0), stop=False)
                    return e.matmul(
                        pr[par][:], self.onesr_bf[0:1, :],
                        bias_rows[0:1, bias_off + 512 * half:bias_off + 512 * (half + 1)],
                        start=False, stop=True)
                tsrc = t_src[t // 8] if isinstance(t_src, list) else t_src
                tm = self.cstep("tensor", [tsrc, t_wt, prev.get((t - 2, half)),
                                           self.t_rowsbf], mm, tag + "mm")
                xsrc = xload[t % 2] if from_dram else self.stream[t]
                deps = [tm] + ([t_xl] if from_dram else [])

                def ev(e, t=t, half=half, par=par, newt=newt, xsrc=xsrc):
                    return e.tensor_tensor(newt[:, 512 * half:512 * (half + 1)],
                                           pr[par][:],
                                           xsrc[:, 512 * half:512 * (half + 1)],
                                           ALU.add)
                tev = self.cstep("vector", deps, ev, tag + "ev")
                prev[(t, half)] = tev
                prev[(t, "res")] = tev
            if not from_dram:
                self.free(self.stream[t], after=[prev[(t, "res")]])
            self.stream[t] = newt
        self.t_stream_done = [prev[(t, "res")] for t in range(NTT)]
        self.free(*pr, after=self.t_stream_done)
        if from_dram:
            self.free(*xload, after=self.t_stream_done)

    def quant_weights(self):
        wcol = self.sb("wcol", [128, 4], F32)
        self.convc_s = self.sb("convcs", [128, 96], F32)
        stg = [self.sb("wq", [128, DHID], F32) for _ in range(2)]
        qst = [self.sb("wqs", [128, DHID], BF16) for _ in range(2)]
        g2 = self.sb("wg2", [128, DHID], BF16)
        racc = self.sb("racc", [128, 3], F32)
        scal = self.sb("wsc", [1, 4], F32)
        pt = self.ps("pwq", [128, 4])
        t_r = self.cstep("vector", [], lambda e: e.memset(racc[:], 0.0), "wr0")
        spills = {"fw1": [], "fw2": []}
        t_wcol = None
        for wi, (wn, rows, cols, sp) in enumerate(
                [("fw1", DHID, D, self.w1i_sp), ("fw2", D, DHID, self.w2i_sp)]):
            ntile = rows // 128
            t_ld = {}
            prev = {}
            for i in range(ntile):
                def ld(e, dma, i=i, wn=wn, cols=cols):
                    dma(out=stg[i % 2][:, 0:cols],
                        in_=self.w[wn][128 * i:128 * (i + 1), :])
                t_ld[i] = self.dstep("sync", [prev.get(i % 2)], ld, "wql")

                def red(e, i=i, wi=wi, cols=cols):
                    e.tensor_reduce(racc[:, 2:3], stg[i % 2][:, 0:cols], AX.X,
                                    ALU.add, apply_absolute_value=True)
                    e.tensor_tensor(racc[:, wi:wi + 1], racc[:, wi:wi + 1],
                                    racc[:, 2:3], ALU.add)
                prev[i % 2] = self.cstep("vector", [t_ld[i], t_r], red, "wqr")
            t_p1 = [prev[0], prev[1]]

            onec = self.sb("onec", [128, 1], F32)
            t_oc = self.cstep("vector", [], lambda e: e.memset(onec[:], 1.0), "oc")

            def tot2(e, wi=wi):
                return e.matmul(pt[0:1, wi:wi + 1], racc[:, wi:wi + 1], onec[:],
                                start=True, stop=True)
            t_tot = self.cstep("tensor", t_p1 + [t_oc], tot2, "wqt")

            def mkscal(e, wi=wi, rows=rows, cols=cols):
                e.tensor_scalar(scal[0:1, wi:wi + 1], pt[0:1, wi:wi + 1],
                                1.0 / (rows * cols), 1e-5, ALU.mult, ALU.max)
                e.reciprocal(scal[0:1, 2 + wi:3 + wi], scal[0:1, wi:wi + 1])
            t_sc = self.cstep("vector", [t_tot], mkscal, "wqsc")

            def bcast(e, wi=wi):
                e.matmul(pt[:, 2:3], self.one_f32[0:1, :],
                         scal[0:1, 2 + wi:3 + wi], start=True, stop=False)
                return e.matmul(pt[:, 3:4], self.one_f32[0:1, :],
                                scal[0:1, wi:wi + 1], start=False, stop=True)
            t_bc = self.cstep("tensor", [t_sc], bcast, "wqbc")

            def evc(e, wi=wi):
                e.tensor_copy(wcol[:, wi:wi + 1], pt[:, 2:3])
                e.tensor_copy(wcol[:, 2 + wi:3 + wi], pt[:, 3:4])
            t_wcol = self.cstep("vector", [t_bc], evc, "wqec")

            prev2 = {}
            for i in range(ntile):
                def ld2(e, dma, i=i, wn=wn, cols=cols):
                    dma(out=stg[i % 2][:, 0:cols],
                        in_=self.w[wn][128 * i:128 * (i + 1), :])
                tl = self.dstep("sync", [prev2.get(i % 2, t_p1[i % 2])], ld2, "wql2")

                def qz(e, i=i, wi=wi, cols=cols):
                    ws = wcol[:, wi:wi + 1]
                    e.tensor_scalar(qst[i % 2][:, 0:cols], stg[i % 2][:, 0:cols],
                                    ws, 0.5, ALU.mult, ALU.is_gt)
                    e.tensor_scalar(g2[:, 0:cols], stg[i % 2][:, 0:cols],
                                    ws, -0.5, ALU.mult, ALU.is_lt)
                    e.tensor_tensor(qst[i % 2][:, 0:cols], qst[i % 2][:, 0:cols],
                                    g2[:, 0:cols], ALU.subtract)
                tq = self.cstep("vector", [tl, t_wcol], qz, "wqq")

                def spl(e, dma, i=i, sp=sp, cols=cols):
                    dma(out=sp[128 * i:128 * (i + 1), :], in_=qst[i % 2][:, 0:cols])
                ts_ = self.dstep("sync", [tq], spl, "wqspl")
                prev2[i % 2] = ts_
                spills[wn].append(ts_)
            self.free(onec, after=[t_tot])
        def foldc(e):
            e.tensor_scalar(self.convc_s[:], self.convc[:], wcol[:, 2:3],
                            None, ALU.mult)
        t_cf = self.cstep("vector", [t_wcol, self.t_conv], foldc, "convf")
        self.wcol = wcol
        self._wq_done = (spills["fw1"], spills["fw2"], t_cf, t_wcol)
        self.free(pt, g2, *qst, *stg, racc, scal,
                  after=spills["fw1"] + spills["fw2"] + [t_cf])

    def halo_exchange(self, cin, cout, tag):
        def put(e, dma):
            dma(out=cin[0:1, :], in_=self.stream[0][0:1, :])
            dma(out=cin[1:2, :], in_=self.stream[NTT - 1][127:128, :])
        t_put = self.dstep("gpsimd", getattr(self, "t_stream_done", []), put,
                           tag + "put")

        ccw = self._resolve("gpsimd", [t_put])
        self.ccsem.n += 1
        ccv = self.ccsem.n

        def docc(e):
            for (s_, v) in ccw:
                e.wait_ge(s_.h, v)
            e.collective_compute(
                "AllGather", ALU.bypass, replica_groups=[[0, 1, 2, 3, 4, 5, 6, 7]],
                ins=[cin], outs=[cout]).then_inc(self.ccsem.h, 1)
        self.rawstep("gpsimd", docc, tag + "cc")
        t_cc = (self.ccsem, ccv)
        halo = self.sb(tag + "hx", [2, D], F32)
        gat = self.sb(tag + "g", [16, D], F32)
        t_g = self.dstep("gpsimd", [t_cc],
                         lambda e, dma: dma(out=gat[:], in_=cout), tag + "gin")
        ph = self.ps(tag + "ph", [2, 512])

        def hsel_mm(e):
            e.matmul(ph[:, 0:512], self.hsel[:], gat[:, 0:512],
                     start=True, stop=False)
            return e.matmul(ph[:, 0:512], self.hsel[:], gat[:, 0:512],
                            start=False, stop=True)
        def hmm1(e):
            return e.matmul(ph[:], self.hsel[:], gat[:, 0:512],
                            start=True, stop=True)
        t_m1 = self.cstep("tensor", [t_g, self.t_x], hmm1, tag + "m1")
        t_e1 = self.cstep("vector", [t_m1],
                          lambda e: e.tensor_copy(halo[:, 0:512], ph[:]),
                          tag + "e1")

        def hmm2(e):
            return e.matmul(ph[:], self.hsel[:], gat[:, 512:1024],
                            start=True, stop=True)
        t_m2 = self.cstep("tensor", [t_e1], hmm2, tag + "m2")
        t_e2 = self.cstep("vector", [t_m2],
                          lambda e: e.tensor_copy(halo[:, 512:1024], ph[:]),
                          tag + "e2")
        self.free(ph, gat, after=[t_e2])
        return halo, t_e2

    def sl_ffn(self, first):
        idx = 2 if first else 4
        tg = f"f{idx}"
        if first:
            self.quant_weights()
        w1toks, w2toks, t_cf, t_wcol = self._wq_done

        cin = self.cc2_in if first else self.cc3_in
        cout = self.cc2_out if first else self.cc3_out
        halo, t_halo = self.halo_exchange(cin, cout, tg + "h")

        qs = self.sb(tg + "qs", [128, 8 * (NTT + 1)], F32)
        NS = NTT + 1
        SUM, SSQ, MX, MN, RS, NMRS, DQ, SCR = [k * NS for k in range(8)]
        sq = self.sb(tg + "sq", [128, D], F32)
        i32t = [self.sb(tg + "i32", [128, D], I32) for _ in range(2)]
        xqst = [self.sb(tg + "xq", [128, D], BF16) for _ in range(2)]
        t_z = self.cstep("vector", [], lambda e: e.memset(qs[:], 0.0), tg + "qz")
        t_stats = [t_z]
        for t in range(NTT + 1):
            if t < NTT:
                xt, xsl = self.stream[t], (slice(0, 128), slice(0, D))
                src = xt[:]
                deps = []
            else:
                src = halo[:]
                deps = [t_halo]

            def st_v(e, t=t, src=src):
                e.tensor_reduce(qs[0:src.shape[0], SUM + t:SUM + t + 1], src,
                                AX.X, ALU.add)
                e.tensor_reduce(qs[0:src.shape[0], MX + t:MX + t + 1], src,
                                AX.X, ALU.max)
                e.tensor_reduce(qs[0:src.shape[0], MN + t:MN + t + 1], src,
                                AX.X, ALU.min)
            tv = self.cstep("vector", deps, st_v, tg + "qsv")

            def st_s(e, t=t, src=src):
                e.activation(sq[0:src.shape[0], :], src, AF.Square,
                             accum_out=qs[0:src.shape[0], SSQ + t:SSQ + t + 1])
            ts_ = self.cstep("scalar", deps + [t_z], st_s, tg + "qss")
            t_stats += [tv, ts_]

        def colmath(e):
            S_, Q_, X_, N_ = qs[:, SUM:SUM + NS], qs[:, SSQ:SSQ + NS], \
                qs[:, MX:MX + NS], qs[:, MN:MN + NS]
            RS_, NM_, DQ_, SC_ = qs[:, RS:RS + NS], qs[:, NMRS:NMRS + NS], \
                qs[:, DQ:DQ + NS], qs[:, SCR:SCR + NS]
            e.tensor_scalar(S_, S_, 1.0 / D, None, ALU.mult)
            e.tensor_scalar(Q_, Q_, 1.0 / D, None, ALU.mult)
            e.tensor_tensor(SC_, S_, S_, ALU.mult)
            e.tensor_tensor(Q_, Q_, SC_, ALU.subtract)
            e.tensor_scalar(Q_, Q_, 1e-5, None, ALU.add)
        t_c1 = self.cstep("vector", t_stats, colmath, tg + "qc1")
        t_c2 = self.cstep("scalar", [t_c1], lambda e: e.activation(
            qs[:, SSQ:SSQ + NS], qs[:, SSQ:SSQ + NS], AF.Sqrt), tg + "qc2")

        def colmath2(e):
            S_, Q_, X_, N_ = qs[:, SUM:SUM + NS], qs[:, SSQ:SSQ + NS], \
                qs[:, MX:MX + NS], qs[:, MN:MN + NS]
            RS_, NM_, DQ_, SC_ = qs[:, RS:RS + NS], qs[:, NMRS:NMRS + NS], \
                qs[:, DQ:DQ + NS], qs[:, SCR:SCR + NS]
            e.reciprocal(SC_, Q_)
            e.tensor_tensor(X_, X_, S_, ALU.subtract)
            e.tensor_tensor(N_, S_, N_, ALU.subtract)
            e.tensor_tensor(X_, X_, N_, ALU.max)
            e.tensor_tensor(X_, X_, SC_, ALU.mult)
            e.tensor_scalar(X_, X_, 1e-5, None, ALU.max)
            e.tensor_scalar(DQ_, X_, 1.0 / 127.0, None, ALU.mult)
            e.reciprocal(N_, X_)
            e.tensor_scalar(N_, N_, 127.0, None, ALU.mult)
            e.tensor_tensor(RS_, SC_, N_, ALU.mult)
            e.tensor_tensor(NM_, S_, RS_, ALU.mult)
            e.tensor_scalar(NM_, NM_, -1.0, None, ALU.mult)
        t_c3 = self.cstep("vector", [t_c2], colmath2, tg + "qc3")

        t_xq = []
        prevq = {}
        for t in range(NTT + 1):
            par = t % 2
            if t < NTT:
                src = self.stream[t][:]
                rows = 128
            else:
                src = halo[:]
                rows = 2

            def ap1(e, t=t, src=src, par=par, rows=rows):
                e.activation(i32t[par][0:rows, :], src, AF.Identity,
                             scale=qs[0:rows, RS + t:RS + t + 1],
                             bias=qs[0:rows, NMRS + t:NMRS + t + 1])
            ta = self.cstep("scalar", [t_c3, prevq.get(par)], ap1, tg + "qt1")

            def ap3(e, t=t, par=par, rows=rows):
                e.tensor_scalar(xqst[par][0:rows, :], i32t[par][0:rows, :],
                                qs[0:rows, DQ + t:DQ + t + 1], None, ALU.mult)
            tc = self.cstep("vector", [ta], ap3, tg + "qdq")

            def spl(e, dma, t=t, par=par):
                if t < NTT:
                    dma(out=self.xq_sp[128 * t + 1:128 * (t + 1) + 1, :],
                        in_=xqst[par][:])
                    dma(out=self.x_sp[128 * t:128 * (t + 1), :],
                        in_=self.stream[t][:])
                else:
                    dma(out=self.xq_sp[0:1, :], in_=xqst[par][0:1, :])
                    dma(out=self.xq_sp[2049:2050, :], in_=xqst[par][1:2, :])
                    dma(out=self.xq_sp[2050:2064, :], in_=self.zero16[0:14, :])
            td = self.dstep("sync", [tc] + ([self.t_const] if t == NTT else []),
                            spl, tg + "qspl")
            prevq[par] = td
            t_xq.append(td)
        self.free(*xqst, *i32t, sq, after=t_xq)
        self.free(*[self.stream[t] for t in range(NTT)], after=t_xq)
        for t in range(NTT):
            self.stream[t] = None

        dqh = self.sb(tg + "dqh", [128, 2 * NTT], F32)
        w1iT = self.sb(tg + "w1T", [128, 8 * DHID], BF16)

        def w1ld(e, dma):
            dma(out=w1iT[:].rearrange("p (c n) -> p c n", c=8),
                in_=self.w1i_sp, transpose=True)
        t_w1 = self.tstep(w1toks, w1ld, tg + "w1T")

        htm = [self.sb(tg + "htm", [128, DHID], BF16) for _ in range(2)]
        i32h = [self.sb(tg + "i32h", [128, DHID], I32) for _ in range(2)]
        hqst = [self.sb(tg + "hqst", [128, DHID], BF16) for _ in range(2)]
        xqg = self.sb(tg + "xqg", [128, 8 * 528], BF16)
        hb = [self.sb(tg + "hb", [128, 514], BF16) for _ in range(2)]
        cb1 = [self.sb(tg + "cb1", [128, 512], F32) for _ in range(2)]
        cb2 = [self.sb(tg + "cb2", [128, 512], F32) for _ in range(2)]
        cb3 = [self.sb(tg + "cb3", [128, 512], F32) for _ in range(2)]
        cbo = [self.sb(tg + "cbo", [128, 512], F32) for _ in range(2)]
        hst = self.sb(tg + "hst", [128, 4 * 512], BF16)
        pm = [self.ps(tg + "pm", [128, 512]) for _ in range(2)]
        ptl = [self.ps(tg + "ptl", [128, 2]) for _ in range(2)]
        t_hspl = []
        t_hq = []
        t_bc_prev = {}
        t_xqld_prev = None
        t_spl4 = None
        for g in range(NG):
            g_spl = []
            t_xqld = self.tstep(t_xq + [t_xqld_prev], lambda e, dma, g=g: dma(
                out=xqg[:].rearrange("p (c n) -> p c n", c=8),
                in_=self.xq_sp[TG * g:TG * g + 528, :], transpose=True),
                tg + "xql")
            prev_ev = {}
            prev_sil = {}
            for hc in range(32):
                par = hc % 2

                def w1mm2(e, g=g, hc=hc, par=par):
                    last = None
                    for kc in range(8):
                        e.matmul(pm[par][:],
                                 w1iT[:, kc * DHID + 128 * hc:kc * DHID + 128 * (hc + 1)],
                                 xqg[:, kc * 528:kc * 528 + 512],
                                 start=(kc == 0), stop=(kc == 7))
                    for kc in range(8):
                        last = e.matmul(
                            ptl[par][:],
                            w1iT[:, kc * DHID + 128 * hc:kc * DHID + 128 * (hc + 1)],
                            xqg[:, kc * 528 + 512:kc * 528 + 514],
                            start=(kc == 0), stop=(kc == 7))
                    return last
                tm = self.cstep("tensor", [t_xqld, t_w1, prev_ev.get(par),
                                           prev_sil.get(par)], w1mm2, tg + "w1m")

                def hev(e, hc=hc, par=par):
                    e.tensor_copy(hb[par][:, 0:512], pm[par][:])
                    e.tensor_copy(hb[par][:, 512:514], ptl[par][:])
                    e.tensor_scalar(cb1[par][:], hb[par][:, 0:512],
                                    self.convc_s[:, 3 * hc:3 * hc + 1],
                                    None, ALU.mult)
                    e.scalar_tensor_tensor(cb2[par][:], hb[par][:, 1:513],
                                           self.convc_s[:, 3 * hc + 1:3 * hc + 2],
                                           cb1[par][:], ALU.mult, ALU.add)
                    e.scalar_tensor_tensor(cbo[par][:], hb[par][:, 2:514],
                                           self.convc_s[:, 3 * hc + 2:3 * hc + 3],
                                           cb2[par][:], ALU.mult, ALU.add)
                tc3 = self.cstep("vector", [tm, t_cf], hev, tg + "hev")
                prev_ev[par] = tc3

                def sil(e, hc=hc, par=par):
                    e.activation(hst[:, 512 * (hc % 4):512 * (hc % 4 + 1)],
                                 cbo[par][:], AF.Silu,
                                 bias=self.convb[:, hc:hc + 1])
                ts_ = self.cstep("scalar", [tc3, t_spl4], sil, tg + "sil")
                prev_sil[par] = ts_
                if hc % 4 == 3:
                    q4 = hc // 4

                    def spl4(e, dma, g=g, q4=q4):
                        dma(out=self.h_sp[512 * q4:512 * (q4 + 1),
                                          TG * g:TG * (g + 1)].rearrange(
                                              "(c p) n -> p c n", p=128),
                            in_=hst[:].rearrange("p (c n) -> p c n", c=4))
                    t_spl4 = self.dstep("sync", [ts_], spl4, tg + "hspl")
                    t_hspl.append(t_spl4)
                    g_spl.append(t_spl4)
            t_xqld_prev = tm
            for tt in range(4):
                t = 4 * g + tt
                par = t % 2
                t_hl = self.tstep(g_spl + [t_bc_prev.get(par)],
                                  lambda e, dma, t=t, par=par: dma(
                    out=htm[par][:], in_=self.h_sp[:, 128 * t:128 * (t + 1)],
                    transpose=True), tg + "htl")

                def amax(e, t=t, par=par):
                    e.tensor_reduce(dqh[:, NTT + t:NTT + t + 1], htm[par][:],
                                    AX.X, ALU.max, apply_absolute_value=True)
                    e.tensor_scalar(dqh[:, NTT + t:NTT + t + 1],
                                    dqh[:, NTT + t:NTT + t + 1], 1e-5, None,
                                    ALU.max)
                    e.tensor_scalar(dqh[:, t:t + 1],
                                    dqh[:, NTT + t:NTT + t + 1],
                                    1.0 / 127.0, None, ALU.mult)
                    e.reciprocal(dqh[:, NTT + t:NTT + t + 1],
                                 dqh[:, NTT + t:NTT + t + 1])
                    e.tensor_scalar(dqh[:, NTT + t:NTT + t + 1],
                                    dqh[:, NTT + t:NTT + t + 1], 127.0, None,
                                    ALU.mult)
                t_am = self.cstep("vector", [t_hl], amax, tg + "amax")

                def rh1(e, t=t, par=par):
                    e.activation(i32h[par][:], htm[par][:], AF.Identity,
                                 scale=dqh[:, NTT + t:NTT + t + 1])
                    e.copy(hqst[par][:], i32h[par][:])
                t_r1 = self.cstep("scalar", [t_am], rh1, tg + "rh1")
                t_sp = self.dstep("sync", [t_r1], lambda e, dma, t=t, par=par:
                                  dma(out=self.hq_sp[128 * t:128 * (t + 1), :],
                                      in_=hqst[par][:]), tg + "hqspl")
                t_hq.append(t_sp)
                t_bc_prev[par] = t_sp
        self.free(xqg, *hb, *cb1, *cb2, *cb3, *cbo, hst, *pm, *ptl,
                  after=t_hspl + [prev_sil[0], prev_sil[1]])
        self.free(*hqst, *i32h, *htm, after=t_hq)
        self.free(w1iT, after=t_hspl)

        w2iT = self.sb(tg + "w2T", [128, 32 * D], BF16)

        def w2ld(e, dma):
            dma(out=w2iT[:].rearrange("p (c n) -> p c n", c=32),
                in_=self.w2i_sp, transpose=True)
        t_w2 = self.tstep(w2toks, w2ld, tg + "w2T")
        hqfm = self.sb(tg + "hqfm", [128, 32 * TG], BF16)
        hqv = hqfm[:].rearrange("p (c n) -> p c n", c=32)
        xld = [self.sb(tg + "xld", [128, D], F32) for _ in range(2)]
        dqq = self.sb(tg + "dqq", [128, NTT], F32)
        pd = [self.ps(tg + "pd", [128, 512]) for _ in range(4)]
        t_dqq = self.cstep("vector", [t_wcol, t_c3] + t_hq, lambda e:
                           e.tensor_scalar(dqq[:, 0:NTT], dqh[:, 0:NTT],
                                           self.wcol[:, 3:4], None, ALU.mult),
                           tg + "dqq")
        newts = {}
        prev_hql = None
        prev_evd = {}
        for g in range(NG):
            t_hqls = []
            for qq in range(4):
                t_hqls.append(self.tstep(
                    t_hq + [prev_hql],
                    lambda e, dma, g=g, qq=qq: dma(
                        out=hqv[:, 8 * qq:8 * (qq + 1), :],
                        in_=self.hq_sp[TG * g:TG * (g + 1),
                                       1024 * qq:1024 * (qq + 1)],
                        transpose=True), tg + "hql"))
            for tt in range(4):
                t = 4 * g + tt

                def xl(e, dma, t=t, par=t % 2):
                    dma(out=xld[par][:], in_=self.x_sp[128 * t:128 * (t + 1), :])
                t_xl = self.dstep("sync", [prev_evd.get(t % 2)], xl, tg + "dxl")
                newt = self.sb(tg + "nx", [128, D], F32, side="right")
                tevs = []
                for half in range(2):
                    par = (2 * t + half) % 4

                    tms = []
                    for qq in range(4):
                        def dmmq(e, tt=tt, half=half, par=par, qq=qq):
                            last = None
                            for kc in range(8 * qq, 8 * (qq + 1)):
                                last = e.matmul(
                                    pd[par][:],
                                    hqfm[:, kc * TG + 128 * tt:kc * TG + 128 * (tt + 1)],
                                    w2iT[:, kc * D + 512 * half:kc * D + 512 * (half + 1)],
                                    start=(kc == 8 * qq and qq == 0),
                                    stop=(kc == 8 * qq + 7 and qq == 3))
                            return last
                        tms.append(self.cstep(
                            "tensor", [t_hqls[qq], t_w2,
                                       prev_evd.get((t - 2, half))],
                            dmmq, tg + "w2m"))
                    tm = tms[-1]

                    def evd(e, t=t, half=half, par=par, newt=newt):
                        e.scalar_tensor_tensor(
                            newt[:, 512 * half:512 * (half + 1)], pd[par][:],
                            dqq[:, t:t + 1], xld[t % 2][:, 512 * half:512 * (half + 1)],
                            ALU.mult, ALU.add)
                    tev = self.cstep("vector", [tm, t_xl, t_dqq], evd, tg + "evd")
                    prev_evd[(t, half)] = tev
                    tevs.append(tev)
                prev_evd[t % 2] = tevs[-1]
                self.stream[t] = newt
                if not first and self.n_sub == 4:
                    tevs = [self.dstep("sync", tevs, lambda e, dma, t=t, newt=newt:
                                       dma(out=self.y_h[128 * t:128 * (t + 1), :],
                                           in_=newt[:]), tg + "ydma")]
                newts[t] = tevs
            prev_hql = prev_evd[(4 * g + 3, 1)]
        self.t_stream_done = [newts[t][-1] for t in range(NTT)]
        self.free(*pd, dqq, *xld, hqfm, after=self.t_stream_done)
        self.free(w2iT, after=self.t_stream_done)
        self.free(dqh, qs, halo, after=[t_dqq] + t_xq)

    def sl3(self):
        tg = "s3"
        def bcast(e, dma):
            dma(out=self.wbf["bWqkv"], in_=self.w["bWqkv"])
            dma(out=self.wbf["bWout"], in_=self.w["bWout"])
        t_bc = self.dstep("gpsimd", [], bcast, tg + "cast")

        t_ln3 = self.ln_pass(self.ln_sp, spill_x=False, tag="l3")

        WqkvT = self.sb(tg + "WqkvT", [128, 8 * 3 * D], BF16)
        WoutT = self.sb(tg + "WoutT", [128, 8 * D], BF16)

        def wld(e, dma):
            dma(out=WqkvT[:].rearrange("p (c n) -> p c n", c=8),
                in_=self.wbf["bWqkv"], transpose=True)
            dma(out=WoutT[:].rearrange("p (c n) -> p c n", c=8),
                in_=self.wbf["bWout"], transpose=True)
        t_w = self.tstep([t_bc], wld, tg + "wT")

        ln3g = self.sb(tg + "lng", [128, 8 * TG], BF16)
        qg = self.sb(tg + "qg", [128, 8 * TG], BF16)
        kg = self.sb(tg + "kg", [128, 8 * TG], BF16)
        vg = self.sb(tg + "vg", [128, 4 * D], BF16)
        esb = self.sb(tg + "es", [128, 256], F32)
        sm3 = self.sb(tg + "sm", [128, 8], F32)
        attb = self.sb(tg + "att", [128, 256], BF16)
        attT = self.sb(tg + "attT", [128, 256], BF16)
        og = self.sb(tg + "og", [128, 8 * TG], BF16)
        xld3 = [self.sb(tg + "xl", [128, D], F32) for _ in range(2)]
        pA = [self.ps(tg + "pA", [128, 512]) for _ in range(2)]
        psc = [self.ps(tg + "psc", [128, 512]) for _ in range(2)]
        ptE = self.ps(tg + "ptE", [128, 256], BF16)
        ptO = self.ps(tg + "ptO", [128, 256], BF16)
        po = [self.ps(tg + "po", [128, 512]) for _ in range(2)]

        prev_lnld = None
        prev_pA = {}
        prev_psc = {}
        prev_po = {}
        prev_pt = None
        prev_res = {}
        for g in range(NG):
            t_lng = self.tstep(t_ln3 + [prev_lnld], lambda e, dma, g=g: dma(
                out=ln3g[:].rearrange("p (c n) -> p c n", c=8),
                in_=self.ln_sp[TG * g:TG * (g + 1), :], transpose=True),
                tg + "lnl")
            t_qkv = []
            for kind in range(3):
                if kind < 2:
                    dst = qg if kind == 0 else kg
                    fo0 = D * kind
                    for oc in range(8):
                        par = oc % 2

                        def mm(e, fo0=fo0, oc=oc, par=par):
                            last = None
                            for kc in range(8):
                                last = e.matmul(
                                    pA[par][:],
                                    WqkvT[:, kc * 3 * D + fo0 + 128 * oc:
                                          kc * 3 * D + fo0 + 128 * (oc + 1)],
                                    ln3g[:, kc * TG:(kc + 1) * TG],
                                    start=(kc == 0), stop=(kc == 7))
                            return last
                        tm = self.cstep("tensor", [t_lng, t_w, prev_pA.get(par),
                                                   self.t_cols], mm, tg + "qkm")
                        bcol = self.cols[:, 16 + 8 * kind + oc:17 + 8 * kind + oc]

                        def ev(e, dst=dst, oc=oc, par=par, bcol=bcol):
                            e.tensor_scalar(dst[:, oc * TG:(oc + 1) * TG],
                                            pA[par][:], bcol, None, ALU.add)
                        tev = self.cstep("vector", [tm], ev, tg + "qke")
                        prev_pA[par] = tev
                        t_qkv.append(tev)
                else:
                    for tt in range(4):
                        for half in range(2):
                            par = (tt * 2 + half) % 2

                            def mmv(e, tt=tt, half=half, par=par):
                                for kc in range(8):
                                    e.matmul(
                                        pA[par][:],
                                        ln3g[:, kc * TG + 128 * tt:
                                             kc * TG + 128 * (tt + 1)],
                                        WqkvT[:, kc * 3 * D + 2 * D + 512 * half:
                                              kc * 3 * D + 2 * D + 512 * (half + 1)],
                                        start=(kc == 0), stop=False)
                                return e.matmul(
                                    pA[par][:], self.onesr_bf[0:1, :],
                                    self.rows2_bf[0:1, 2 * D + 512 * half:
                                                  2 * D + 512 * (half + 1)],
                                    start=False, stop=True)
                            tm = self.cstep("tensor",
                                            [t_lng, t_w, prev_pA.get(par),
                                             self.t_rowsbf, self.t_const],
                                            mmv, tg + "vm")

                            def evv(e, tt=tt, half=half, par=par):
                                e.copy(vg[:, tt * D + 512 * half:
                                          tt * D + 512 * (half + 1)], pA[par][:])
                            tev = self.cstep("scalar", [tm], evv, tg + "vev")
                            prev_pA[par] = tev
                            t_qkv.append(tev)

            t_oev = []
            for h in range(H):
                par = h % 2
                pb = 64 * (h % 2)
                fc = h // 2

                def smm(e, h=h, par=par, pb=pb, fc=fc):
                    last = None
                    for b in range(8):
                        last = e.matmul(
                            psc[par][64 * (b % 2):64 * (b % 2) + 64,
                                     64 * (b // 2):64 * (b // 2) + 64],
                            qg[pb:pb + 64, fc * TG + 64 * b:fc * TG + 64 * (b + 1)],
                            kg[pb:pb + 64, fc * TG + 64 * b:fc * TG + 64 * (b + 1)],
                            start=(b < 2), stop=(b >= 6))
                    return last
                def smm2(e, h=h, par=par, pb=pb, fc=fc):
                    z = self.zbig
                    e.matmul(psc[par][:], z[pb:pb + 64, 0:128], z[pb:pb + 64, 0:512],
                             start=True, stop=False)
                    for b in range(8):
                        e.matmul(
                            psc[par][64 * (b % 2):64 * (b % 2) + 64,
                                     64 * (b // 2):64 * (b // 2) + 64],
                            qg[pb:pb + 64, fc * TG + 64 * b:fc * TG + 64 * (b + 1)],
                            kg[pb:pb + 64, fc * TG + 64 * b:fc * TG + 64 * (b + 1)],
                            start=False, stop=False)
                    return e.matmul(psc[par][:], z[pb:pb + 64, 0:128],
                                    z[pb:pb + 64, 0:512],
                                    start=False, stop=True)
                tsc = self.cstep("tensor", [prev_psc.get(par), self.t_const]
                                 + t_qkv, smm2, tg + "sc")

                def vmax(e, par=par):
                    e.tensor_reduce(sm3[:, 0:1], psc[par][:, 0:256], AX.X,
                                    ALU.max)
                    e.tensor_scalar(sm3[:, 1:2], sm3[:, 0:1], -0.125, None,
                                    ALU.mult)
                tmx = self.cstep("vector", [tsc], vmax, tg + "mx")

                def vexp(e, par=par):
                    e.activation(esb[:], psc[par][:, 0:256], AF.Exp,
                                 scale=0.125, bias=sm3[:, 1:2])
                tex = self.cstep("scalar", [tmx], vexp, tg + "exp")

                def vsum(e):
                    e.tensor_reduce(sm3[:, 4:8], esb[:].rearrange(
                        "p (s k) -> p s k", k=64), AX.X, ALU.add)
                    e.reciprocal(sm3[:, 4:8], sm3[:, 4:8])
                    rv = sm3[:, 4:8].unsqueeze(2).broadcast_to((128, 4, 64))
                    e.tensor_tensor(attb[:].rearrange("p (s k) -> p s k", k=64),
                                    esb[:].rearrange("p (s k) -> p s k", k=64),
                                    rv, ALU.mult)
                tat = self.cstep("vector", [tex], vsum, tg + "sm")

                def trans(e, h=h):
                    last = None
                    for b in range(8):
                        pt = ptE if b % 2 == 0 else ptO
                        last = e.transpose(
                            pt[64 * (b % 2):64 * (b % 2) + 64,
                               64 * (b // 2):64 * (b // 2) + 64],
                            attb[64 * (b % 2):64 * (b % 2) + 64,
                                 64 * (b // 2):64 * (b // 2) + 64],
                            self.ident64[64 * (b % 2):64 * (b % 2) + 64, :])
                    return last
                ttr = self.cstep("tensor", [tat, prev_pt, self.t_id64], trans,
                                 tg + "tr")

                def evtr(e):
                    e.tensor_copy(attT[0:64, :], ptE[0:64, :])
                    e.tensor_copy(attT[64:128, :], ptO[64:128, :])
                t_aT = self.cstep("vector", [ttr], evtr, tg + "evT")
                prev_pt = t_aT

                def omm(e, h=h, par=par, fc=fc):
                    z = self.zbig
                    e.matmul(po[par][:], z[:, 0:128], z[:, 0:512],
                             start=True, stop=False)
                    for j, b in enumerate([0, 2, 4, 6, 1, 3, 5, 7]):
                        if j == 4:
                            e.matmul(po[par][:, 0:1], z[:, 0:128], z[:, 0:1],
                                     start=False, stop=False)
                        pb_b = 64 * (b % 2)
                        e.matmul(
                            po[par][64 * (h % 2):64 * (h % 2) + 64,
                                    64 * b:64 * (b + 1)],
                            vg[pb_b:pb_b + 64,
                               (b // 2) * D + 64 * h:(b // 2) * D + 64 * (h + 1)],
                            attT[pb_b:pb_b + 64, 64 * (b // 2):64 * (b // 2) + 64],
                            start=False, stop=False)
                    return e.matmul(po[par][:, 0:1], z[:, 0:128], z[:, 0:1],
                                    start=False, stop=True)
                tom = self.cstep("tensor", [t_aT, prev_po.get(par)], omm,
                                 tg + "om")

                def evo(e, h=h, par=par, fc=fc):
                    e.tensor_copy(og[64 * (h % 2):64 * (h % 2) + 64,
                                     fc * TG:(fc + 1) * TG],
                                  po[par][64 * (h % 2):64 * (h % 2) + 64, :])
                teo = self.cstep("vector", [tom], evo, tg + "evo")
                prev_po[par] = teo
                prev_psc[par] = tat
                t_oev.append(teo)

            for tt in range(4):
                t = 4 * g + tt
                par = tt % 2

                def xcp(e, t=t, par=par):
                    e.tensor_copy(xld3[par][:], self.stream[t][:])
                t_xc = self.cstep("gpsimd", [prev_res.get(par)], xcp, tg + "xc")
                newt = self.stream[t]
                tevs = []
                for half in range(2):
                    parp = half

                    def pmm(e, tt=tt, half=half, parp=parp):
                        last = None
                        for kc in range(8):
                            last = e.matmul(
                                pA[parp][:],
                                og[:, kc * TG + 128 * tt:kc * TG + 128 * (tt + 1)],
                                WoutT[:, kc * D + 512 * half:kc * D + 512 * (half + 1)],
                                start=(kc == 0), stop=False)
                        return e.matmul(
                            pA[parp][:], self.onesr_bf[0:1, :],
                            self.rows_bf[0:1, 4 * D + 512 * half:
                                         4 * D + 512 * (half + 1)],
                            start=False, stop=True)
                    tm = self.cstep("tensor", [prev_pA.get(parp), self.t_rowsbf]
                                    + t_oev, pmm, tg + "pm")

                    def pev(e, t=t, half=half, parp=parp, par=par, newt=newt):
                        e.tensor_tensor(newt[:, 512 * half:512 * (half + 1)],
                                        pA[parp][:],
                                        xld3[par][:, 512 * half:512 * (half + 1)],
                                        ALU.add)
                    tev = self.cstep("vector", [tm, t_xc], pev, tg + "pev")
                    prev_pA[parp] = tev
                    tevs.append(tev)
                prev_res[par] = tevs[-1]
            prev_lnld = t_qkv[-1]
        self.t_stream_done = [prev_res[0], prev_res[1]]
        self.free(*po, ptO, ptE, *psc, *pA, *xld3, og, attT, attb, sm3, esb,
                  vg, kg, qg, ln3g, WoutT, WqkvT,
                  after=self.t_stream_done + list(prev_pA.values()))

    def finish(self):
        if self.n_sub == 4:
            w1 = self._resolve("sync", self.t_stream_done)
            self.rawstep("sync", lambda e: [e.wait_ge(s_.h, v) for s_, v in w1]
                         or e.nop(), "fin")
            return
        if self.stream[0] is None:
            zt = self.sb("zfin", [128, D], F32)
            tz = self.cstep("vector", [], lambda e: e.memset(zt[:], 0.0), "zfin")

            def yz(e, dma):
                for t in range(NTT):
                    dma(out=self.y_h[128 * t:128 * (t + 1), :], in_=zt[:])
            toks = [self.dstep("sync", [tz] + getattr(self, "_stage_toks", []),
                               yz, "yz")]

            w0 = self._resolve("sync", toks)
            self.rawstep("sync", lambda e: [e.wait_ge(s_.h, v) for s_, v in w0],
                         "fin0")
            return
        if True:
            def yspl(e, dma):
                for t in range(NTT):
                    dma(out=self.y_h[128 * t:128 * (t + 1), :], in_=self.stream[t][:])
            deps = getattr(self, "t_stream_done", []) + \
                getattr(self, "_stage_toks", [])
            toks = [self.dstep("sync", deps, yspl, "yspl")]

        w1 = self._resolve("sync", toks)
        self.rawstep("sync", lambda e: [e.wait_ge(s_.h, v) for s_, v in w1],
                     "fin")


def build(n_sub=4, stage=99):
    return KB(n_sub, stage).build()


_IN_MAP = {
    "aWq": "L0_attn_a_Wq", "abq": "L0_attn_a_bq", "aWk": "L0_attn_a_Wk",
    "abk": "L0_attn_a_bk", "aWv": "L0_attn_a_Wv", "abv": "L0_attn_a_bv",
    "aWo": "L0_attn_a_Wo", "abo": "L0_attn_a_bo",
    "bWqkv": "L0_attn_b_W_qkv", "bbqkv": "L0_attn_b_b_qkv",
    "bWout": "L0_attn_b_W_out", "bbout": "L0_attn_b_b_out",
    "fw1": "L0_ffn_w1", "fcb": "L0_ffn_conv_b", "fw2": "L0_ffn_w2",
}


def _run(inputs, n_sub=4, stage=99):
    x = np.ascontiguousarray(np.asarray(inputs["L0_attn_a_x"],
                                        dtype=np.float32).reshape(8 * T, D))
    nc = build(n_sub, stage)
    base = {k: np.ascontiguousarray(np.asarray(inputs[v], np.float32))
            for k, v in _IN_MAP.items()}
    base["fcw"] = np.ascontiguousarray(
        np.asarray(inputs["L0_ffn_conv_w"], np.float32).reshape(3 * DHID))
    for k in list(base):
        if base[k].ndim > 1 and k not in ("fcw",):
            base[k] = base[k].reshape(base[k].shape[-2], base[k].shape[-1]) \
                if base[k].ndim == 2 else base[k]
    in_maps = []
    for c in range(8):
        m = dict(base)
        m["x"] = x[T * c:T * (c + 1)]
        pm = np.zeros((128, 8), np.float32)
        pm[:, :c] = 1.0
        m["pmask"] = pm
        hs = np.zeros((16, 2), np.float32)
        if c > 0:
            hs[2 * (c - 1) + 1, 0] = 1.0
        if c < 7:
            hs[2 * (c + 1), 1] = 1.0
        m["hsel"] = hs
        in_maps.append(m)
    res = run_bass_kernel_spmd(nc, in_maps, core_ids=list(range(8)))
    y = np.concatenate([res.results[c]["y"] for c in range(8)], axis=0)
    return y.reshape(1, 8 * T, D).astype(np.float32)


def kernel(**inputs):
    return _run(inputs, n_sub=4)


# version 18
# speedup vs baseline: 1.0156x; 1.0156x over previous
import numpy as np
import ml_dtypes
from contextlib import ExitStack

import concourse.bass as bass
import concourse.mybir as mybir
from concourse.bass_utils import run_bass_kernel_spmd

F32 = mybir.dt.float32
BF16 = mybir.dt.bfloat16
I32 = mybir.dt.int32
AF = mybir.ActivationFunctionType
ALU = mybir.AluOpType
AX = mybir.AxisListType

T = 2048
D = 1024
H = 16
HD = 64
CH = 128
NCH = T // CH
DHID = 4096
NTT = T // 128
TG = 512
NG = T // TG


class Sem:
    __slots__ = ("h", "n", "fn")

    def __init__(self, es, nc, name):
        self.h = es.enter_context(nc.semaphore(name))
        self.n = 0
        self.fn = 0


class KB:
    def __init__(self, n_sub=4, stage=99):
        self.stage = stage
        self.n_sub = n_sub
        self.nc = bass.Bass("TRN2", target_bir_lowering=False)
        self.es = ExitStack()
        self.uid = 0
        self._cms = {}
        self._seq = 0
        self._free_toks = {}
        self.waited = {e: {} for e in ("sync", "tensor", "vector", "scalar", "gpsimd")}

    def _nm(self, p):
        self.uid += 1
        return f"{p}{self.uid}"

    def sb(self, p, shape, dtype, side=None):
        cm = self.nc.sbuf_tensor(self._nm(p), shape, dtype, side=side)
        t = cm.__enter__()
        self._seq += 1
        self._cms[id(t)] = (self._seq, cm)
        return t

    def ps(self, p, shape, dtype=F32):
        cm = self.nc.psum_tensor(self._nm(p), shape, dtype)
        t = cm.__enter__()
        self._seq += 1
        self._cms[id(t)] = (self._seq, cm)
        return t

    def free(self, *ts, after=()):
        self.flush()
        for tok in after:
            if tok is None:
                continue
            sem, v = tok
            cur = self._free_toks.get(id(sem))
            if cur is None or cur[1] < v:
                self._free_toks[id(sem)] = (sem, v)
        items = sorted((self._cms.pop(id(t)) for t in ts), key=lambda x: -x[0])
        for _, cm in items:
            cm.__exit__(None, None, None)

    def dram(self, name, shape, dtype, kind=None):
        if kind is None:
            return self.nc.dram_tensor(name, shape, dtype).ap()
        return self.nc.dram_tensor(name, shape, dtype, kind=kind).ap()

    def _resolve(self, eng, after):
        own = self.esem[eng]
        out = []
        w = self.waited[eng]
        for tok in after:
            if tok is None:
                continue
            sem, v = tok
            if sem is own or v is None:
                continue
            if w.get(id(sem), -1) >= v:
                continue
            out.append((sem, v))
            w[id(sem)] = v
        return out

    _CHAIN_OPS = frozenset((
        "tensor_tensor", "tensor_scalar", "tensor_copy", "tensor_reduce",
        "reduce_max", "reduce_sum", "reciprocal", "memset", "select",
        "scalar_tensor_tensor", "tensor_tensor_reduce", "affine_select",
        "iota", "tensor_add", "tensor_mul", "tensor_sub", "tensor_max",
        "bn_stats", "bn_aggr", "pool_max", "pool_avg", "transpose",
        "tensor_scalar_add", "tensor_scalar_mul", "tensor_scalar_max",
        "tensor_scalar_min", "copy_predicated", "max",
        "activation", "copy", "add", "mul", "sqrt", "square", "sign"))

    class _CountEng:

        class _Fake:
            def then_inc(self, *a):
                return self

        def __init__(self):
            self.n_chain = 0

        def __getattr__(self, name):
            if name in KB._CHAIN_OPS:
                def f(*a, **kw):
                    self.n_chain += 1
                    return KB._CountEng._Fake()
                return f
            return lambda *a, **kw: KB._CountEng._Fake()

    class _ChainEng:

        def __init__(self, kb, eng, e):
            self._kb, self._eng, self._e = kb, eng, e

        def __getattr__(self, name):
            f = getattr(self._e, name)
            if name not in KB._CHAIN_OPS:
                return f
            kb, eng, e = self._kb, self._eng, self._e

            def wrapped(*a, **kw):
                sem = kb.esem[eng]
                w = kb.fwaited[eng]
                if sem.fn > 0 and w.get(id(sem), -1) < sem.fn:
                    e.wait_ge(sem.h, sem.fn)
                    w[id(sem)] = sem.fn
                ins = f(*a, **kw)
                ins.then_inc(sem.h, 1)
                sem.fn += 1
                return ins
            return wrapped

    def cstep(self, eng, after, fn, tag=""):
        sem = self.esem[eng]
        chained = eng in ("vector", "gpsimd", "scalar")
        waits = self._resolve(eng, list(self._free_toks.values()) + list(after))
        if chained:
            ce = KB._CountEng()
            fn(ce)
            assert ce.n_chain > 0, f"cstep {tag}: no chained ops"
            sem.n += ce.n_chain
            self.pending[eng].append(("c", waits, fn, ce.n_chain, tag))
        else:
            sem.n += 1
            self.pending[eng].append(("c1", waits, fn, 1, tag))
        return (sem, sem.n)

    def dma_alldone(self, eng):
        return [(ln, ln.n) for ln in self.dlanes[eng]]

    def tstep(self, after, fn, tag=""):
        return self.dstep("sync", list(after) + self.dma_alldone("gpsimd"),
                          fn, tag)

    def dstep(self, eng, after, fn, tag=""):
        self.dturn[eng] += 1
        lane = self.dlanes[eng][self.dturn[eng] % len(self.dlanes[eng])]
        waits = self._resolve(eng, list(self._free_toks.values()) + list(after)
                              + [(lane, lane.n)])
        ce = KB._CountEng()
        ndma = [0]

        def cdma(**kw):
            ndma[0] += 1
        fn(ce, cdma)
        lane.n += 16 * ndma[0]
        self.pending[eng].append(("d", waits, fn, (lane, ndma[0]), tag))
        return (lane, lane.n)

    def rawstep(self, eng, fn, tag=""):
        self.pending[eng].append(("r", [], fn, None, tag))

    def flush(self):
        for eng in ("sync", "tensor", "vector", "scalar", "gpsimd"):
            plist = self.pending[eng]
            if not plist:
                continue

            def body(e, eng=eng, plist=plist):
                sem = self.esem[eng]
                for kind, waits, fn, info, tag in plist:
                    for (s_, v) in waits:
                        e.wait_ge(s_.h, v)
                    if kind == "c":
                        n0 = sem.fn
                        fn(KB._ChainEng(self, eng, e))
                        assert sem.fn == n0 + info, \
                            f"{tag}: {sem.fn - n0} ops, predicted {info}"
                    elif kind == "c1":
                        last = fn(e)
                        assert last is not None, tag
                        last.then_inc(sem.h, 1)
                        sem.fn += 1
                    elif kind == "d":
                        lane, nd = info
                        cnt = [0]

                        def dma(**kw):
                            e.dma_start(**kw).then_inc(lane.h, 16)
                            cnt[0] += 1
                        fn(e, dma)
                        assert cnt[0] == nd, f"{tag}: dma count {cnt[0]} != {nd}"
                    else:
                        fn(e)
            getattr(self.blk, eng)(body)
            self.pending[eng] = []

    def build(self):
        nc = self.nc
        es = self.es
        self.x_h = self.dram("x", [T, D], F32, kind="ExternalInput")
        w = {}
        for n, shp in [("aWq", [D, D]), ("aWk", [D, D]), ("aWv", [D, D]),
                       ("aWo", [D, D]), ("bWqkv", [3 * D, D]), ("bWout", [D, D]),
                       ("fw1", [DHID, D]), ("fw2", [D, DHID])]:
            w[n] = self.dram(n, shp, F32, kind="ExternalInput")
        for n, ln in [("abq", D), ("abk", D), ("abv", D), ("abo", D),
                      ("bbqkv", 3 * D), ("bbout", D), ("fcw", 3 * DHID), ("fcb", DHID)]:
            w[n] = self.dram(n, [ln], F32, kind="ExternalInput")
        self.w = w
        self.pmask_h = self.dram("pmask", [128, 8], F32, kind="ExternalInput")
        self.hsel_h = self.dram("hsel", [16, 2], F32, kind="ExternalInput")
        self.y_h = self.dram("y", [T, D], F32, kind="ExternalOutput")

        self.ln_sp = self.dram("ln_sp", [T, D], BF16)
        self.attn_sp = self.dram("attn_sp", [T, D], BF16)
        self.k_sp = self.dram("k_sp", [D, T], BF16)
        self.spref_sp = self.dram("spref_sp", [NCH * 128, 512], F32)
        self.zpref_sp = self.dram("zpref_sp", [NCH * 128, 128], F32)
        self.cc_in = self.dram("cc_in", [128, 640], F32)
        self.cc_out = self.dram("cc_out", [8 * 128, 640], F32)
        self.cc2_in = self.dram("cc2_in", [2, D], F32)
        self.cc2_out = self.dram("cc2_out", [16, D], F32)
        self.cc3_in = self.dram("cc3_in", [2, D], F32)
        self.cc3_out = self.dram("cc3_out", [16, D], F32)
        self.wbf = {n: self.dram(n + "_bf", list(self.w[n].shape), BF16)
                    for n in ["aWq", "aWk", "aWv", "aWo", "bWqkv", "bWout"]}
        self.w1i_sp = self.dram("w1i_sp", [DHID, D], BF16)
        self.w2i_sp = self.dram("w2i_sp", [D, DHID], BF16)
        self.xq_sp = self.dram("xq_sp", [2064, D], BF16)
        self.h_sp = self.dram("h_sp", [DHID, T], BF16)
        self.hq_sp = self.dram("hq_sp", [T, DHID], BF16)
        self.x_sp = self.dram("x_sp", [T, D], F32)

        self.esem = {e: Sem(es, nc, "es_" + e)
                     for e in ("sync", "tensor", "vector", "scalar", "gpsimd")}
        self.pending = {e: [] for e in ("sync", "tensor", "vector", "scalar",
                                        "gpsimd")}
        self.fwaited = {e: {} for e in ("sync", "tensor", "vector", "scalar",
                                        "gpsimd")}
        self.dlanes = {"sync": [Sem(es, nc, f"dsy{i}") for i in range(6)],
                       "gpsimd": [Sem(es, nc, f"dgp{i}") for i in range(3)]}
        self.dturn = {"sync": 0, "gpsimd": 0}
        self.ccsem = Sem(es, nc, "ccs")

        self.ones_bf = self.sb("ones_bf", [128, 128], BF16)
        self.onesr_bf = self.sb("onesr_bf", [1, 128], BF16)
        self.zrow_bf = self.sb("zrow_bf", [1, 512], BF16)
        self.one_f32 = self.sb("one_f32", [1, 128], F32)
        self.ident_bf = self.sb("ident_bf", [128, 128], BF16)
        self.ident64 = self.sb("ident64", [128, 64], BF16)
        self.zbig = self.sb("zbig", [128, 512], BF16)
        self.mask4 = self.sb("mask4", [128, 512], F32)
        self.zero16 = self.sb("zero16", [16, D], BF16)
        self.rows_bf = self.sb("rows_bf", [1, 5 * D], BF16)
        self.rows2_bf = self.sb("rows2_bf", [1, 3 * D], BF16)
        self.convc = self.sb("convc", [128, 96], F32)
        self.convb = self.sb("convb", [128, 32], F32)
        self.pmask = self.sb("pmask", [128, 8], F32)
        self.hsel = self.sb("hsel", [16, 2], F32)
        self.cols = self.sb("cols", [128, 40], F32)

        self.stream = [None] * NTT

        with nc.Block("main") as blk:
            self.blk = blk
            self.phase0()
            if self.stage >= 2:
                self.sl1()
            if self.n_sub >= 2:
                self.sl_ffn(first=True)
            if self.n_sub >= 3:
                self.sl3()
            if self.n_sub >= 4:
                self.sl_ffn(first=False)
            self.finish()
            self.flush()
        self.es.close()
        return nc

    def phase0(self):
        rows = self.sb("rows", [1, 5 * D], F32)
        rows2 = self.sb("rows2", [1, 3 * D], F32)
        rowc = self.sb("rowc", [1, 3 * DHID], F32)
        rowcb = self.sb("rowcb", [1, DHID], F32)
        for i in range(NTT):
            self.stream[i] = self.sb("x0", [128, D], F32, side="right")

        def ld(e, dma):
            for i in range(NTT):
                dma(out=self.stream[i][:], in_=self.x_h[128 * i:128 * (i + 1), :])
            dma(out=self.pmask[:], in_=self.pmask_h)
            dma(out=self.hsel[:], in_=self.hsel_h)
            for nm, off in [("abq", 0), ("abk", D), ("abv", 2 * D),
                            ("abo", 3 * D), ("bbout", 4 * D)]:
                dma(out=rows[0:1, off:off + D],
                    in_=self.w[nm].rearrange("(a b) -> a b", a=1))
            dma(out=rows2[0:1, :],
                in_=self.w["bbqkv"].rearrange("(a b) -> a b", a=1))
            dma(out=rowc[0:1, :],
                in_=self.w["fcw"].rearrange("(a b) -> a b", a=1))
            dma(out=rowcb[0:1, :],
                in_=self.w["fcb"].rearrange("(a b) -> a b", a=1))
        self.t_x = self.dstep("sync", [], ld, "p0load")

        def consts(e):
            e.memset(self.ones_bf[:], 1.0)
            e.memset(self.onesr_bf[:], 1.0)
            e.memset(self.zrow_bf[:], 0.0)
            e.memset(self.zbig[:], 0.0)
            e.memset(self.one_f32[:], 1.0)
            e.memset(self.zero16[:], 0.0)
            e.memset(self.ident_bf[:], 1.0)
            e.affine_select(self.ident_bf[:], self.ident_bf[:], [[1, 128]],
                            ALU.is_equal, 0.0, base=0, channel_multiplier=-1)
            e.memset(self.mask4[:], 1.0)
            return e.affine_select(self.mask4[:], self.mask4[:], [[0, 4], [1, 128]],
                                   ALU.is_ge, 0.0, base=0, channel_multiplier=-1)
        self.t_const = self.cstep("gpsimd", [], consts, "consts")

        def casts(e, dma):
            for n in ["aWq", "aWk", "aWv", "aWo"]:
                dma(out=self.wbf[n], in_=self.w[n])
        self.t_acast = self.dstep("gpsimd", [], casts, "acast")

        pc = self.ps("pcols", [128, 32])

        def mkcols(e):
            last = None
            for j in range(8):
                last = e.matmul(pc[:, j:j + 1], rows[0:1, 128 * j:128 * (j + 1)],
                                self.one_f32[0:1, 0:1], start=(j == 0), stop=False)
            for j in range(8):
                last = e.matmul(pc[:, 8 + j:9 + j],
                                rows[0:1, D + 128 * j:D + 128 * (j + 1)],
                                self.one_f32[0:1, 0:1], start=False, stop=False)
            for j in range(16):
                last = e.matmul(pc[:, 16 + j:17 + j],
                                rows2[0:1, 128 * j:128 * (j + 1)],
                                self.one_f32[0:1, 0:1], start=False, stop=(j == 15))
            return last
        tcm = self.cstep("tensor", [self.t_x, self.t_const], mkcols, "mkcols")

        def evcols(e):
            return e.tensor_copy(self.cols[:, 0:32], pc[:])
        self.t_cols = self.cstep("vector", [tcm], evcols, "evcols")

        def bfrows(e):
            e.tensor_copy(self.rows_bf[:], rows[:])
            return e.tensor_copy(self.rows2_bf[:], rows2[:])
        self.t_rowsbf = self.cstep("vector", [self.t_x], bfrows, "bfrows")

        def mkid64(e):
            e.tensor_copy(self.ident64[0:64, :], self.ident_bf[0:64, 0:64])
            e.tensor_copy(self.ident64[64:128, :], self.ident_bf[0:64, 0:64])
        self.t_id64 = self.cstep("vector", [self.t_const], mkid64, "id64")

        self.free(pc, after=[self.t_cols])
        pc2 = self.ps("pconv", [128, 128])
        rcv = rowc[0:1, :].rearrange("a (c p k) -> a c p k", p=128, k=3)

        def mkconv(e):
            last = None
            for cc in range(32):
                for kk in range(3):
                    last = e.matmul(pc2[:, 3 * cc + kk:3 * cc + kk + 1],
                                    rcv[0:1, cc, :, kk],
                                    self.one_f32[0:1, 0:1],
                                    start=(cc == 0 and kk == 0), stop=False)
            for cc in range(32):
                last = e.matmul(pc2[:, 96 + cc:97 + cc],
                                rowcb[0:1, 128 * cc:128 * (cc + 1)],
                                self.one_f32[0:1, 0:1],
                                start=False, stop=(cc == 31))
            return last
        tcv = self.cstep("tensor", [self.t_x, self.t_cols], mkconv, "mkconv")

        def evconv(e):
            e.tensor_copy(self.convc[:], pc2[:, 0:96])
            return e.tensor_copy(self.convb[:], pc2[:, 96:128])
        self.t_conv = self.cstep("vector", [tcv], evconv, "evconv")
        self.free(pc2, rowc, rowcb, rows, rows2,
                  after=[self.t_conv, self.t_rowsbf, self.t_cols])

    def ln_pass(self, dst_sp, spill_x, tag):
        stats = self.sb(tag + "st", [128, 4 * NTT], F32)
        sq = self.sb(tag + "sq", [128, D], F32)
        lnt = [self.sb(tag + "ln", [128, D], BF16) for _ in range(2)]
        t_stats = []
        for t in range(NTT):
            xt = self.stream[t]
            tv = self.cstep("vector", [self.t_x], lambda e, xt=xt, t=t: e.tensor_reduce(
                stats[:, t:t + 1], xt[:], AX.X, ALU.add), "lnred")
            ts = self.cstep("scalar", [self.t_x], lambda e, xt=xt, t=t: e.activation(
                sq[:], xt[:], AF.Square,
                accum_out=stats[:, NTT + t:NTT + t + 1]), "lnsq")
            t_stats += [tv, ts]

        def colmath(e):
            M = stats[:, 0:NTT]
            S2 = stats[:, NTT:2 * NTT]
            V = stats[:, 2 * NTT:3 * NTT]
            e.tensor_scalar(M, M, 1.0 / D, None, ALU.mult)
            e.tensor_scalar(S2, S2, 1.0 / D, None, ALU.mult)
            e.tensor_tensor(V, M, M, ALU.mult)
            e.tensor_tensor(V, S2, V, ALU.subtract)
            return e.tensor_scalar(V, V, 1e-5, None, ALU.add)
        t1 = self.cstep("vector", t_stats, colmath, "lncol")
        t2 = self.cstep("scalar", [t1], lambda e: e.activation(
            stats[:, 2 * NTT:3 * NTT], stats[:, 2 * NTT:3 * NTT], AF.Sqrt), "lnsqrt")

        def recip(e):
            V = stats[:, 2 * NTT:3 * NTT]
            R = stats[:, 3 * NTT:4 * NTT]
            e.reciprocal(R, V)
            e.tensor_tensor(V, stats[:, 0:NTT], R, ALU.mult)
            return e.tensor_scalar(V, V, -1.0, None, ALU.mult)
        t3 = self.cstep("vector", [t2], recip, "lnrec")

        sp_tok = None
        sp_toks = []
        prev_spill = [None, None]
        for t in range(NTT):
            xt = self.stream[t]
            R = stats[:, 3 * NTT + t:3 * NTT + t + 1]
            NMR = stats[:, 2 * NTT + t:2 * NTT + t + 1]
            buf = lnt[t % 2]
            ta = self.cstep("scalar", [t3, prev_spill[t % 2]],
                            lambda e, xt=xt, buf=buf, R=R, NMR=NMR: e.activation(
                                buf[:], xt[:], AF.Identity, scale=R, bias=NMR), "lnap")

            def spl(e, dma, t=t, xt=xt, buf=buf):
                dma(out=dst_sp[128 * t:128 * (t + 1), :], in_=buf[:])
                if spill_x:
                    dma(out=self.x_sp[128 * t:128 * (t + 1), :], in_=xt[:])
            sp_tok = self.dstep("sync", [ta], spl, "lnspl")
            prev_spill[t % 2] = sp_tok
            sp_toks.append(sp_tok)
        self.free(stats, sq, *lnt, after=sp_toks)
        if spill_x:
            self._stream_spilled_toks = sp_toks
            self.free(*[self.stream[t] for t in range(NTT)], after=sp_toks)
            for t in range(NTT):
                self.stream[t] = None
        return sp_toks

    def sl1(self):
        nc = self.nc
        if self.stage < 3:
            self._stage_toks = []
            return
        t_lns = self.ln_pass(self.ln_sp, spill_x=True, tag="l1")
        if self.stage < 4:
            self._stage_toks = t_lns
            return
        self.q_fm = self.sb("q_fm", [128, 8 * T], BF16)
        self.k_fm = self.sb("k_fm", [128, 8 * T], BF16)
        self.v_tm = self.sb("v_tm", [128, NTT * D], BF16)
        self.WqT = self.sb("WqT", [128, 8 * D], BF16)
        self.WkT = self.sb("WkT", [128, 8 * D], BF16)
        self.WvT = self.sb("WvT", [128, 8 * D], BF16)

        def wload(e, dma):
            for dst, src in [(self.WqT, self.wbf["aWq"]), (self.WkT, self.wbf["aWk"]),
                             (self.WvT, self.wbf["aWv"])]:
                dma(out=dst[:].rearrange("p (c n) -> p c n", c=8),
                    in_=src, transpose=True)
        t_w = self.tstep([self.t_acast], wload, "s1wT")

        lnb = [self.sb("lnb", [128, 8 * TG], BF16) for _ in range(2)]
        pp = [self.ps("pp", [128, 512]) for _ in range(2)]
        e2b = [self.sb("e2b", [128, 512], F32) for _ in range(2)]
        rb = [self.sb("rb", [128, 512], F32) for _ in range(2)]
        prev_p = {}
        t_lnld = {}
        ti = 0
        for tb in range(4):
            def lnload(e, dma, tb=tb):
                dma(out=lnb[tb % 2][:].rearrange("p (c n) -> p c n", c=8),
                    in_=self.ln_sp[TG * tb:TG * (tb + 1), :], transpose=True)
            t_lnld[tb] = self.tstep(t_lns[4 * tb:4 * tb + 4]
                                    + [prev_p.get((tb - 2, "last"))],
                                    lnload, "s1lnld")
            lnc = lnb[tb % 2]
            for kind in ("q", "k", "v"):
                WT = {"q": self.WqT, "k": self.WkT, "v": self.WvT}[kind]
                if kind != "v":
                    dst = self.q_fm if kind == "q" else self.k_fm
                    boff = 0 if kind == "q" else 8
                    for oc in range(8):
                        par = ti % 2
                        ti += 1

                        def mm(e, WT=WT, oc=oc, par=par, lnc=lnc):
                            last = None
                            for kc in range(8):
                                last = e.matmul(
                                    pp[par][:],
                                    WT[:, kc * D + 128 * oc:kc * D + 128 * (oc + 1)],
                                    lnc[:, kc * TG:(kc + 1) * TG],
                                    start=(kc == 0), stop=(kc == 7))
                            return last
                        tm = self.cstep("tensor",
                                        [t_lnld[tb], t_w, prev_p.get(par),
                                         self.t_cols], mm, "s1proj")
                        bcol = self.cols[:, boff + oc:boff + oc + 1]
                        te = self.cstep("scalar", [tm], lambda e, par=par, bcol=bcol:
                                        e.activation(e2b[par][:], pp[par][:], AF.Exp,
                                                     bias=bcol), "s1exp")
                        dslice_off = oc * T + TG * tb

                        def stt(e, par=par, bcol=bcol, dst=dst, off=dslice_off):
                            e.tensor_scalar(rb[par][:], pp[par][:], bcol,
                                            0.0, ALU.add, ALU.max)
                            e.scalar_tensor_tensor(
                                dst[:, off:off + TG], e2b[par][:], 1.0, rb[par][:],
                                ALU.min, ALU.add)
                        tg = self.cstep("vector", [te], stt, "s1phi")
                        prev_p[par] = tg
                        prev_p[(tb, "last")] = tg
                else:
                    for tt4 in range(4):
                        tglob = 4 * tb + tt4
                        for half in range(2):
                            par = ti % 2
                            ti += 1

                            def mmv(e, tt4=tt4, half=half, par=par, lnc=lnc):
                                last = None
                                for kc in range(8):
                                    last = e.matmul(
                                        pp[par][:],
                                        lnc[:, kc * TG + 128 * tt4:kc * TG + 128 * (tt4 + 1)],
                                        WT[:, kc * D + 512 * half:kc * D + 512 * (half + 1)],
                                        start=(kc == 0), stop=False)
                                return e.matmul(
                                    pp[par][:], self.onesr_bf[0:1, :],
                                    self.rows_bf[0:1, 2 * D + 512 * half:2 * D + 512 * (half + 1)],
                                    start=False, stop=True)
                            tm = self.cstep("tensor",
                                            [t_lnld[tb], t_w, prev_p.get(par),
                                             self.t_rowsbf, self.t_const],
                                            mmv, "s1vproj")
                            voff = tglob * D + 512 * half
                            tv = self.cstep("vector", [tm], lambda e, par=par, voff=voff:
                                            e.tensor_copy(self.v_tm[:, voff:voff + 512],
                                                          pp[par][:]), "s1vev")
                            prev_p[par] = tv
                            prev_p[(tb, "last")] = tv
        t_proj_done = [prev_p[0], prev_p[1]]
        self.free(*lnb, *pp, *e2b, *rb)

        if self.stage < 5:
            self._stage_toks = t_proj_done
            return
        def kspill(e, dma):
            dma(out=self.k_sp.rearrange("(c p) t -> p c t", p=128),
                in_=self.k_fm[:].rearrange("p (c t) -> p c t", c=8))
        t_ks = self.dstep("sync", t_proj_done, kspill, "kspill")
        self.free(self.WqT, self.WkT, self.WvT, after=t_proj_done)
        self.k_tm = self.sb("k_tm", [128, NTT * D], BF16)

        def ktload(e, dma):
            dma(out=self.k_tm[:].rearrange("p (c n) -> p c n", c=NTT),
                in_=self.k_sp, transpose=True)
        t_kt = self.tstep([t_ks], ktload, "ktload")

        psd = [self.ps("psd", [128, 512]) for _ in range(2)]
        pzd = [self.ps("pzd", [128, 128]) for _ in range(2)]
        spref = [self.sb("spref", [128, 512], F32) for _ in range(2)]
        zpref = [self.sb("zpref", [128, 128], F32) for _ in range(2)]
        t_pref = {}
        t_prefdma = {}
        prev_ev = {}
        for ci in range(NCH):
            par = ci % 2

            def dmm(e, ci=ci, par=par):
                ko = ci * D
                z = self.zrow_bf
                e.matmul(pzd[par][:], z[0:1, 0:128], z[0:1, 0:128],
                         start=True, stop=False)
                e.matmul(psd[par][:], z[0:1, 0:128], z[0:1, 0:512],
                         start=True, stop=False)
                for h in range(H):
                    pb = 64 * (h % 2)
                    e.matmul(
                        psd[par][pb:pb + 64, 64 * (h // 2):64 * (h // 2) + 64],
                        self.k_tm[:, ko + 64 * h:ko + 64 * (h + 1)],
                        self.v_tm[:, ko + 64 * h:ko + 64 * (h + 1)],
                        start=False, stop=False)
                for h in range(H):
                    fc = h // 2
                    pb = 64 * (h % 2)
                    col = 16 * fc + h
                    e.matmul(pzd[par][pb:pb + 64, col:col + 1],
                             self.k_tm[:, ci * D + 64 * h:ci * D + 64 * (h + 1)],
                             self.ones_bf[:, 0:1],
                             start=False, stop=False)
                e.matmul(pzd[par][:], z[0:1, 0:128], z[0:1, 0:128],
                         start=False, stop=True)
                return e.matmul(psd[par][:], z[0:1, 0:128], z[0:1, 0:512],
                                start=False, stop=True)
            tm = self.cstep("tensor", [t_kt, prev_ev.get(par), self.t_const], dmm, "s1dS")

            def pref(e, ci=ci, par=par):
                if ci == 0:
                    e.tensor_copy(spref[par][:], psd[par][:])
                    return e.tensor_copy(zpref[par][:], pzd[par][:])
                e.tensor_tensor(spref[par][:], psd[par][:], spref[1 - par][:], ALU.add)
                return e.tensor_tensor(zpref[par][:], pzd[par][:], zpref[1 - par][:],
                                       ALU.add)
            deps = [tm] if ci == 0 else [tm, t_pref[ci - 1],
                                         t_prefdma.get(ci - 2)]
            t_pref[ci] = self.cstep("vector", deps, pref, "s1pref")
            prev_ev[par] = t_pref[ci]

            def prefspl(e, dma, ci=ci, par=par):
                dma(out=self.spref_sp[128 * ci:128 * (ci + 1), :], in_=spref[par][:])
                dma(out=self.zpref_sp[128 * ci:128 * (ci + 1), :], in_=zpref[par][:])
            t_prefdma[ci] = self.dstep("sync", [t_pref[ci]], prefspl, "s1prefspl")

        if self.stage < 6:
            self._stage_toks = list(t_prefdma.values())
            return
        def ccout(e, dma, par=(NCH - 1) % 2):
            dma(out=self.cc_in[:, 0:512], in_=spref[par][:])
            dma(out=self.cc_in[:, 512:640], in_=zpref[par][:])
        t_cco = self.dstep("gpsimd", [t_pref[NCH - 1]], ccout, "s1cco")

        ccw = self._resolve("gpsimd", [t_cco])
        self.ccsem.n += 1
        ccv = self.ccsem.n

        def docc(e):
            for (s_, v) in ccw:
                e.wait_ge(s_.h, v)
            e.collective_compute(
                "AllGather", ALU.bypass,
                replica_groups=[[0, 1, 2, 3, 4, 5, 6, 7]],
                ins=[self.cc_in], outs=[self.cc_out],
            ).then_inc(self.ccsem.h, 1)
        self.rawstep("gpsimd", docc, "s1cc")
        t_cc = (self.ccsem, ccv)

        self.free(*zpref, *spref, *pzd, *psd, self.k_tm,
                  after=[t_cco, t_pref[NCH - 1]] + list(t_prefdma.values()))
        gath = self.sb("gath", [128, 8 * 640], F32)

        def ccin(e, dma):
            for r in range(8):
                dma(out=gath[:, 640 * r:640 * (r + 1)],
                    in_=self.cc_out[128 * r:128 * (r + 1), :])
        t_gin = self.dstep("gpsimd", [t_cc], ccin, "s1gin")

        sacc = self.sb("sacc", [128, 640], F32)
        stmp = self.sb("stmp", [128, 640], F32)

        def prefsum(e):
            e.tensor_scalar(sacc[:], gath[:, 0:640], self.pmask[:, 0:1], None, ALU.mult)
            last = None
            for r in range(1, 8):
                e.tensor_scalar(stmp[:], gath[:, 640 * r:640 * (r + 1)],
                                self.pmask[:, r:r + 1], None, ALU.mult)
                last = e.tensor_tensor(sacc[:], sacc[:], stmp[:], ALU.add)
            return last
        t_sin = self.cstep("vector", [t_gin, self.t_x], prefsum, "s1prefsum")
        S_in = sacc[:, 0:512]
        z_in = sacc[:, 512:640]

        if self.stage < 7:
            self._stage_toks = [t_sin]
            return
        sprefB = [self.sb("sprefB", [128, 512], F32) for _ in range(2)]
        zprefB = [self.sb("zprefB", [128, 128], F32) for _ in range(2)]
        seff = [self.sb("seff", [128, 512], BF16) for _ in range(2)]
        zeff = self.sb("zeff", [128, 128], F32)
        zhi = [self.sb("zhi", [128, 128], BF16) for _ in range(2)]
        zlo = [self.sb("zlo", [128, 128], BF16) for _ in range(2)]
        msc = [self.sb("msc", [128, 2048], BF16) for _ in range(2)]
        attn_c = [self.sb("attn_c", [128, D], BF16) for _ in range(2)]
        psc = [self.ps("psc", [128, 512]) for _ in range(2)]
        pnum = [self.ps("pnum", [128, 512]) for _ in range(4)]
        pden = [self.ps("pden", [128, 16]) for _ in range(2)]
        rden = self.sb("rden", [128, 16], F32)
        t_attnspl = {}
        t_evnum = None
        t_eff_prev = None
        t_numprev = {}
        t_evprev = {}
        t_mev = {}
        for ci in range(NCH):
            if ci > 0:
                def prefld(e, dma, ci=ci):
                    dma(out=sprefB[ci % 2][:],
                        in_=self.spref_sp[128 * (ci - 1):128 * ci, :])
                    dma(out=zprefB[ci % 2][:],
                        in_=self.zpref_sp[128 * (ci - 1):128 * ci, :])
                t_pl = self.dstep("sync", [t_prefdma[ci - 1], t_eff_prev], prefld,
                                  "s1Bpl")
            else:
                t_pl = None

            def eff(e, ci=ci, cp=ci % 2):
                if ci == 0:
                    e.tensor_copy(seff[cp][:], S_in)
                    e.tensor_copy(zeff[:], z_in)
                else:
                    e.tensor_tensor(seff[cp][:], S_in, sprefB[ci % 2][:], ALU.add)
                    e.tensor_tensor(zeff[:], z_in, zprefB[ci % 2][:], ALU.add)
                e.tensor_copy(zhi[cp][:], zeff[:])
                return e.tensor_tensor(zlo[cp][:], zeff[:], zhi[cp][:],
                                       ALU.subtract)
            t_eff = self.cstep("gpsimd", [t_pl, t_sin, t_numprev.get(ci % 2)],
                               eff, "s1eff")
            t_eff_prev = t_eff
            if self.stage < 72:
                self._stage_toks = [t_eff]
                continue

            mscv = msc[ci % 2][:].rearrange("p (u two c) -> p two u c",
                                            two=2, c=128)
            for hg in range(4):
                par = hg % 2
                heads = [(hg % 2) + 8 * (hg // 2) + 2 * j for j in range(4)]

                def smm(e, ci=ci, heads=heads, par=par):
                    last = None
                    pb = 64 * (heads[0] % 2)
                    for j, h in enumerate(heads):
                        fc = h // 2
                        last = e.matmul(
                            psc[par][:, 128 * j:128 * (j + 1)],
                            self.k_fm[pb:pb + 64, fc * T + CH * ci:fc * T + CH * (ci + 1)],
                            self.q_fm[pb:pb + 64, fc * T + CH * ci:fc * T + CH * (ci + 1)],
                            start=(j == 0), stop=(j == 3))
                    return last
                tsc = self.cstep("tensor", [t_mev.get(par)] + t_proj_done, smm,
                                 "s1sc")

                def mev(e, hg=hg, par=par, mscv=mscv):
                    u0 = 4 * (hg // 2)
                    return e.tensor_tensor(
                        mscv[:, hg % 2, u0:u0 + 4, :],
                        psc[par][:].rearrange("p (a c) -> p a c", c=128),
                        self.mask4[:].rearrange("p (a c) -> p a c", c=128),
                        ALU.mult)
                t_mev[par] = self.cstep("vector", [tsc, self.t_const], mev, "s1mev")
            if self.stage < 73:
                self._stage_toks = [t_mev[0], t_mev[1]]
                continue

            def nummm(e, ci=ci, cp=ci % 2):
                last = None
                for h in range(H):
                    bk = 2 * cp + h // 8
                    first = (h % 8 == 0)
                    lasth = (h % 8 == 7)
                    hcol = 64 * (h % 8)
                    last = e.matmul(
                        pnum[bk][:, hcol:hcol + 64],
                        msc[cp][:, 128 * h:128 * (h + 1)],
                        self.v_tm[:, ci * D + 64 * h:ci * D + 64 * (h + 1)],
                        start=first, stop=False)
                    fc = h // 2
                    pb = 64 * (h % 2)
                    last = e.matmul(
                        pnum[bk][:, hcol:hcol + 64],
                        self.q_fm[pb:pb + 64, fc * T + CH * ci:fc * T + CH * (ci + 1)],
                        seff[cp][pb:pb + 64, 64 * (h // 2):64 * (h // 2) + 64],
                        start=False, stop=lasth)
                for h in range(H):
                    last = e.matmul(pden[cp][:, h:h + 1],
                                    msc[cp][:, 128 * h:128 * (h + 1)],
                                    self.ones_bf[:, 0:1], start=(h == 0),
                                    stop=False)
                for fc in range(8):
                    last = e.matmul(pden[cp][:, 0:16],
                                    self.q_fm[:, fc * T + CH * ci:fc * T + CH * (ci + 1)],
                                    zhi[cp][:, 16 * fc:16 * (fc + 1)],
                                    start=False, stop=False)
                    last = e.matmul(pden[cp][:, 0:16],
                                    self.q_fm[:, fc * T + CH * ci:fc * T + CH * (ci + 1)],
                                    zlo[cp][:, 16 * fc:16 * (fc + 1)],
                                    start=False, stop=(fc == 7))
                return last
            t_num = self.cstep("tensor", [t_mev[0], t_mev[1], t_eff, self.t_const,
                                          t_evprev.get(ci % 2),
                                          t_attnspl.get(ci - 2)], nummm, "s1num")
            t_numprev[ci % 2] = t_num

            def evnum(e, ci=ci, cp=ci % 2):
                e.reciprocal(rden[:], pden[cp][:])
                buf = attn_c[ci % 2]
                rv = rden[:].unsqueeze(2).broadcast_to((128, 16, 64))
                e.tensor_tensor(buf[:, 0:512].rearrange("p (h e) -> p h e", h=8),
                                pnum[2 * cp][:].rearrange("p (h e) -> p h e", h=8),
                                rv[:, 0:8, :], ALU.mult)
                return e.tensor_tensor(
                    buf[:, 512:1024].rearrange("p (h e) -> p h e", h=8),
                    pnum[2 * cp + 1][:].rearrange("p (h e) -> p h e", h=8),
                    rv[:, 8:16, :], ALU.mult)
            t_evnum = self.cstep("vector", [t_num], evnum, "s1evnum")
            t_evprev[ci % 2] = t_evnum

            def aspl(e, dma, ci=ci):
                dma(out=self.attn_sp[128 * ci:128 * (ci + 1), :],
                    in_=attn_c[ci % 2][:])
            t_attnspl[ci] = self.dstep("sync", [t_evnum], aspl, "s1aspl")

        if self.stage < 80:
            self._stage_toks = list(t_attnspl.values()) or \
                list(getattr(self, "_stage_toks", []))
            return
        self.free(self.q_fm, self.k_fm, self.v_tm,
                  *sprefB, *zprefB, *seff, zeff, *zhi, *zlo, *msc, *attn_c,
                  *psc, *pnum, *pden, rden, gath, sacc, stmp,
                  after=[t_attnspl[NCH - 1], t_evnum])

        self.WoT = self.sb("WoT", [128, 8 * D], BF16)
        self.attn_fm = self.sb("attn_fm", [128, 8 * T], BF16)

        def wo_ld(e, dma):
            dma(out=self.WoT[:].rearrange("p (c n) -> p c n", c=8),
                in_=self.wbf["aWo"], transpose=True)
        t_wo = self.tstep([self.t_acast], wo_ld, "s1woT")

        afmv = self.attn_fm[:].rearrange("p (c n) -> p c n", c=8)
        t_afm_h = []
        for hf in range(2):
            t_afm_h.append(self.tstep(
                [t_attnspl[ci] for ci in range(8 * hf, 8 * hf + 8)],
                lambda e, dma, hf=hf: dma(
                    out=afmv[:, :, 1024 * hf:1024 * (hf + 1)],
                    in_=self.attn_sp[1024 * hf:1024 * (hf + 1), :],
                    transpose=True), "s1afm"))
        self.proj_tm_residual(self.attn_fm, t_afm_h, self.WoT, t_wo,
                              self.rows_bf, 3 * D, "s1o", from_dram=True)
        self.free(self.WoT, self.attn_fm, after=self.t_stream_done)

    def proj_tm_residual(self, src_fm, t_src, WT, t_wt, bias_rows, bias_off, tag,
                         from_dram):
        pr = [self.ps(tag + "pr", [128, 512]) for _ in range(4)]
        xload = [self.sb(tag + "xl", [128, D], F32) for _ in range(2)] \
            if from_dram else None
        prev = {}
        for t in range(NTT):
            if from_dram:
                def xld(e, dma, t=t):
                    dma(out=xload[t % 2][:], in_=self.x_sp[128 * t:128 * (t + 1), :])
                t_xl = self.dstep("sync", [prev.get((t - 2, "res"))], xld, tag + "xld")
            newt = self.sb(tag + "nx", [128, D], F32, side="right")
            for half in range(2):
                par = (2 * t + half) % 4

                def mm(e, t=t, half=half, par=par):
                    last = None
                    for kc in range(8):
                        last = e.matmul(
                            pr[par][:],
                            src_fm[:, kc * T + 128 * t:kc * T + 128 * (t + 1)],
                            WT[:, kc * D + 512 * half:kc * D + 512 * (half + 1)],
                            start=(kc == 0), stop=False)
                    return e.matmul(
                        pr[par][:], self.onesr_bf[0:1, :],
                        bias_rows[0:1, bias_off + 512 * half:bias_off + 512 * (half + 1)],
                        start=False, stop=True)
                tsrc = t_src[t // 8] if isinstance(t_src, list) else t_src
                tm = self.cstep("tensor", [tsrc, t_wt, prev.get((t - 2, half)),
                                           self.t_rowsbf], mm, tag + "mm")
                xsrc = xload[t % 2] if from_dram else self.stream[t]
                deps = [tm] + ([t_xl] if from_dram else [])

                def ev(e, t=t, half=half, par=par, newt=newt, xsrc=xsrc):
                    return e.tensor_tensor(newt[:, 512 * half:512 * (half + 1)],
                                           pr[par][:],
                                           xsrc[:, 512 * half:512 * (half + 1)],
                                           ALU.add)
                tev = self.cstep("vector", deps, ev, tag + "ev")
                prev[(t, half)] = tev
                prev[(t, "res")] = tev
            if not from_dram:
                self.free(self.stream[t], after=[prev[(t, "res")]])
            self.stream[t] = newt
        self.t_stream_done = [prev[(t, "res")] for t in range(NTT)]
        self.free(*pr, after=self.t_stream_done)
        if from_dram:
            self.free(*xload, after=self.t_stream_done)

    def quant_weights(self):
        wcol = self.sb("wcol", [128, 4], F32)
        self.convc_s = self.sb("convcs", [128, 96], F32)
        stg = [self.sb("wq", [128, DHID], F32) for _ in range(2)]
        qst = [self.sb("wqs", [128, DHID], BF16) for _ in range(2)]
        g2 = self.sb("wg2", [128, DHID], BF16)
        racc = self.sb("racc", [128, 3], F32)
        scal = self.sb("wsc", [1, 4], F32)
        pt = self.ps("pwq", [128, 4])
        t_r = self.cstep("vector", [], lambda e: e.memset(racc[:], 0.0), "wr0")
        spills = {"fw1": [], "fw2": []}
        t_wcol = None
        for wi, (wn, rows, cols, sp) in enumerate(
                [("fw1", DHID, D, self.w1i_sp), ("fw2", D, DHID, self.w2i_sp)]):
            ntile = rows // 128
            t_ld = {}
            prev = {}
            for i in range(ntile):
                def ld(e, dma, i=i, wn=wn, cols=cols):
                    dma(out=stg[i % 2][:, 0:cols],
                        in_=self.w[wn][128 * i:128 * (i + 1), :])
                t_ld[i] = self.dstep("sync", [prev.get(i % 2)], ld, "wql")

                def red(e, i=i, wi=wi, cols=cols):
                    e.tensor_reduce(racc[:, 2:3], stg[i % 2][:, 0:cols], AX.X,
                                    ALU.add, apply_absolute_value=True)
                    e.tensor_tensor(racc[:, wi:wi + 1], racc[:, wi:wi + 1],
                                    racc[:, 2:3], ALU.add)
                prev[i % 2] = self.cstep("vector", [t_ld[i], t_r], red, "wqr")
            t_p1 = [prev[0], prev[1]]

            onec = self.sb("onec", [128, 1], F32)
            t_oc = self.cstep("vector", [], lambda e: e.memset(onec[:], 1.0), "oc")

            def tot2(e, wi=wi):
                return e.matmul(pt[0:1, wi:wi + 1], racc[:, wi:wi + 1], onec[:],
                                start=True, stop=True)
            t_tot = self.cstep("tensor", t_p1 + [t_oc], tot2, "wqt")

            def mkscal(e, wi=wi, rows=rows, cols=cols):
                e.tensor_scalar(scal[0:1, wi:wi + 1], pt[0:1, wi:wi + 1],
                                1.0 / (rows * cols), 1e-5, ALU.mult, ALU.max)
                e.reciprocal(scal[0:1, 2 + wi:3 + wi], scal[0:1, wi:wi + 1])
            t_sc = self.cstep("vector", [t_tot], mkscal, "wqsc")

            def bcast(e, wi=wi):
                e.matmul(pt[:, 2:3], self.one_f32[0:1, :],
                         scal[0:1, 2 + wi:3 + wi], start=True, stop=False)
                return e.matmul(pt[:, 3:4], self.one_f32[0:1, :],
                                scal[0:1, wi:wi + 1], start=False, stop=True)
            t_bc = self.cstep("tensor", [t_sc], bcast, "wqbc")

            def evc(e, wi=wi):
                e.tensor_copy(wcol[:, wi:wi + 1], pt[:, 2:3])
                e.tensor_copy(wcol[:, 2 + wi:3 + wi], pt[:, 3:4])
            t_wcol = self.cstep("vector", [t_bc], evc, "wqec")

            prev2 = {}
            for i in range(ntile):
                def ld2(e, dma, i=i, wn=wn, cols=cols):
                    dma(out=stg[i % 2][:, 0:cols],
                        in_=self.w[wn][128 * i:128 * (i + 1), :])
                tl = self.dstep("sync", [prev2.get(i % 2, t_p1[i % 2])], ld2, "wql2")

                def qz(e, i=i, wi=wi, cols=cols):
                    ws = wcol[:, wi:wi + 1]
                    e.tensor_scalar(qst[i % 2][:, 0:cols], stg[i % 2][:, 0:cols],
                                    ws, 0.5, ALU.mult, ALU.is_gt)
                    e.tensor_scalar(g2[:, 0:cols], stg[i % 2][:, 0:cols],
                                    ws, -0.5, ALU.mult, ALU.is_lt)
                    e.tensor_tensor(qst[i % 2][:, 0:cols], qst[i % 2][:, 0:cols],
                                    g2[:, 0:cols], ALU.subtract)
                tq = self.cstep("vector", [tl, t_wcol], qz, "wqq")

                def spl(e, dma, i=i, sp=sp, cols=cols):
                    dma(out=sp[128 * i:128 * (i + 1), :], in_=qst[i % 2][:, 0:cols])
                ts_ = self.dstep("sync", [tq], spl, "wqspl")
                prev2[i % 2] = ts_
                spills[wn].append(ts_)
            self.free(onec, after=[t_tot])
        def foldc(e):
            e.tensor_scalar(self.convc_s[:], self.convc[:], wcol[:, 2:3],
                            None, ALU.mult)
        t_cf = self.cstep("vector", [t_wcol, self.t_conv], foldc, "convf")
        self.wcol = wcol
        self._wq_done = (spills["fw1"], spills["fw2"], t_cf, t_wcol)
        self.free(pt, g2, *qst, *stg, racc, scal,
                  after=spills["fw1"] + spills["fw2"] + [t_cf])

    def halo_exchange(self, cin, cout, tag):
        def put(e, dma):
            dma(out=cin[0:1, :], in_=self.stream[0][0:1, :])
            dma(out=cin[1:2, :], in_=self.stream[NTT - 1][127:128, :])
        t_put = self.dstep("gpsimd", getattr(self, "t_stream_done", []), put,
                           tag + "put")

        ccw = self._resolve("gpsimd", [t_put])
        self.ccsem.n += 1
        ccv = self.ccsem.n

        def docc(e):
            for (s_, v) in ccw:
                e.wait_ge(s_.h, v)
            e.collective_compute(
                "AllGather", ALU.bypass, replica_groups=[[0, 1, 2, 3, 4, 5, 6, 7]],
                ins=[cin], outs=[cout]).then_inc(self.ccsem.h, 1)
        self.rawstep("gpsimd", docc, tag + "cc")
        t_cc = (self.ccsem, ccv)
        halo = self.sb(tag + "hx", [2, D], F32)
        gat = self.sb(tag + "g", [16, D], F32)
        t_g = self.dstep("gpsimd", [t_cc],
                         lambda e, dma: dma(out=gat[:], in_=cout), tag + "gin")
        ph = self.ps(tag + "ph", [2, 512])

        def hsel_mm(e):
            e.matmul(ph[:, 0:512], self.hsel[:], gat[:, 0:512],
                     start=True, stop=False)
            return e.matmul(ph[:, 0:512], self.hsel[:], gat[:, 0:512],
                            start=False, stop=True)
        def hmm1(e):
            return e.matmul(ph[:], self.hsel[:], gat[:, 0:512],
                            start=True, stop=True)
        t_m1 = self.cstep("tensor", [t_g, self.t_x], hmm1, tag + "m1")
        t_e1 = self.cstep("vector", [t_m1],
                          lambda e: e.tensor_copy(halo[:, 0:512], ph[:]),
                          tag + "e1")

        def hmm2(e):
            return e.matmul(ph[:], self.hsel[:], gat[:, 512:1024],
                            start=True, stop=True)
        t_m2 = self.cstep("tensor", [t_e1], hmm2, tag + "m2")
        t_e2 = self.cstep("vector", [t_m2],
                          lambda e: e.tensor_copy(halo[:, 512:1024], ph[:]),
                          tag + "e2")
        self.free(ph, gat, after=[t_e2])
        return halo, t_e2

    def sl_ffn(self, first):
        idx = 2 if first else 4
        tg = f"f{idx}"
        if first:
            self.quant_weights()
        w1toks, w2toks, t_cf, t_wcol = self._wq_done

        cin = self.cc2_in if first else self.cc3_in
        cout = self.cc2_out if first else self.cc3_out
        halo, t_halo = self.halo_exchange(cin, cout, tg + "h")

        qs = self.sb(tg + "qs", [128, 8 * (NTT + 1)], F32)
        NS = NTT + 1
        SUM, SSQ, MX, MN, RS, NMRS, DQ, SCR = [k * NS for k in range(8)]
        sq = self.sb(tg + "sq", [128, D], F32)
        i32t = [self.sb(tg + "i32", [128, D], I32) for _ in range(2)]
        xqst = [self.sb(tg + "xq", [128, D], BF16) for _ in range(2)]
        t_z = self.cstep("vector", [], lambda e: e.memset(qs[:], 0.0), tg + "qz")
        t_stats = [t_z]
        for t in range(NTT + 1):
            if t < NTT:
                xt, xsl = self.stream[t], (slice(0, 128), slice(0, D))
                src = xt[:]
                deps = []
            else:
                src = halo[:]
                deps = [t_halo]

            def st_v(e, t=t, src=src):
                e.tensor_reduce(qs[0:src.shape[0], SUM + t:SUM + t + 1], src,
                                AX.X, ALU.add)
                e.tensor_reduce(qs[0:src.shape[0], MX + t:MX + t + 1], src,
                                AX.X, ALU.max)
                e.tensor_reduce(qs[0:src.shape[0], MN + t:MN + t + 1], src,
                                AX.X, ALU.min)
            tv = self.cstep("vector", deps, st_v, tg + "qsv")

            def st_s(e, t=t, src=src):
                e.activation(sq[0:src.shape[0], :], src, AF.Square,
                             accum_out=qs[0:src.shape[0], SSQ + t:SSQ + t + 1])
            ts_ = self.cstep("scalar", deps + [t_z], st_s, tg + "qss")
            t_stats += [tv, ts_]

        def colmath(e):
            S_, Q_, X_, N_ = qs[:, SUM:SUM + NS], qs[:, SSQ:SSQ + NS], \
                qs[:, MX:MX + NS], qs[:, MN:MN + NS]
            RS_, NM_, DQ_, SC_ = qs[:, RS:RS + NS], qs[:, NMRS:NMRS + NS], \
                qs[:, DQ:DQ + NS], qs[:, SCR:SCR + NS]
            e.tensor_scalar(S_, S_, 1.0 / D, None, ALU.mult)
            e.tensor_scalar(Q_, Q_, 1.0 / D, None, ALU.mult)
            e.tensor_tensor(SC_, S_, S_, ALU.mult)
            e.tensor_tensor(Q_, Q_, SC_, ALU.subtract)
            e.tensor_scalar(Q_, Q_, 1e-5, None, ALU.add)
        t_c1 = self.cstep("vector", t_stats, colmath, tg + "qc1")
        t_c2 = self.cstep("scalar", [t_c1], lambda e: e.activation(
            qs[:, SSQ:SSQ + NS], qs[:, SSQ:SSQ + NS], AF.Sqrt), tg + "qc2")

        def colmath2(e):
            S_, Q_, X_, N_ = qs[:, SUM:SUM + NS], qs[:, SSQ:SSQ + NS], \
                qs[:, MX:MX + NS], qs[:, MN:MN + NS]
            RS_, NM_, DQ_, SC_ = qs[:, RS:RS + NS], qs[:, NMRS:NMRS + NS], \
                qs[:, DQ:DQ + NS], qs[:, SCR:SCR + NS]
            e.reciprocal(SC_, Q_)
            e.tensor_tensor(X_, X_, S_, ALU.subtract)
            e.tensor_tensor(N_, S_, N_, ALU.subtract)
            e.tensor_tensor(X_, X_, N_, ALU.max)
            e.tensor_tensor(X_, X_, SC_, ALU.mult)
            e.tensor_scalar(X_, X_, 1e-5, None, ALU.max)
            e.tensor_scalar(DQ_, X_, 1.0 / 127.0, None, ALU.mult)
            e.reciprocal(N_, X_)
            e.tensor_scalar(N_, N_, 127.0, None, ALU.mult)
            e.tensor_tensor(RS_, SC_, N_, ALU.mult)
            e.tensor_tensor(NM_, S_, RS_, ALU.mult)
            e.tensor_scalar(NM_, NM_, -1.0, None, ALU.mult)
        t_c3 = self.cstep("vector", [t_c2], colmath2, tg + "qc3")

        t_xq = []
        prevq = {}
        for t in range(NTT + 1):
            par = t % 2
            if t < NTT:
                src = self.stream[t][:]
                rows = 128
            else:
                src = halo[:]
                rows = 2

            def ap1(e, t=t, src=src, par=par, rows=rows):
                e.activation(i32t[par][0:rows, :], src, AF.Identity,
                             scale=qs[0:rows, RS + t:RS + t + 1],
                             bias=qs[0:rows, NMRS + t:NMRS + t + 1])
            ta = self.cstep("scalar", [t_c3, prevq.get(par)], ap1, tg + "qt1")

            def ap3(e, t=t, par=par, rows=rows):
                e.tensor_scalar(xqst[par][0:rows, :], i32t[par][0:rows, :],
                                qs[0:rows, DQ + t:DQ + t + 1], None, ALU.mult)
            tc = self.cstep("vector", [ta], ap3, tg + "qdq")

            def spl(e, dma, t=t, par=par):
                if t < NTT:
                    dma(out=self.xq_sp[128 * t + 1:128 * (t + 1) + 1, :],
                        in_=xqst[par][:])
                    dma(out=self.x_sp[128 * t:128 * (t + 1), :],
                        in_=self.stream[t][:])
                else:
                    dma(out=self.xq_sp[0:1, :], in_=xqst[par][0:1, :])
                    dma(out=self.xq_sp[2049:2050, :], in_=xqst[par][1:2, :])
                    dma(out=self.xq_sp[2050:2064, :], in_=self.zero16[0:14, :])
            td = self.dstep("sync", [tc] + ([self.t_const] if t == NTT else []),
                            spl, tg + "qspl")
            prevq[par] = td
            t_xq.append(td)
        self.free(*xqst, *i32t, sq, after=t_xq)
        self.free(*[self.stream[t] for t in range(NTT)], after=t_xq)
        for t in range(NTT):
            self.stream[t] = None

        dqh = self.sb(tg + "dqh", [128, 2 * NTT], F32)
        w1iT = self.sb(tg + "w1T", [128, 8 * DHID], BF16)

        def w1ld(e, dma):
            dma(out=w1iT[:].rearrange("p (c n) -> p c n", c=8),
                in_=self.w1i_sp, transpose=True)
        t_w1 = self.tstep(w1toks, w1ld, tg + "w1T")

        htm = [self.sb(tg + "htm", [128, DHID], BF16) for _ in range(2)]
        i32h = [self.sb(tg + "i32h", [128, DHID], I32) for _ in range(2)]
        hqst = [self.sb(tg + "hqst", [128, DHID], BF16) for _ in range(2)]
        xqg = self.sb(tg + "xqg", [128, 8 * 528], BF16)
        hb = [self.sb(tg + "hb", [128, 514], BF16) for _ in range(4)]
        cb1 = [self.sb(tg + "cb1", [128, 512], F32) for _ in range(4)]
        cb2 = [self.sb(tg + "cb2", [128, 512], F32) for _ in range(4)]
        cbo = [self.sb(tg + "cbo", [128, 512], F32) for _ in range(4)]
        hst = self.sb(tg + "hst", [128, 4 * 512], BF16)
        pm = [self.ps(tg + "pm", [128, 512]) for _ in range(4)]
        ptl = [self.ps(tg + "ptl", [128, 2]) for _ in range(4)]
        t_hspl = []
        t_hq = []
        t_bc_prev = {}
        t_xqld_prev = None
        t_spl4 = None
        for g in range(NG):
            g_spl = []
            t_xqld = self.tstep(t_xq + [t_xqld_prev], lambda e, dma, g=g: dma(
                out=xqg[:].rearrange("p (c n) -> p c n", c=8),
                in_=self.xq_sp[TG * g:TG * g + 528, :], transpose=True),
                tg + "xql")
            prev_ev = {}
            prev_sil = {}
            for hc in range(32):
                par = hc % 4

                def w1mm2(e, g=g, hc=hc, par=par):
                    last = None
                    for kc in range(8):
                        e.matmul(pm[par][:],
                                 w1iT[:, kc * DHID + 128 * hc:kc * DHID + 128 * (hc + 1)],
                                 xqg[:, kc * 528:kc * 528 + 512],
                                 start=(kc == 0), stop=(kc == 7))
                    for kc in range(8):
                        last = e.matmul(
                            ptl[par][:],
                            w1iT[:, kc * DHID + 128 * hc:kc * DHID + 128 * (hc + 1)],
                            xqg[:, kc * 528 + 512:kc * 528 + 514],
                            start=(kc == 0), stop=(kc == 7))
                    return last
                tm = self.cstep("tensor", [t_xqld, t_w1, prev_ev.get(par),
                                           prev_sil.get(par)], w1mm2, tg + "w1m")

                def hev(e, hc=hc, par=par):
                    e.tensor_copy(hb[par][:, 0:512], pm[par][:])
                    e.tensor_copy(hb[par][:, 512:514], ptl[par][:])
                    e.tensor_scalar(cb1[par][:], hb[par][:, 0:512],
                                    self.convc_s[:, 3 * hc:3 * hc + 1],
                                    None, ALU.mult)
                    e.scalar_tensor_tensor(cb2[par][:], hb[par][:, 1:513],
                                           self.convc_s[:, 3 * hc + 1:3 * hc + 2],
                                           cb1[par][:], ALU.mult, ALU.add)
                    e.scalar_tensor_tensor(cbo[par][:], hb[par][:, 2:514],
                                           self.convc_s[:, 3 * hc + 2:3 * hc + 3],
                                           cb2[par][:], ALU.mult, ALU.add)
                tc3 = self.cstep("vector", [tm, t_cf], hev, tg + "hev")
                prev_ev[par] = tc3

                def sil(e, hc=hc, par=par):
                    e.activation(hst[:, 512 * (hc % 4):512 * (hc % 4 + 1)],
                                 cbo[par][:], AF.Silu,
                                 bias=self.convb[:, hc:hc + 1])
                ts_ = self.cstep("scalar", [tc3, t_spl4], sil, tg + "sil")
                prev_sil[par] = ts_
                if hc % 4 == 3:
                    q4 = hc // 4

                    def spl4(e, dma, g=g, q4=q4):
                        dma(out=self.h_sp[512 * q4:512 * (q4 + 1),
                                          TG * g:TG * (g + 1)].rearrange(
                                              "(c p) n -> p c n", p=128),
                            in_=hst[:].rearrange("p (c n) -> p c n", c=4))
                    t_spl4 = self.dstep("sync", [ts_], spl4, tg + "hspl")
                    t_hspl.append(t_spl4)
                    g_spl.append(t_spl4)
            t_xqld_prev = tm
            for tt in range(4):
                t = 4 * g + tt
                par = t % 2
                t_hl = self.tstep(g_spl + [t_bc_prev.get(par)],
                                  lambda e, dma, t=t, par=par: dma(
                    out=htm[par][:], in_=self.h_sp[:, 128 * t:128 * (t + 1)],
                    transpose=True), tg + "htl")

                def amax(e, t=t, par=par):
                    e.tensor_reduce(dqh[:, NTT + t:NTT + t + 1], htm[par][:],
                                    AX.X, ALU.max, apply_absolute_value=True)
                    e.tensor_scalar(dqh[:, NTT + t:NTT + t + 1],
                                    dqh[:, NTT + t:NTT + t + 1], 1e-5, None,
                                    ALU.max)
                    e.tensor_scalar(dqh[:, t:t + 1],
                                    dqh[:, NTT + t:NTT + t + 1],
                                    1.0 / 127.0, None, ALU.mult)
                    e.reciprocal(dqh[:, NTT + t:NTT + t + 1],
                                 dqh[:, NTT + t:NTT + t + 1])
                    e.tensor_scalar(dqh[:, NTT + t:NTT + t + 1],
                                    dqh[:, NTT + t:NTT + t + 1], 127.0, None,
                                    ALU.mult)
                t_am = self.cstep("vector", [t_hl], amax, tg + "amax")

                def rh1(e, t=t, par=par):
                    e.activation(i32h[par][:], htm[par][:], AF.Identity,
                                 scale=dqh[:, NTT + t:NTT + t + 1])
                    e.copy(hqst[par][:], i32h[par][:])
                t_r1 = self.cstep("scalar", [t_am], rh1, tg + "rh1")
                t_sp = self.dstep("sync", [t_r1], lambda e, dma, t=t, par=par:
                                  dma(out=self.hq_sp[128 * t:128 * (t + 1), :],
                                      in_=hqst[par][:]), tg + "hqspl")
                t_hq.append(t_sp)
                t_bc_prev[par] = t_sp
        self.free(xqg, *hb, *cb1, *cb2, *cbo, hst, *pm, *ptl,
                  after=t_hspl + [prev_sil[0], prev_sil[1]])
        self.free(*hqst, *i32h, *htm, after=t_hq)
        self.free(w1iT, after=t_hspl)

        w2iT = self.sb(tg + "w2T", [128, 32 * D], BF16)

        def w2ld(e, dma):
            dma(out=w2iT[:].rearrange("p (c n) -> p c n", c=32),
                in_=self.w2i_sp, transpose=True)
        t_w2 = self.tstep(w2toks, w2ld, tg + "w2T")
        hqfm = self.sb(tg + "hqfm", [128, 32 * TG], BF16)
        hqv = hqfm[:].rearrange("p (c n) -> p c n", c=32)
        xld = [self.sb(tg + "xld", [128, D], F32) for _ in range(2)]
        dqq = self.sb(tg + "dqq", [128, NTT], F32)
        pd = [self.ps(tg + "pd", [128, 512]) for _ in range(4)]
        t_dqq = self.cstep("vector", [t_wcol, t_c3] + t_hq, lambda e:
                           e.tensor_scalar(dqq[:, 0:NTT], dqh[:, 0:NTT],
                                           self.wcol[:, 3:4], None, ALU.mult),
                           tg + "dqq")
        newts = {}
        prev_hql = None
        prev_evd = {}
        for g in range(NG):
            t_hqls = []
            for qq in range(4):
                t_hqls.append(self.tstep(
                    t_hq + [prev_hql],
                    lambda e, dma, g=g, qq=qq: dma(
                        out=hqv[:, 8 * qq:8 * (qq + 1), :],
                        in_=self.hq_sp[TG * g:TG * (g + 1),
                                       1024 * qq:1024 * (qq + 1)],
                        transpose=True), tg + "hql"))
            for tt in range(4):
                t = 4 * g + tt

                def xl(e, dma, t=t, par=t % 2):
                    dma(out=xld[par][:], in_=self.x_sp[128 * t:128 * (t + 1), :])
                t_xl = self.dstep("sync", [prev_evd.get(t % 2)], xl, tg + "dxl")
                newt = self.sb(tg + "nx", [128, D], F32, side="right")
                tevs = []
                for half in range(2):
                    par = (2 * t + half) % 4

                    tms = []
                    for qq in range(4):
                        def dmmq(e, tt=tt, half=half, par=par, qq=qq):
                            last = None
                            for kc in range(8 * qq, 8 * (qq + 1)):
                                last = e.matmul(
                                    pd[par][:],
                                    hqfm[:, kc * TG + 128 * tt:kc * TG + 128 * (tt + 1)],
                                    w2iT[:, kc * D + 512 * half:kc * D + 512 * (half + 1)],
                                    start=(kc == 8 * qq and qq == 0),
                                    stop=(kc == 8 * qq + 7 and qq == 3))
                            return last
                        tms.append(self.cstep(
                            "tensor", [t_hqls[qq], t_w2,
                                       prev_evd.get((t - 2, half))],
                            dmmq, tg + "w2m"))
                    tm = tms[-1]

                    def evd(e, t=t, half=half, par=par, newt=newt):
                        e.scalar_tensor_tensor(
                            newt[:, 512 * half:512 * (half + 1)], pd[par][:],
                            dqq[:, t:t + 1], xld[t % 2][:, 512 * half:512 * (half + 1)],
                            ALU.mult, ALU.add)
                    tev = self.cstep("vector", [tm, t_xl, t_dqq], evd, tg + "evd")
                    prev_evd[(t, half)] = tev
                    tevs.append(tev)
                prev_evd[t % 2] = tevs[-1]
                self.stream[t] = newt
                if not first and self.n_sub == 4:
                    tevs = [self.dstep("sync", tevs, lambda e, dma, t=t, newt=newt:
                                       dma(out=self.y_h[128 * t:128 * (t + 1), :],
                                           in_=newt[:]), tg + "ydma")]
                newts[t] = tevs
            prev_hql = prev_evd[(4 * g + 3, 1)]
        self.t_stream_done = [newts[t][-1] for t in range(NTT)]
        self.free(*pd, dqq, *xld, hqfm, after=self.t_stream_done)
        self.free(w2iT, after=self.t_stream_done)
        self.free(dqh, qs, halo, after=[t_dqq] + t_xq)

    def sl3(self):
        tg = "s3"
        def bcast(e, dma):
            dma(out=self.wbf["bWqkv"], in_=self.w["bWqkv"])
            dma(out=self.wbf["bWout"], in_=self.w["bWout"])
        t_bc = self.dstep("gpsimd", [], bcast, tg + "cast")

        t_ln3 = self.ln_pass(self.ln_sp, spill_x=False, tag="l3")

        WqkvT = self.sb(tg + "WqkvT", [128, 8 * 3 * D], BF16)
        WoutT = self.sb(tg + "WoutT", [128, 8 * D], BF16)

        def wld(e, dma):
            dma(out=WqkvT[:].rearrange("p (c n) -> p c n", c=8),
                in_=self.wbf["bWqkv"], transpose=True)
            dma(out=WoutT[:].rearrange("p (c n) -> p c n", c=8),
                in_=self.wbf["bWout"], transpose=True)
        t_w = self.tstep([t_bc], wld, tg + "wT")

        ln3g = self.sb(tg + "lng", [128, 8 * TG], BF16)
        qg = self.sb(tg + "qg", [128, 8 * TG], BF16)
        kg = self.sb(tg + "kg", [128, 8 * TG], BF16)
        vg = self.sb(tg + "vg", [128, 4 * D], BF16)
        esb = self.sb(tg + "es", [128, 256], F32)
        sm3 = self.sb(tg + "sm", [128, 8], F32)
        attb = self.sb(tg + "att", [128, 256], BF16)
        attT = self.sb(tg + "attT", [128, 256], BF16)
        og = self.sb(tg + "og", [128, 8 * TG], BF16)
        xld3 = [self.sb(tg + "xl", [128, D], F32) for _ in range(2)]
        pA = [self.ps(tg + "pA", [128, 512]) for _ in range(2)]
        psc = [self.ps(tg + "psc", [128, 512]) for _ in range(2)]
        ptE = self.ps(tg + "ptE", [128, 256], BF16)
        ptO = self.ps(tg + "ptO", [128, 256], BF16)
        po = [self.ps(tg + "po", [128, 512]) for _ in range(2)]

        prev_lnld = None
        prev_pA = {}
        prev_psc = {}
        prev_po = {}
        prev_pt = None
        prev_res = {}
        for g in range(NG):
            t_lng = self.tstep(t_ln3 + [prev_lnld], lambda e, dma, g=g: dma(
                out=ln3g[:].rearrange("p (c n) -> p c n", c=8),
                in_=self.ln_sp[TG * g:TG * (g + 1), :], transpose=True),
                tg + "lnl")
            t_qkv = []
            for kind in range(3):
                if kind < 2:
                    dst = qg if kind == 0 else kg
                    fo0 = D * kind
                    for oc in range(8):
                        par = oc % 2

                        def mm(e, fo0=fo0, oc=oc, par=par):
                            last = None
                            for kc in range(8):
                                last = e.matmul(
                                    pA[par][:],
                                    WqkvT[:, kc * 3 * D + fo0 + 128 * oc:
                                          kc * 3 * D + fo0 + 128 * (oc + 1)],
                                    ln3g[:, kc * TG:(kc + 1) * TG],
                                    start=(kc == 0), stop=(kc == 7))
                            return last
                        tm = self.cstep("tensor", [t_lng, t_w, prev_pA.get(par),
                                                   self.t_cols], mm, tg + "qkm")
                        bcol = self.cols[:, 16 + 8 * kind + oc:17 + 8 * kind + oc]

                        def ev(e, dst=dst, oc=oc, par=par, bcol=bcol):
                            e.tensor_scalar(dst[:, oc * TG:(oc + 1) * TG],
                                            pA[par][:], bcol, None, ALU.add)
                        tev = self.cstep("vector", [tm], ev, tg + "qke")
                        prev_pA[par] = tev
                        t_qkv.append(tev)
                else:
                    for tt in range(4):
                        for half in range(2):
                            par = (tt * 2 + half) % 2

                            def mmv(e, tt=tt, half=half, par=par):
                                for kc in range(8):
                                    e.matmul(
                                        pA[par][:],
                                        ln3g[:, kc * TG + 128 * tt:
                                             kc * TG + 128 * (tt + 1)],
                                        WqkvT[:, kc * 3 * D + 2 * D + 512 * half:
                                              kc * 3 * D + 2 * D + 512 * (half + 1)],
                                        start=(kc == 0), stop=False)
                                return e.matmul(
                                    pA[par][:], self.onesr_bf[0:1, :],
                                    self.rows2_bf[0:1, 2 * D + 512 * half:
                                                  2 * D + 512 * (half + 1)],
                                    start=False, stop=True)
                            tm = self.cstep("tensor",
                                            [t_lng, t_w, prev_pA.get(par),
                                             self.t_rowsbf, self.t_const],
                                            mmv, tg + "vm")

                            def evv(e, tt=tt, half=half, par=par):
                                e.copy(vg[:, tt * D + 512 * half:
                                          tt * D + 512 * (half + 1)], pA[par][:])
                            tev = self.cstep("scalar", [tm], evv, tg + "vev")
                            prev_pA[par] = tev
                            t_qkv.append(tev)

            t_oev = []
            for h in range(H):
                par = h % 2
                pb = 64 * (h % 2)
                fc = h // 2

                def smm(e, h=h, par=par, pb=pb, fc=fc):
                    last = None
                    for b in range(8):
                        last = e.matmul(
                            psc[par][64 * (b % 2):64 * (b % 2) + 64,
                                     64 * (b // 2):64 * (b // 2) + 64],
                            qg[pb:pb + 64, fc * TG + 64 * b:fc * TG + 64 * (b + 1)],
                            kg[pb:pb + 64, fc * TG + 64 * b:fc * TG + 64 * (b + 1)],
                            start=(b < 2), stop=(b >= 6))
                    return last
                def smm2(e, h=h, par=par, pb=pb, fc=fc):
                    z = self.zbig
                    e.matmul(psc[par][:], z[pb:pb + 64, 0:128], z[pb:pb + 64, 0:512],
                             start=True, stop=False)
                    for b in range(8):
                        e.matmul(
                            psc[par][64 * (b % 2):64 * (b % 2) + 64,
                                     64 * (b // 2):64 * (b // 2) + 64],
                            qg[pb:pb + 64, fc * TG + 64 * b:fc * TG + 64 * (b + 1)],
                            kg[pb:pb + 64, fc * TG + 64 * b:fc * TG + 64 * (b + 1)],
                            start=False, stop=False)
                    return e.matmul(psc[par][:], z[pb:pb + 64, 0:128],
                                    z[pb:pb + 64, 0:512],
                                    start=False, stop=True)
                tsc = self.cstep("tensor", [prev_psc.get(par), self.t_const]
                                 + t_qkv, smm2, tg + "sc")

                def vmax(e, par=par):
                    e.tensor_reduce(sm3[:, 0:1], psc[par][:, 0:256], AX.X,
                                    ALU.max)
                    e.tensor_scalar(sm3[:, 1:2], sm3[:, 0:1], -0.125, None,
                                    ALU.mult)
                tmx = self.cstep("vector", [tsc], vmax, tg + "mx")

                def vexp(e, par=par):
                    e.activation(esb[:], psc[par][:, 0:256], AF.Exp,
                                 scale=0.125, bias=sm3[:, 1:2])
                tex = self.cstep("scalar", [tmx], vexp, tg + "exp")

                def vsum(e):
                    e.tensor_reduce(sm3[:, 4:8], esb[:].rearrange(
                        "p (s k) -> p s k", k=64), AX.X, ALU.add)
                    e.reciprocal(sm3[:, 4:8], sm3[:, 4:8])
                    rv = sm3[:, 4:8].unsqueeze(2).broadcast_to((128, 4, 64))
                    e.tensor_tensor(attb[:].rearrange("p (s k) -> p s k", k=64),
                                    esb[:].rearrange("p (s k) -> p s k", k=64),
                                    rv, ALU.mult)
                tat = self.cstep("vector", [tex], vsum, tg + "sm")

                def trans(e, h=h):
                    last = None
                    for b in range(8):
                        pt = ptE if b % 2 == 0 else ptO
                        last = e.transpose(
                            pt[64 * (b % 2):64 * (b % 2) + 64,
                               64 * (b // 2):64 * (b // 2) + 64],
                            attb[64 * (b % 2):64 * (b % 2) + 64,
                                 64 * (b // 2):64 * (b // 2) + 64],
                            self.ident64[64 * (b % 2):64 * (b % 2) + 64, :])
                    return last
                ttr = self.cstep("tensor", [tat, prev_pt, self.t_id64], trans,
                                 tg + "tr")

                def evtr(e):
                    e.tensor_copy(attT[0:64, :], ptE[0:64, :])
                    e.tensor_copy(attT[64:128, :], ptO[64:128, :])
                t_aT = self.cstep("vector", [ttr], evtr, tg + "evT")
                prev_pt = t_aT

                def omm(e, h=h, par=par, fc=fc):
                    z = self.zbig
                    e.matmul(po[par][:], z[:, 0:128], z[:, 0:512],
                             start=True, stop=False)
                    for j, b in enumerate([0, 2, 4, 6, 1, 3, 5, 7]):
                        if j == 4:
                            e.matmul(po[par][:, 0:1], z[:, 0:128], z[:, 0:1],
                                     start=False, stop=False)
                        pb_b = 64 * (b % 2)
                        e.matmul(
                            po[par][64 * (h % 2):64 * (h % 2) + 64,
                                    64 * b:64 * (b + 1)],
                            vg[pb_b:pb_b + 64,
                               (b // 2) * D + 64 * h:(b // 2) * D + 64 * (h + 1)],
                            attT[pb_b:pb_b + 64, 64 * (b // 2):64 * (b // 2) + 64],
                            start=False, stop=False)
                    return e.matmul(po[par][:, 0:1], z[:, 0:128], z[:, 0:1],
                                    start=False, stop=True)
                tom = self.cstep("tensor", [t_aT, prev_po.get(par)], omm,
                                 tg + "om")

                def evo(e, h=h, par=par, fc=fc):
                    e.tensor_copy(og[64 * (h % 2):64 * (h % 2) + 64,
                                     fc * TG:(fc + 1) * TG],
                                  po[par][64 * (h % 2):64 * (h % 2) + 64, :])
                teo = self.cstep("vector", [tom], evo, tg + "evo")
                prev_po[par] = teo
                prev_psc[par] = tat
                t_oev.append(teo)

            for tt in range(4):
                t = 4 * g + tt
                par = tt % 2

                def xcp(e, t=t, par=par):
                    e.tensor_copy(xld3[par][:], self.stream[t][:])
                t_xc = self.cstep("gpsimd", [prev_res.get(par)], xcp, tg + "xc")
                newt = self.stream[t]
                tevs = []
                for half in range(2):
                    parp = half

                    def pmm(e, tt=tt, half=half, parp=parp):
                        last = None
                        for kc in range(8):
                            last = e.matmul(
                                pA[parp][:],
                                og[:, kc * TG + 128 * tt:kc * TG + 128 * (tt + 1)],
                                WoutT[:, kc * D + 512 * half:kc * D + 512 * (half + 1)],
                                start=(kc == 0), stop=False)
                        return e.matmul(
                            pA[parp][:], self.onesr_bf[0:1, :],
                            self.rows_bf[0:1, 4 * D + 512 * half:
                                         4 * D + 512 * (half + 1)],
                            start=False, stop=True)
                    tm = self.cstep("tensor", [prev_pA.get(parp), self.t_rowsbf]
                                    + t_oev, pmm, tg + "pm")

                    def pev(e, t=t, half=half, parp=parp, par=par, newt=newt):
                        e.tensor_tensor(newt[:, 512 * half:512 * (half + 1)],
                                        pA[parp][:],
                                        xld3[par][:, 512 * half:512 * (half + 1)],
                                        ALU.add)
                    tev = self.cstep("vector", [tm, t_xc], pev, tg + "pev")
                    prev_pA[parp] = tev
                    tevs.append(tev)
                prev_res[par] = tevs[-1]
            prev_lnld = t_qkv[-1]
        self.t_stream_done = [prev_res[0], prev_res[1]]
        self.free(*po, ptO, ptE, *psc, *pA, *xld3, og, attT, attb, sm3, esb,
                  vg, kg, qg, ln3g, WoutT, WqkvT,
                  after=self.t_stream_done + list(prev_pA.values()))

    def finish(self):
        if self.n_sub == 4:
            w1 = self._resolve("sync", self.t_stream_done)
            self.rawstep("sync", lambda e: [e.wait_ge(s_.h, v) for s_, v in w1]
                         or e.nop(), "fin")
            return
        if self.stream[0] is None:
            zt = self.sb("zfin", [128, D], F32)
            tz = self.cstep("vector", [], lambda e: e.memset(zt[:], 0.0), "zfin")

            def yz(e, dma):
                for t in range(NTT):
                    dma(out=self.y_h[128 * t:128 * (t + 1), :], in_=zt[:])
            toks = [self.dstep("sync", [tz] + getattr(self, "_stage_toks", []),
                               yz, "yz")]

            w0 = self._resolve("sync", toks)
            self.rawstep("sync", lambda e: [e.wait_ge(s_.h, v) for s_, v in w0],
                         "fin0")
            return
        if True:
            def yspl(e, dma):
                for t in range(NTT):
                    dma(out=self.y_h[128 * t:128 * (t + 1), :], in_=self.stream[t][:])
            deps = getattr(self, "t_stream_done", []) + \
                getattr(self, "_stage_toks", [])
            toks = [self.dstep("sync", deps, yspl, "yspl")]

        w1 = self._resolve("sync", toks)
        self.rawstep("sync", lambda e: [e.wait_ge(s_.h, v) for s_, v in w1],
                     "fin")


def build(n_sub=4, stage=99):
    return KB(n_sub, stage).build()


_IN_MAP = {
    "aWq": "L0_attn_a_Wq", "abq": "L0_attn_a_bq", "aWk": "L0_attn_a_Wk",
    "abk": "L0_attn_a_bk", "aWv": "L0_attn_a_Wv", "abv": "L0_attn_a_bv",
    "aWo": "L0_attn_a_Wo", "abo": "L0_attn_a_bo",
    "bWqkv": "L0_attn_b_W_qkv", "bbqkv": "L0_attn_b_b_qkv",
    "bWout": "L0_attn_b_W_out", "bbout": "L0_attn_b_b_out",
    "fw1": "L0_ffn_w1", "fcb": "L0_ffn_conv_b", "fw2": "L0_ffn_w2",
}


def _run(inputs, n_sub=4, stage=99):
    x = np.ascontiguousarray(np.asarray(inputs["L0_attn_a_x"],
                                        dtype=np.float32).reshape(8 * T, D))
    nc = build(n_sub, stage)
    base = {k: np.ascontiguousarray(np.asarray(inputs[v], np.float32))
            for k, v in _IN_MAP.items()}
    base["fcw"] = np.ascontiguousarray(
        np.asarray(inputs["L0_ffn_conv_w"], np.float32).reshape(3 * DHID))
    for k in list(base):
        if base[k].ndim > 1 and k not in ("fcw",):
            base[k] = base[k].reshape(base[k].shape[-2], base[k].shape[-1]) \
                if base[k].ndim == 2 else base[k]
    in_maps = []
    for c in range(8):
        m = dict(base)
        m["x"] = x[T * c:T * (c + 1)]
        pm = np.zeros((128, 8), np.float32)
        pm[:, :c] = 1.0
        m["pmask"] = pm
        hs = np.zeros((16, 2), np.float32)
        if c > 0:
            hs[2 * (c - 1) + 1, 0] = 1.0
        if c < 7:
            hs[2 * (c + 1), 1] = 1.0
        m["hsel"] = hs
        in_maps.append(m)
    res = run_bass_kernel_spmd(nc, in_maps, core_ids=list(range(8)))
    y = np.concatenate([res.results[c]["y"] for c in range(8)], axis=0)
    return y.reshape(1, 8 * T, D).astype(np.float32)


def kernel(**inputs):
    return _run(inputs, n_sub=4)


# version 19 (final)
# speedup vs baseline: 1.0162x; 1.0005x over previous
import numpy as np
import ml_dtypes
from contextlib import ExitStack

import concourse.bass as bass
import concourse.mybir as mybir
from concourse.bass_utils import run_bass_kernel_spmd

F32 = mybir.dt.float32
BF16 = mybir.dt.bfloat16
I32 = mybir.dt.int32
AF = mybir.ActivationFunctionType
ALU = mybir.AluOpType
AX = mybir.AxisListType

T = 2048
D = 1024
H = 16
HD = 64
CH = 128
NCH = T // CH
DHID = 4096
NTT = T // 128
TG = 512
NG = T // TG


class Sem:
    __slots__ = ("h", "n", "fn")

    def __init__(self, es, nc, name):
        self.h = es.enter_context(nc.semaphore(name))
        self.n = 0
        self.fn = 0


class KB:
    def __init__(self, n_sub=4, stage=99):
        self.stage = stage
        self.n_sub = n_sub
        self.nc = bass.Bass("TRN2", target_bir_lowering=False)
        self.es = ExitStack()
        self.uid = 0
        self._cms = {}
        self._seq = 0
        self._free_toks = {}
        self.waited = {e: {} for e in ("sync", "tensor", "vector", "scalar", "gpsimd")}

    def _nm(self, p):
        self.uid += 1
        return f"{p}{self.uid}"

    def sb(self, p, shape, dtype, side=None):
        cm = self.nc.sbuf_tensor(self._nm(p), shape, dtype, side=side)
        t = cm.__enter__()
        self._seq += 1
        self._cms[id(t)] = (self._seq, cm)
        return t

    def ps(self, p, shape, dtype=F32):
        cm = self.nc.psum_tensor(self._nm(p), shape, dtype)
        t = cm.__enter__()
        self._seq += 1
        self._cms[id(t)] = (self._seq, cm)
        return t

    def free(self, *ts, after=()):
        self.flush()
        for tok in after:
            if tok is None:
                continue
            sem, v = tok
            cur = self._free_toks.get(id(sem))
            if cur is None or cur[1] < v:
                self._free_toks[id(sem)] = (sem, v)
        items = sorted((self._cms.pop(id(t)) for t in ts), key=lambda x: -x[0])
        for _, cm in items:
            cm.__exit__(None, None, None)

    def dram(self, name, shape, dtype, kind=None):
        if kind is None:
            return self.nc.dram_tensor(name, shape, dtype).ap()
        return self.nc.dram_tensor(name, shape, dtype, kind=kind).ap()

    def _resolve(self, eng, after):
        own = self.esem[eng]
        out = []
        w = self.waited[eng]
        for tok in after:
            if tok is None:
                continue
            sem, v = tok
            if sem is own or v is None:
                continue
            if w.get(id(sem), -1) >= v:
                continue
            out.append((sem, v))
            w[id(sem)] = v
        return out

    _CHAIN_OPS = frozenset((
        "tensor_tensor", "tensor_scalar", "tensor_copy", "tensor_reduce",
        "reduce_max", "reduce_sum", "reciprocal", "memset", "select",
        "scalar_tensor_tensor", "tensor_tensor_reduce", "affine_select",
        "iota", "tensor_add", "tensor_mul", "tensor_sub", "tensor_max",
        "bn_stats", "bn_aggr", "pool_max", "pool_avg", "transpose",
        "tensor_scalar_add", "tensor_scalar_mul", "tensor_scalar_max",
        "tensor_scalar_min", "copy_predicated", "max",
        "activation", "copy", "add", "mul", "sqrt", "square", "sign"))

    class _CountEng:

        class _Fake:
            def then_inc(self, *a):
                return self

        def __init__(self):
            self.n_chain = 0

        def __getattr__(self, name):
            if name in KB._CHAIN_OPS:
                def f(*a, **kw):
                    self.n_chain += 1
                    return KB._CountEng._Fake()
                return f
            return lambda *a, **kw: KB._CountEng._Fake()

    class _ChainEng:

        def __init__(self, kb, eng, e):
            self._kb, self._eng, self._e = kb, eng, e

        def __getattr__(self, name):
            f = getattr(self._e, name)
            if name not in KB._CHAIN_OPS:
                return f
            kb, eng, e = self._kb, self._eng, self._e

            def wrapped(*a, **kw):
                sem = kb.esem[eng]
                w = kb.fwaited[eng]
                if sem.fn > 0 and w.get(id(sem), -1) < sem.fn:
                    e.wait_ge(sem.h, sem.fn)
                    w[id(sem)] = sem.fn
                ins = f(*a, **kw)
                ins.then_inc(sem.h, 1)
                sem.fn += 1
                return ins
            return wrapped

    def cstep(self, eng, after, fn, tag=""):
        sem = self.esem[eng]
        chained = eng in ("vector", "gpsimd", "scalar")
        waits = self._resolve(eng, list(self._free_toks.values()) + list(after))
        if chained:
            ce = KB._CountEng()
            fn(ce)
            assert ce.n_chain > 0, f"cstep {tag}: no chained ops"
            sem.n += ce.n_chain
            self.pending[eng].append(("c", waits, fn, ce.n_chain, tag))
        else:
            sem.n += 1
            self.pending[eng].append(("c1", waits, fn, 1, tag))
        return (sem, sem.n)

    def dma_alldone(self, eng):
        return [(ln, ln.n) for ln in self.dlanes[eng]]

    def tstep(self, after, fn, tag=""):
        return self.dstep("sync", list(after) + self.dma_alldone("gpsimd"),
                          fn, tag)

    def dstep(self, eng, after, fn, tag=""):
        self.dturn[eng] += 1
        lane = self.dlanes[eng][self.dturn[eng] % len(self.dlanes[eng])]
        waits = self._resolve(eng, list(self._free_toks.values()) + list(after)
                              + [(lane, lane.n)])
        ce = KB._CountEng()
        ndma = [0]

        def cdma(**kw):
            ndma[0] += 1
        fn(ce, cdma)
        lane.n += 16 * ndma[0]
        self.pending[eng].append(("d", waits, fn, (lane, ndma[0]), tag))
        return (lane, lane.n)

    def rawstep(self, eng, fn, tag=""):
        self.pending[eng].append(("r", [], fn, None, tag))

    def flush(self):
        for eng in ("sync", "tensor", "vector", "scalar", "gpsimd"):
            plist = self.pending[eng]
            if not plist:
                continue

            def body(e, eng=eng, plist=plist):
                sem = self.esem[eng]
                for kind, waits, fn, info, tag in plist:
                    for (s_, v) in waits:
                        e.wait_ge(s_.h, v)
                    if kind == "c":
                        n0 = sem.fn
                        fn(KB._ChainEng(self, eng, e))
                        assert sem.fn == n0 + info, \
                            f"{tag}: {sem.fn - n0} ops, predicted {info}"
                    elif kind == "c1":
                        last = fn(e)
                        assert last is not None, tag
                        last.then_inc(sem.h, 1)
                        sem.fn += 1
                    elif kind == "d":
                        lane, nd = info
                        cnt = [0]

                        def dma(**kw):
                            e.dma_start(**kw).then_inc(lane.h, 16)
                            cnt[0] += 1
                        fn(e, dma)
                        assert cnt[0] == nd, f"{tag}: dma count {cnt[0]} != {nd}"
                    else:
                        fn(e)
            getattr(self.blk, eng)(body)
            self.pending[eng] = []

    def build(self):
        nc = self.nc
        es = self.es
        self.x_h = self.dram("x", [T, D], F32, kind="ExternalInput")
        w = {}
        for n, shp in [("aWq", [D, D]), ("aWk", [D, D]), ("aWv", [D, D]),
                       ("aWo", [D, D]), ("bWqkv", [3 * D, D]), ("bWout", [D, D]),
                       ("fw1", [DHID, D]), ("fw2", [D, DHID])]:
            w[n] = self.dram(n, shp, F32, kind="ExternalInput")
        for n, ln in [("abq", D), ("abk", D), ("abv", D), ("abo", D),
                      ("bbqkv", 3 * D), ("bbout", D), ("fcw", 3 * DHID), ("fcb", DHID)]:
            w[n] = self.dram(n, [ln], F32, kind="ExternalInput")
        self.w = w
        self.pmask_h = self.dram("pmask", [128, 8], F32, kind="ExternalInput")
        self.hsel_h = self.dram("hsel", [16, 2], F32, kind="ExternalInput")
        self.y_h = self.dram("y", [T, D], F32, kind="ExternalOutput")

        self.ln_sp = self.dram("ln_sp", [T, D], BF16)
        self.attn_sp = self.dram("attn_sp", [T, D], BF16)
        self.k_sp = self.dram("k_sp", [D, T], BF16)
        self.spref_sp = self.dram("spref_sp", [NCH * 128, 512], F32)
        self.zpref_sp = self.dram("zpref_sp", [NCH * 128, 128], F32)
        self.cc_in = self.dram("cc_in", [128, 640], F32)
        self.cc_out = self.dram("cc_out", [8 * 128, 640], F32)
        self.cc2_in = self.dram("cc2_in", [2, D], F32)
        self.cc2_out = self.dram("cc2_out", [16, D], F32)
        self.cc3_in = self.dram("cc3_in", [2, D], F32)
        self.cc3_out = self.dram("cc3_out", [16, D], F32)
        self.wbf = {n: self.dram(n + "_bf", list(self.w[n].shape), BF16)
                    for n in ["aWq", "aWk", "aWv", "aWo", "bWqkv", "bWout"]}
        self.w1i_sp = self.dram("w1i_sp", [DHID, D], BF16)
        self.w2i_sp = self.dram("w2i_sp", [D, DHID], BF16)
        self.xq_sp = self.dram("xq_sp", [2064, D], BF16)
        self.h_sp = self.dram("h_sp", [DHID, T], BF16)
        self.hq_sp = self.dram("hq_sp", [T, DHID], BF16)
        self.x_sp = self.dram("x_sp", [T, D], F32)

        self.esem = {e: Sem(es, nc, "es_" + e)
                     for e in ("sync", "tensor", "vector", "scalar", "gpsimd")}
        self.pending = {e: [] for e in ("sync", "tensor", "vector", "scalar",
                                        "gpsimd")}
        self.fwaited = {e: {} for e in ("sync", "tensor", "vector", "scalar",
                                        "gpsimd")}
        self.dlanes = {"sync": [Sem(es, nc, f"dsy{i}") for i in range(8)],
                       "gpsimd": [Sem(es, nc, f"dgp{i}") for i in range(3)]}
        self.dturn = {"sync": 0, "gpsimd": 0}
        self.ccsem = Sem(es, nc, "ccs")

        self.ones_bf = self.sb("ones_bf", [128, 128], BF16)
        self.onesr_bf = self.sb("onesr_bf", [1, 128], BF16)
        self.zrow_bf = self.sb("zrow_bf", [1, 512], BF16)
        self.one_f32 = self.sb("one_f32", [1, 128], F32)
        self.ident_bf = self.sb("ident_bf", [128, 128], BF16)
        self.ident64 = self.sb("ident64", [128, 64], BF16)
        self.zbig = self.sb("zbig", [128, 512], BF16)
        self.mask4 = self.sb("mask4", [128, 512], F32)
        self.zero16 = self.sb("zero16", [16, D], BF16)
        self.rows_bf = self.sb("rows_bf", [1, 5 * D], BF16)
        self.rows2_bf = self.sb("rows2_bf", [1, 3 * D], BF16)
        self.convc = self.sb("convc", [128, 96], F32)
        self.convb = self.sb("convb", [128, 32], F32)
        self.pmask = self.sb("pmask", [128, 8], F32)
        self.hsel = self.sb("hsel", [16, 2], F32)
        self.cols = self.sb("cols", [128, 40], F32)

        self.stream = [None] * NTT

        with nc.Block("main") as blk:
            self.blk = blk
            self.phase0()
            if self.stage >= 2:
                self.sl1()
            if self.n_sub >= 2:
                self.sl_ffn(first=True)
            if self.n_sub >= 3:
                self.sl3()
            if self.n_sub >= 4:
                self.sl_ffn(first=False)
            self.finish()
            self.flush()
        self.es.close()
        return nc

    def phase0(self):
        rows = self.sb("rows", [1, 5 * D], F32)
        rows2 = self.sb("rows2", [1, 3 * D], F32)
        rowc = self.sb("rowc", [1, 3 * DHID], F32)
        rowcb = self.sb("rowcb", [1, DHID], F32)
        for i in range(NTT):
            self.stream[i] = self.sb("x0", [128, D], F32, side="right")

        def ld(e, dma):
            for i in range(NTT):
                dma(out=self.stream[i][:], in_=self.x_h[128 * i:128 * (i + 1), :])
            dma(out=self.pmask[:], in_=self.pmask_h)
            dma(out=self.hsel[:], in_=self.hsel_h)
            for nm, off in [("abq", 0), ("abk", D), ("abv", 2 * D),
                            ("abo", 3 * D), ("bbout", 4 * D)]:
                dma(out=rows[0:1, off:off + D],
                    in_=self.w[nm].rearrange("(a b) -> a b", a=1))
            dma(out=rows2[0:1, :],
                in_=self.w["bbqkv"].rearrange("(a b) -> a b", a=1))
            dma(out=rowc[0:1, :],
                in_=self.w["fcw"].rearrange("(a b) -> a b", a=1))
            dma(out=rowcb[0:1, :],
                in_=self.w["fcb"].rearrange("(a b) -> a b", a=1))
        self.t_x = self.dstep("sync", [], ld, "p0load")

        def consts(e):
            e.memset(self.ones_bf[:], 1.0)
            e.memset(self.onesr_bf[:], 1.0)
            e.memset(self.zrow_bf[:], 0.0)
            e.memset(self.zbig[:], 0.0)
            e.memset(self.one_f32[:], 1.0)
            e.memset(self.zero16[:], 0.0)
            e.memset(self.ident_bf[:], 1.0)
            e.affine_select(self.ident_bf[:], self.ident_bf[:], [[1, 128]],
                            ALU.is_equal, 0.0, base=0, channel_multiplier=-1)
            e.memset(self.mask4[:], 1.0)
            return e.affine_select(self.mask4[:], self.mask4[:], [[0, 4], [1, 128]],
                                   ALU.is_ge, 0.0, base=0, channel_multiplier=-1)
        self.t_const = self.cstep("gpsimd", [], consts, "consts")

        def casts(e, dma):
            for n in ["aWq", "aWk", "aWv", "aWo"]:
                dma(out=self.wbf[n], in_=self.w[n])
        self.t_acast = self.dstep("gpsimd", [], casts, "acast")

        pc = self.ps("pcols", [128, 32])

        def mkcols(e):
            last = None
            for j in range(8):
                last = e.matmul(pc[:, j:j + 1], rows[0:1, 128 * j:128 * (j + 1)],
                                self.one_f32[0:1, 0:1], start=(j == 0), stop=False)
            for j in range(8):
                last = e.matmul(pc[:, 8 + j:9 + j],
                                rows[0:1, D + 128 * j:D + 128 * (j + 1)],
                                self.one_f32[0:1, 0:1], start=False, stop=False)
            for j in range(16):
                last = e.matmul(pc[:, 16 + j:17 + j],
                                rows2[0:1, 128 * j:128 * (j + 1)],
                                self.one_f32[0:1, 0:1], start=False, stop=(j == 15))
            return last
        tcm = self.cstep("tensor", [self.t_x, self.t_const], mkcols, "mkcols")

        def evcols(e):
            return e.tensor_copy(self.cols[:, 0:32], pc[:])
        self.t_cols = self.cstep("vector", [tcm], evcols, "evcols")

        def bfrows(e):
            e.tensor_copy(self.rows_bf[:], rows[:])
            return e.tensor_copy(self.rows2_bf[:], rows2[:])
        self.t_rowsbf = self.cstep("vector", [self.t_x], bfrows, "bfrows")

        def mkid64(e):
            e.tensor_copy(self.ident64[0:64, :], self.ident_bf[0:64, 0:64])
            e.tensor_copy(self.ident64[64:128, :], self.ident_bf[0:64, 0:64])
        self.t_id64 = self.cstep("vector", [self.t_const], mkid64, "id64")

        self.free(pc, after=[self.t_cols])
        pc2 = self.ps("pconv", [128, 128])
        rcv = rowc[0:1, :].rearrange("a (c p k) -> a c p k", p=128, k=3)

        def mkconv(e):
            last = None
            for cc in range(32):
                for kk in range(3):
                    last = e.matmul(pc2[:, 3 * cc + kk:3 * cc + kk + 1],
                                    rcv[0:1, cc, :, kk],
                                    self.one_f32[0:1, 0:1],
                                    start=(cc == 0 and kk == 0), stop=False)
            for cc in range(32):
                last = e.matmul(pc2[:, 96 + cc:97 + cc],
                                rowcb[0:1, 128 * cc:128 * (cc + 1)],
                                self.one_f32[0:1, 0:1],
                                start=False, stop=(cc == 31))
            return last
        tcv = self.cstep("tensor", [self.t_x, self.t_cols], mkconv, "mkconv")

        def evconv(e):
            e.tensor_copy(self.convc[:], pc2[:, 0:96])
            return e.tensor_copy(self.convb[:], pc2[:, 96:128])
        self.t_conv = self.cstep("vector", [tcv], evconv, "evconv")
        self.free(pc2, rowc, rowcb, rows, rows2,
                  after=[self.t_conv, self.t_rowsbf, self.t_cols])

    def ln_pass(self, dst_sp, spill_x, tag):
        stats = self.sb(tag + "st", [128, 4 * NTT], F32)
        sq = self.sb(tag + "sq", [128, D], F32)
        lnt = [self.sb(tag + "ln", [128, D], BF16) for _ in range(2)]
        t_stats = []
        for t in range(NTT):
            xt = self.stream[t]
            tv = self.cstep("vector", [self.t_x], lambda e, xt=xt, t=t: e.tensor_reduce(
                stats[:, t:t + 1], xt[:], AX.X, ALU.add), "lnred")
            ts = self.cstep("scalar", [self.t_x], lambda e, xt=xt, t=t: e.activation(
                sq[:], xt[:], AF.Square,
                accum_out=stats[:, NTT + t:NTT + t + 1]), "lnsq")
            t_stats += [tv, ts]

        def colmath(e):
            M = stats[:, 0:NTT]
            S2 = stats[:, NTT:2 * NTT]
            V = stats[:, 2 * NTT:3 * NTT]
            e.tensor_scalar(M, M, 1.0 / D, None, ALU.mult)
            e.tensor_scalar(S2, S2, 1.0 / D, None, ALU.mult)
            e.tensor_tensor(V, M, M, ALU.mult)
            e.tensor_tensor(V, S2, V, ALU.subtract)
            return e.tensor_scalar(V, V, 1e-5, None, ALU.add)
        t1 = self.cstep("vector", t_stats, colmath, "lncol")
        t2 = self.cstep("scalar", [t1], lambda e: e.activation(
            stats[:, 2 * NTT:3 * NTT], stats[:, 2 * NTT:3 * NTT], AF.Sqrt), "lnsqrt")

        def recip(e):
            V = stats[:, 2 * NTT:3 * NTT]
            R = stats[:, 3 * NTT:4 * NTT]
            e.reciprocal(R, V)
            e.tensor_tensor(V, stats[:, 0:NTT], R, ALU.mult)
            return e.tensor_scalar(V, V, -1.0, None, ALU.mult)
        t3 = self.cstep("vector", [t2], recip, "lnrec")

        sp_tok = None
        sp_toks = []
        prev_spill = [None, None]
        for t in range(NTT):
            xt = self.stream[t]
            R = stats[:, 3 * NTT + t:3 * NTT + t + 1]
            NMR = stats[:, 2 * NTT + t:2 * NTT + t + 1]
            buf = lnt[t % 2]
            ta = self.cstep("scalar", [t3, prev_spill[t % 2]],
                            lambda e, xt=xt, buf=buf, R=R, NMR=NMR: e.activation(
                                buf[:], xt[:], AF.Identity, scale=R, bias=NMR), "lnap")

            def spl(e, dma, t=t, xt=xt, buf=buf):
                dma(out=dst_sp[128 * t:128 * (t + 1), :], in_=buf[:])
                if spill_x:
                    dma(out=self.x_sp[128 * t:128 * (t + 1), :], in_=xt[:])
            sp_tok = self.dstep("sync", [ta], spl, "lnspl")
            prev_spill[t % 2] = sp_tok
            sp_toks.append(sp_tok)
        self.free(stats, sq, *lnt, after=sp_toks)
        if spill_x:
            self._stream_spilled_toks = sp_toks
            self.free(*[self.stream[t] for t in range(NTT)], after=sp_toks)
            for t in range(NTT):
                self.stream[t] = None
        return sp_toks

    def sl1(self):
        nc = self.nc
        if self.stage < 3:
            self._stage_toks = []
            return
        t_lns = self.ln_pass(self.ln_sp, spill_x=True, tag="l1")
        if self.stage < 4:
            self._stage_toks = t_lns
            return
        self.q_fm = self.sb("q_fm", [128, 8 * T], BF16)
        self.k_fm = self.sb("k_fm", [128, 8 * T], BF16)
        self.v_tm = self.sb("v_tm", [128, NTT * D], BF16)
        self.WqT = self.sb("WqT", [128, 8 * D], BF16)
        self.WkT = self.sb("WkT", [128, 8 * D], BF16)
        self.WvT = self.sb("WvT", [128, 8 * D], BF16)

        def wload(e, dma):
            for dst, src in [(self.WqT, self.wbf["aWq"]), (self.WkT, self.wbf["aWk"]),
                             (self.WvT, self.wbf["aWv"])]:
                dma(out=dst[:].rearrange("p (c n) -> p c n", c=8),
                    in_=src, transpose=True)
        t_w = self.tstep([self.t_acast], wload, "s1wT")

        lnb = [self.sb("lnb", [128, 8 * TG], BF16) for _ in range(2)]
        pp = [self.ps("pp", [128, 512]) for _ in range(2)]
        e2b = [self.sb("e2b", [128, 512], F32) for _ in range(2)]
        rb = [self.sb("rb", [128, 512], F32) for _ in range(2)]
        prev_p = {}
        t_lnld = {}
        ti = 0
        for tb in range(4):
            def lnload(e, dma, tb=tb):
                dma(out=lnb[tb % 2][:].rearrange("p (c n) -> p c n", c=8),
                    in_=self.ln_sp[TG * tb:TG * (tb + 1), :], transpose=True)
            t_lnld[tb] = self.tstep(t_lns[4 * tb:4 * tb + 4]
                                    + [prev_p.get((tb - 2, "last"))],
                                    lnload, "s1lnld")
            lnc = lnb[tb % 2]
            for kind in ("q", "k", "v"):
                WT = {"q": self.WqT, "k": self.WkT, "v": self.WvT}[kind]
                if kind != "v":
                    dst = self.q_fm if kind == "q" else self.k_fm
                    boff = 0 if kind == "q" else 8
                    for oc in range(8):
                        par = ti % 2
                        ti += 1

                        def mm(e, WT=WT, oc=oc, par=par, lnc=lnc):
                            last = None
                            for kc in range(8):
                                last = e.matmul(
                                    pp[par][:],
                                    WT[:, kc * D + 128 * oc:kc * D + 128 * (oc + 1)],
                                    lnc[:, kc * TG:(kc + 1) * TG],
                                    start=(kc == 0), stop=(kc == 7))
                            return last
                        tm = self.cstep("tensor",
                                        [t_lnld[tb], t_w, prev_p.get(par),
                                         self.t_cols], mm, "s1proj")
                        bcol = self.cols[:, boff + oc:boff + oc + 1]
                        te = self.cstep("scalar", [tm], lambda e, par=par, bcol=bcol:
                                        e.activation(e2b[par][:], pp[par][:], AF.Exp,
                                                     bias=bcol), "s1exp")
                        dslice_off = oc * T + TG * tb

                        def stt(e, par=par, bcol=bcol, dst=dst, off=dslice_off):
                            e.tensor_scalar(rb[par][:], pp[par][:], bcol,
                                            0.0, ALU.add, ALU.max)
                            e.scalar_tensor_tensor(
                                dst[:, off:off + TG], e2b[par][:], 1.0, rb[par][:],
                                ALU.min, ALU.add)
                        tg = self.cstep("vector", [te], stt, "s1phi")
                        prev_p[par] = tg
                        prev_p[(tb, "last")] = tg
                else:
                    for tt4 in range(4):
                        tglob = 4 * tb + tt4
                        for half in range(2):
                            par = ti % 2
                            ti += 1

                            def mmv(e, tt4=tt4, half=half, par=par, lnc=lnc):
                                last = None
                                for kc in range(8):
                                    last = e.matmul(
                                        pp[par][:],
                                        lnc[:, kc * TG + 128 * tt4:kc * TG + 128 * (tt4 + 1)],
                                        WT[:, kc * D + 512 * half:kc * D + 512 * (half + 1)],
                                        start=(kc == 0), stop=False)
                                return e.matmul(
                                    pp[par][:], self.onesr_bf[0:1, :],
                                    self.rows_bf[0:1, 2 * D + 512 * half:2 * D + 512 * (half + 1)],
                                    start=False, stop=True)
                            tm = self.cstep("tensor",
                                            [t_lnld[tb], t_w, prev_p.get(par),
                                             self.t_rowsbf, self.t_const],
                                            mmv, "s1vproj")
                            voff = tglob * D + 512 * half
                            tv = self.cstep("vector", [tm], lambda e, par=par, voff=voff:
                                            e.tensor_copy(self.v_tm[:, voff:voff + 512],
                                                          pp[par][:]), "s1vev")
                            prev_p[par] = tv
                            prev_p[(tb, "last")] = tv
        t_proj_done = [prev_p[0], prev_p[1]]
        self.free(*lnb, *pp, *e2b, *rb)

        if self.stage < 5:
            self._stage_toks = t_proj_done
            return
        def kspill(e, dma):
            dma(out=self.k_sp.rearrange("(c p) t -> p c t", p=128),
                in_=self.k_fm[:].rearrange("p (c t) -> p c t", c=8))
        t_ks = self.dstep("sync", t_proj_done, kspill, "kspill")
        self.free(self.WqT, self.WkT, self.WvT, after=t_proj_done)
        self.k_tm = self.sb("k_tm", [128, NTT * D], BF16)

        def ktload(e, dma):
            dma(out=self.k_tm[:].rearrange("p (c n) -> p c n", c=NTT),
                in_=self.k_sp, transpose=True)
        t_kt = self.tstep([t_ks], ktload, "ktload")

        psd = [self.ps("psd", [128, 512]) for _ in range(2)]
        pzd = [self.ps("pzd", [128, 128]) for _ in range(2)]
        spref = [self.sb("spref", [128, 512], F32) for _ in range(2)]
        zpref = [self.sb("zpref", [128, 128], F32) for _ in range(2)]
        t_pref = {}
        t_prefdma = {}
        prev_ev = {}
        for ci in range(NCH):
            par = ci % 2

            def dmm(e, ci=ci, par=par):
                ko = ci * D
                z = self.zrow_bf
                e.matmul(pzd[par][:], z[0:1, 0:128], z[0:1, 0:128],
                         start=True, stop=False)
                e.matmul(psd[par][:], z[0:1, 0:128], z[0:1, 0:512],
                         start=True, stop=False)
                for h in range(H):
                    pb = 64 * (h % 2)
                    e.matmul(
                        psd[par][pb:pb + 64, 64 * (h // 2):64 * (h // 2) + 64],
                        self.k_tm[:, ko + 64 * h:ko + 64 * (h + 1)],
                        self.v_tm[:, ko + 64 * h:ko + 64 * (h + 1)],
                        start=False, stop=False)
                for h in range(H):
                    fc = h // 2
                    pb = 64 * (h % 2)
                    col = 16 * fc + h
                    e.matmul(pzd[par][pb:pb + 64, col:col + 1],
                             self.k_tm[:, ci * D + 64 * h:ci * D + 64 * (h + 1)],
                             self.ones_bf[:, 0:1],
                             start=False, stop=False)
                e.matmul(pzd[par][:], z[0:1, 0:128], z[0:1, 0:128],
                         start=False, stop=True)
                return e.matmul(psd[par][:], z[0:1, 0:128], z[0:1, 0:512],
                                start=False, stop=True)
            tm = self.cstep("tensor", [t_kt, prev_ev.get(par), self.t_const], dmm, "s1dS")

            def pref(e, ci=ci, par=par):
                if ci == 0:
                    e.tensor_copy(spref[par][:], psd[par][:])
                    return e.tensor_copy(zpref[par][:], pzd[par][:])
                e.tensor_tensor(spref[par][:], psd[par][:], spref[1 - par][:], ALU.add)
                return e.tensor_tensor(zpref[par][:], pzd[par][:], zpref[1 - par][:],
                                       ALU.add)
            deps = [tm] if ci == 0 else [tm, t_pref[ci - 1],
                                         t_prefdma.get(ci - 2)]
            t_pref[ci] = self.cstep("vector", deps, pref, "s1pref")
            prev_ev[par] = t_pref[ci]

            def prefspl(e, dma, ci=ci, par=par):
                dma(out=self.spref_sp[128 * ci:128 * (ci + 1), :], in_=spref[par][:])
                dma(out=self.zpref_sp[128 * ci:128 * (ci + 1), :], in_=zpref[par][:])
            t_prefdma[ci] = self.dstep("sync", [t_pref[ci]], prefspl, "s1prefspl")

        if self.stage < 6:
            self._stage_toks = list(t_prefdma.values())
            return
        def ccout(e, dma, par=(NCH - 1) % 2):
            dma(out=self.cc_in[:, 0:512], in_=spref[par][:])
            dma(out=self.cc_in[:, 512:640], in_=zpref[par][:])
        t_cco = self.dstep("gpsimd", [t_pref[NCH - 1]], ccout, "s1cco")

        ccw = self._resolve("gpsimd", [t_cco])
        self.ccsem.n += 1
        ccv = self.ccsem.n

        def docc(e):
            for (s_, v) in ccw:
                e.wait_ge(s_.h, v)
            e.collective_compute(
                "AllGather", ALU.bypass,
                replica_groups=[[0, 1, 2, 3, 4, 5, 6, 7]],
                ins=[self.cc_in], outs=[self.cc_out],
            ).then_inc(self.ccsem.h, 1)
        self.rawstep("gpsimd", docc, "s1cc")
        t_cc = (self.ccsem, ccv)

        self.free(*zpref, *spref, *pzd, *psd, self.k_tm,
                  after=[t_cco, t_pref[NCH - 1]] + list(t_prefdma.values()))
        gath = self.sb("gath", [128, 8 * 640], F32)

        def ccin(e, dma):
            for r in range(8):
                dma(out=gath[:, 640 * r:640 * (r + 1)],
                    in_=self.cc_out[128 * r:128 * (r + 1), :])
        t_gin = self.dstep("gpsimd", [t_cc], ccin, "s1gin")

        sacc = self.sb("sacc", [128, 640], F32)
        stmp = self.sb("stmp", [128, 640], F32)

        def prefsum(e):
            e.tensor_scalar(sacc[:], gath[:, 0:640], self.pmask[:, 0:1], None, ALU.mult)
            last = None
            for r in range(1, 8):
                e.tensor_scalar(stmp[:], gath[:, 640 * r:640 * (r + 1)],
                                self.pmask[:, r:r + 1], None, ALU.mult)
                last = e.tensor_tensor(sacc[:], sacc[:], stmp[:], ALU.add)
            return last
        t_sin = self.cstep("vector", [t_gin, self.t_x], prefsum, "s1prefsum")
        S_in = sacc[:, 0:512]
        z_in = sacc[:, 512:640]

        if self.stage < 7:
            self._stage_toks = [t_sin]
            return
        sprefB = [self.sb("sprefB", [128, 512], F32) for _ in range(2)]
        zprefB = [self.sb("zprefB", [128, 128], F32) for _ in range(2)]
        seff = [self.sb("seff", [128, 512], BF16) for _ in range(2)]
        zeff = self.sb("zeff", [128, 128], F32)
        zhi = [self.sb("zhi", [128, 128], BF16) for _ in range(2)]
        zlo = [self.sb("zlo", [128, 128], BF16) for _ in range(2)]
        msc = [self.sb("msc", [128, 2048], BF16) for _ in range(2)]
        attn_c = [self.sb("attn_c", [128, D], BF16) for _ in range(2)]
        psc = [self.ps("psc", [128, 512]) for _ in range(2)]
        pnum = [self.ps("pnum", [128, 512]) for _ in range(4)]
        pden = [self.ps("pden", [128, 16]) for _ in range(2)]
        rden = self.sb("rden", [128, 16], F32)
        t_attnspl = {}
        t_evnum = None
        t_eff_prev = None
        t_numprev = {}
        t_evprev = {}
        t_mev = {}
        for ci in range(NCH):
            if ci > 0:
                def prefld(e, dma, ci=ci):
                    dma(out=sprefB[ci % 2][:],
                        in_=self.spref_sp[128 * (ci - 1):128 * ci, :])
                    dma(out=zprefB[ci % 2][:],
                        in_=self.zpref_sp[128 * (ci - 1):128 * ci, :])
                t_pl = self.dstep("sync", [t_prefdma[ci - 1], t_eff_prev], prefld,
                                  "s1Bpl")
            else:
                t_pl = None

            def eff(e, ci=ci, cp=ci % 2):
                if ci == 0:
                    e.tensor_copy(seff[cp][:], S_in)
                    e.tensor_copy(zeff[:], z_in)
                else:
                    e.tensor_tensor(seff[cp][:], S_in, sprefB[ci % 2][:], ALU.add)
                    e.tensor_tensor(zeff[:], z_in, zprefB[ci % 2][:], ALU.add)
                e.tensor_copy(zhi[cp][:], zeff[:])
                return e.tensor_tensor(zlo[cp][:], zeff[:], zhi[cp][:],
                                       ALU.subtract)
            t_eff = self.cstep("gpsimd", [t_pl, t_sin, t_numprev.get(ci % 2)],
                               eff, "s1eff")
            t_eff_prev = t_eff
            if self.stage < 72:
                self._stage_toks = [t_eff]
                continue

            mscv = msc[ci % 2][:].rearrange("p (u two c) -> p two u c",
                                            two=2, c=128)
            for hg in range(4):
                par = hg % 2
                heads = [(hg % 2) + 8 * (hg // 2) + 2 * j for j in range(4)]

                def smm(e, ci=ci, heads=heads, par=par):
                    last = None
                    pb = 64 * (heads[0] % 2)
                    for j, h in enumerate(heads):
                        fc = h // 2
                        last = e.matmul(
                            psc[par][:, 128 * j:128 * (j + 1)],
                            self.k_fm[pb:pb + 64, fc * T + CH * ci:fc * T + CH * (ci + 1)],
                            self.q_fm[pb:pb + 64, fc * T + CH * ci:fc * T + CH * (ci + 1)],
                            start=(j == 0), stop=(j == 3))
                    return last
                tsc = self.cstep("tensor", [t_mev.get(par)] + t_proj_done, smm,
                                 "s1sc")

                def mev(e, hg=hg, par=par, mscv=mscv):
                    u0 = 4 * (hg // 2)
                    return e.tensor_tensor(
                        mscv[:, hg % 2, u0:u0 + 4, :],
                        psc[par][:].rearrange("p (a c) -> p a c", c=128),
                        self.mask4[:].rearrange("p (a c) -> p a c", c=128),
                        ALU.mult)
                t_mev[par] = self.cstep("vector", [tsc, self.t_const], mev, "s1mev")
            if self.stage < 73:
                self._stage_toks = [t_mev[0], t_mev[1]]
                continue

            def nummm(e, ci=ci, cp=ci % 2):
                last = None
                for h in range(H):
                    bk = 2 * cp + h // 8
                    first = (h % 8 == 0)
                    lasth = (h % 8 == 7)
                    hcol = 64 * (h % 8)
                    last = e.matmul(
                        pnum[bk][:, hcol:hcol + 64],
                        msc[cp][:, 128 * h:128 * (h + 1)],
                        self.v_tm[:, ci * D + 64 * h:ci * D + 64 * (h + 1)],
                        start=first, stop=False)
                    fc = h // 2
                    pb = 64 * (h % 2)
                    last = e.matmul(
                        pnum[bk][:, hcol:hcol + 64],
                        self.q_fm[pb:pb + 64, fc * T + CH * ci:fc * T + CH * (ci + 1)],
                        seff[cp][pb:pb + 64, 64 * (h // 2):64 * (h // 2) + 64],
                        start=False, stop=lasth)
                for h in range(H):
                    last = e.matmul(pden[cp][:, h:h + 1],
                                    msc[cp][:, 128 * h:128 * (h + 1)],
                                    self.ones_bf[:, 0:1], start=(h == 0),
                                    stop=False)
                for fc in range(8):
                    last = e.matmul(pden[cp][:, 0:16],
                                    self.q_fm[:, fc * T + CH * ci:fc * T + CH * (ci + 1)],
                                    zhi[cp][:, 16 * fc:16 * (fc + 1)],
                                    start=False, stop=False)
                    last = e.matmul(pden[cp][:, 0:16],
                                    self.q_fm[:, fc * T + CH * ci:fc * T + CH * (ci + 1)],
                                    zlo[cp][:, 16 * fc:16 * (fc + 1)],
                                    start=False, stop=(fc == 7))
                return last
            t_num = self.cstep("tensor", [t_mev[0], t_mev[1], t_eff, self.t_const,
                                          t_evprev.get(ci % 2),
                                          t_attnspl.get(ci - 2)], nummm, "s1num")
            t_numprev[ci % 2] = t_num

            def evnum(e, ci=ci, cp=ci % 2):
                e.reciprocal(rden[:], pden[cp][:])
                buf = attn_c[ci % 2]
                rv = rden[:].unsqueeze(2).broadcast_to((128, 16, 64))
                e.tensor_tensor(buf[:, 0:512].rearrange("p (h e) -> p h e", h=8),
                                pnum[2 * cp][:].rearrange("p (h e) -> p h e", h=8),
                                rv[:, 0:8, :], ALU.mult)
                return e.tensor_tensor(
                    buf[:, 512:1024].rearrange("p (h e) -> p h e", h=8),
                    pnum[2 * cp + 1][:].rearrange("p (h e) -> p h e", h=8),
                    rv[:, 8:16, :], ALU.mult)
            t_evnum = self.cstep("vector", [t_num], evnum, "s1evnum")
            t_evprev[ci % 2] = t_evnum

            def aspl(e, dma, ci=ci):
                dma(out=self.attn_sp[128 * ci:128 * (ci + 1), :],
                    in_=attn_c[ci % 2][:])
            t_attnspl[ci] = self.dstep("sync", [t_evnum], aspl, "s1aspl")

        if self.stage < 80:
            self._stage_toks = list(t_attnspl.values()) or \
                list(getattr(self, "_stage_toks", []))
            return
        self.free(self.q_fm, self.k_fm, self.v_tm,
                  *sprefB, *zprefB, *seff, zeff, *zhi, *zlo, *msc, *attn_c,
                  *psc, *pnum, *pden, rden, gath, sacc, stmp,
                  after=[t_attnspl[NCH - 1], t_evnum])

        self.WoT = self.sb("WoT", [128, 8 * D], BF16)
        self.attn_fm = self.sb("attn_fm", [128, 8 * T], BF16)

        def wo_ld(e, dma):
            dma(out=self.WoT[:].rearrange("p (c n) -> p c n", c=8),
                in_=self.wbf["aWo"], transpose=True)
        t_wo = self.tstep([self.t_acast], wo_ld, "s1woT")

        afmv = self.attn_fm[:].rearrange("p (c n) -> p c n", c=8)
        t_afm_h = []
        for hf in range(2):
            t_afm_h.append(self.tstep(
                [t_attnspl[ci] for ci in range(8 * hf, 8 * hf + 8)],
                lambda e, dma, hf=hf: dma(
                    out=afmv[:, :, 1024 * hf:1024 * (hf + 1)],
                    in_=self.attn_sp[1024 * hf:1024 * (hf + 1), :],
                    transpose=True), "s1afm"))
        self.proj_tm_residual(self.attn_fm, t_afm_h, self.WoT, t_wo,
                              self.rows_bf, 3 * D, "s1o", from_dram=True)
        self.free(self.WoT, self.attn_fm, after=self.t_stream_done)

    def proj_tm_residual(self, src_fm, t_src, WT, t_wt, bias_rows, bias_off, tag,
                         from_dram):
        pr = [self.ps(tag + "pr", [128, 512]) for _ in range(4)]
        xload = [self.sb(tag + "xl", [128, D], F32) for _ in range(2)] \
            if from_dram else None
        prev = {}
        for t in range(NTT):
            if from_dram:
                def xld(e, dma, t=t):
                    dma(out=xload[t % 2][:], in_=self.x_sp[128 * t:128 * (t + 1), :])
                t_xl = self.dstep("sync", [prev.get((t - 2, "res"))], xld, tag + "xld")
            newt = self.sb(tag + "nx", [128, D], F32, side="right")
            for half in range(2):
                par = (2 * t + half) % 4

                def mm(e, t=t, half=half, par=par):
                    last = None
                    for kc in range(8):
                        last = e.matmul(
                            pr[par][:],
                            src_fm[:, kc * T + 128 * t:kc * T + 128 * (t + 1)],
                            WT[:, kc * D + 512 * half:kc * D + 512 * (half + 1)],
                            start=(kc == 0), stop=False)
                    return e.matmul(
                        pr[par][:], self.onesr_bf[0:1, :],
                        bias_rows[0:1, bias_off + 512 * half:bias_off + 512 * (half + 1)],
                        start=False, stop=True)
                tsrc = t_src[t // 8] if isinstance(t_src, list) else t_src
                tm = self.cstep("tensor", [tsrc, t_wt, prev.get((t - 2, half)),
                                           self.t_rowsbf], mm, tag + "mm")
                xsrc = xload[t % 2] if from_dram else self.stream[t]
                deps = [tm] + ([t_xl] if from_dram else [])

                def ev(e, t=t, half=half, par=par, newt=newt, xsrc=xsrc):
                    return e.tensor_tensor(newt[:, 512 * half:512 * (half + 1)],
                                           pr[par][:],
                                           xsrc[:, 512 * half:512 * (half + 1)],
                                           ALU.add)
                tev = self.cstep("vector", deps, ev, tag + "ev")
                prev[(t, half)] = tev
                prev[(t, "res")] = tev
            if not from_dram:
                self.free(self.stream[t], after=[prev[(t, "res")]])
            self.stream[t] = newt
        self.t_stream_done = [prev[(t, "res")] for t in range(NTT)]
        self.free(*pr, after=self.t_stream_done)
        if from_dram:
            self.free(*xload, after=self.t_stream_done)

    def quant_weights(self):
        wcol = self.sb("wcol", [128, 4], F32)
        self.convc_s = self.sb("convcs", [128, 96], F32)
        stg = [self.sb("wq", [128, DHID], F32) for _ in range(2)]
        qst = [self.sb("wqs", [128, DHID], BF16) for _ in range(2)]
        g2 = self.sb("wg2", [128, DHID], BF16)
        racc = self.sb("racc", [128, 3], F32)
        scal = self.sb("wsc", [1, 4], F32)
        pt = self.ps("pwq", [128, 4])
        t_r = self.cstep("vector", [], lambda e: e.memset(racc[:], 0.0), "wr0")
        spills = {"fw1": [], "fw2": []}
        t_wcol = None
        for wi, (wn, rows, cols, sp) in enumerate(
                [("fw1", DHID, D, self.w1i_sp), ("fw2", D, DHID, self.w2i_sp)]):
            ntile = rows // 128
            t_ld = {}
            prev = {}
            for i in range(ntile):
                def ld(e, dma, i=i, wn=wn, cols=cols):
                    dma(out=stg[i % 2][:, 0:cols],
                        in_=self.w[wn][128 * i:128 * (i + 1), :])
                t_ld[i] = self.dstep("sync", [prev.get(i % 2)], ld, "wql")

                def red(e, i=i, wi=wi, cols=cols):
                    e.tensor_reduce(racc[:, 2:3], stg[i % 2][:, 0:cols], AX.X,
                                    ALU.add, apply_absolute_value=True)
                    e.tensor_tensor(racc[:, wi:wi + 1], racc[:, wi:wi + 1],
                                    racc[:, 2:3], ALU.add)
                prev[i % 2] = self.cstep("vector", [t_ld[i], t_r], red, "wqr")
            t_p1 = [prev[0], prev[1]]

            onec = self.sb("onec", [128, 1], F32)
            t_oc = self.cstep("vector", [], lambda e: e.memset(onec[:], 1.0), "oc")

            def tot2(e, wi=wi):
                return e.matmul(pt[0:1, wi:wi + 1], racc[:, wi:wi + 1], onec[:],
                                start=True, stop=True)
            t_tot = self.cstep("tensor", t_p1 + [t_oc], tot2, "wqt")

            def mkscal(e, wi=wi, rows=rows, cols=cols):
                e.tensor_scalar(scal[0:1, wi:wi + 1], pt[0:1, wi:wi + 1],
                                1.0 / (rows * cols), 1e-5, ALU.mult, ALU.max)
                e.reciprocal(scal[0:1, 2 + wi:3 + wi], scal[0:1, wi:wi + 1])
            t_sc = self.cstep("vector", [t_tot], mkscal, "wqsc")

            def bcast(e, wi=wi):
                e.matmul(pt[:, 2:3], self.one_f32[0:1, :],
                         scal[0:1, 2 + wi:3 + wi], start=True, stop=False)
                return e.matmul(pt[:, 3:4], self.one_f32[0:1, :],
                                scal[0:1, wi:wi + 1], start=False, stop=True)
            t_bc = self.cstep("tensor", [t_sc], bcast, "wqbc")

            def evc(e, wi=wi):
                e.tensor_copy(wcol[:, wi:wi + 1], pt[:, 2:3])
                e.tensor_copy(wcol[:, 2 + wi:3 + wi], pt[:, 3:4])
            t_wcol = self.cstep("vector", [t_bc], evc, "wqec")

            prev2 = {}
            for i in range(ntile):
                def ld2(e, dma, i=i, wn=wn, cols=cols):
                    dma(out=stg[i % 2][:, 0:cols],
                        in_=self.w[wn][128 * i:128 * (i + 1), :])
                tl = self.dstep("sync", [prev2.get(i % 2, t_p1[i % 2])], ld2, "wql2")

                def qz(e, i=i, wi=wi, cols=cols):
                    ws = wcol[:, wi:wi + 1]
                    e.tensor_scalar(qst[i % 2][:, 0:cols], stg[i % 2][:, 0:cols],
                                    ws, 0.5, ALU.mult, ALU.is_gt)
                    e.tensor_scalar(g2[:, 0:cols], stg[i % 2][:, 0:cols],
                                    ws, -0.5, ALU.mult, ALU.is_lt)
                    e.tensor_tensor(qst[i % 2][:, 0:cols], qst[i % 2][:, 0:cols],
                                    g2[:, 0:cols], ALU.subtract)
                tq = self.cstep("vector", [tl, t_wcol], qz, "wqq")

                def spl(e, dma, i=i, sp=sp, cols=cols):
                    dma(out=sp[128 * i:128 * (i + 1), :], in_=qst[i % 2][:, 0:cols])
                ts_ = self.dstep("sync", [tq], spl, "wqspl")
                prev2[i % 2] = ts_
                spills[wn].append(ts_)
            self.free(onec, after=[t_tot])
        def foldc(e):
            e.tensor_scalar(self.convc_s[:], self.convc[:], wcol[:, 2:3],
                            None, ALU.mult)
        t_cf = self.cstep("vector", [t_wcol, self.t_conv], foldc, "convf")
        self.wcol = wcol
        self._wq_done = (spills["fw1"], spills["fw2"], t_cf, t_wcol)
        self.free(pt, g2, *qst, *stg, racc, scal,
                  after=spills["fw1"] + spills["fw2"] + [t_cf])

    def halo_exchange(self, cin, cout, tag):
        def put(e, dma):
            dma(out=cin[0:1, :], in_=self.stream[0][0:1, :])
            dma(out=cin[1:2, :], in_=self.stream[NTT - 1][127:128, :])
        t_put = self.dstep("gpsimd", getattr(self, "t_stream_done", []), put,
                           tag + "put")

        ccw = self._resolve("gpsimd", [t_put])
        self.ccsem.n += 1
        ccv = self.ccsem.n

        def docc(e):
            for (s_, v) in ccw:
                e.wait_ge(s_.h, v)
            e.collective_compute(
                "AllGather", ALU.bypass, replica_groups=[[0, 1, 2, 3, 4, 5, 6, 7]],
                ins=[cin], outs=[cout]).then_inc(self.ccsem.h, 1)
        self.rawstep("gpsimd", docc, tag + "cc")
        t_cc = (self.ccsem, ccv)
        halo = self.sb(tag + "hx", [2, D], F32)
        gat = self.sb(tag + "g", [16, D], F32)
        t_g = self.dstep("gpsimd", [t_cc],
                         lambda e, dma: dma(out=gat[:], in_=cout), tag + "gin")
        ph = self.ps(tag + "ph", [2, 512])

        def hsel_mm(e):
            e.matmul(ph[:, 0:512], self.hsel[:], gat[:, 0:512],
                     start=True, stop=False)
            return e.matmul(ph[:, 0:512], self.hsel[:], gat[:, 0:512],
                            start=False, stop=True)
        def hmm1(e):
            return e.matmul(ph[:], self.hsel[:], gat[:, 0:512],
                            start=True, stop=True)
        t_m1 = self.cstep("tensor", [t_g, self.t_x], hmm1, tag + "m1")
        t_e1 = self.cstep("vector", [t_m1],
                          lambda e: e.tensor_copy(halo[:, 0:512], ph[:]),
                          tag + "e1")

        def hmm2(e):
            return e.matmul(ph[:], self.hsel[:], gat[:, 512:1024],
                            start=True, stop=True)
        t_m2 = self.cstep("tensor", [t_e1], hmm2, tag + "m2")
        t_e2 = self.cstep("vector", [t_m2],
                          lambda e: e.tensor_copy(halo[:, 512:1024], ph[:]),
                          tag + "e2")
        self.free(ph, gat, after=[t_e2])
        return halo, t_e2

    def sl_ffn(self, first):
        idx = 2 if first else 4
        tg = f"f{idx}"
        if first:
            self.quant_weights()
        w1toks, w2toks, t_cf, t_wcol = self._wq_done

        cin = self.cc2_in if first else self.cc3_in
        cout = self.cc2_out if first else self.cc3_out
        halo, t_halo = self.halo_exchange(cin, cout, tg + "h")

        qs = self.sb(tg + "qs", [128, 8 * (NTT + 1)], F32)
        NS = NTT + 1
        SUM, SSQ, MX, MN, RS, NMRS, DQ, SCR = [k * NS for k in range(8)]
        sq = self.sb(tg + "sq", [128, D], F32)
        i32t = [self.sb(tg + "i32", [128, D], I32) for _ in range(2)]
        xqst = [self.sb(tg + "xq", [128, D], BF16) for _ in range(2)]
        t_z = self.cstep("vector", [], lambda e: e.memset(qs[:], 0.0), tg + "qz")
        t_stats = [t_z]
        for t in range(NTT + 1):
            if t < NTT:
                xt, xsl = self.stream[t], (slice(0, 128), slice(0, D))
                src = xt[:]
                deps = []
            else:
                src = halo[:]
                deps = [t_halo]

            def st_v(e, t=t, src=src):
                e.tensor_reduce(qs[0:src.shape[0], SUM + t:SUM + t + 1], src,
                                AX.X, ALU.add)
                e.tensor_reduce(qs[0:src.shape[0], MX + t:MX + t + 1], src,
                                AX.X, ALU.max)
                e.tensor_reduce(qs[0:src.shape[0], MN + t:MN + t + 1], src,
                                AX.X, ALU.min)
            tv = self.cstep("vector", deps, st_v, tg + "qsv")

            def st_s(e, t=t, src=src):
                e.activation(sq[0:src.shape[0], :], src, AF.Square,
                             accum_out=qs[0:src.shape[0], SSQ + t:SSQ + t + 1])
            ts_ = self.cstep("scalar", deps + [t_z], st_s, tg + "qss")
            t_stats += [tv, ts_]

        def colmath(e):
            S_, Q_, X_, N_ = qs[:, SUM:SUM + NS], qs[:, SSQ:SSQ + NS], \
                qs[:, MX:MX + NS], qs[:, MN:MN + NS]
            RS_, NM_, DQ_, SC_ = qs[:, RS:RS + NS], qs[:, NMRS:NMRS + NS], \
                qs[:, DQ:DQ + NS], qs[:, SCR:SCR + NS]
            e.tensor_scalar(S_, S_, 1.0 / D, None, ALU.mult)
            e.tensor_scalar(Q_, Q_, 1.0 / D, None, ALU.mult)
            e.tensor_tensor(SC_, S_, S_, ALU.mult)
            e.tensor_tensor(Q_, Q_, SC_, ALU.subtract)
            e.tensor_scalar(Q_, Q_, 1e-5, None, ALU.add)
        t_c1 = self.cstep("vector", t_stats, colmath, tg + "qc1")
        t_c2 = self.cstep("scalar", [t_c1], lambda e: e.activation(
            qs[:, SSQ:SSQ + NS], qs[:, SSQ:SSQ + NS], AF.Sqrt), tg + "qc2")

        def colmath2(e):
            S_, Q_, X_, N_ = qs[:, SUM:SUM + NS], qs[:, SSQ:SSQ + NS], \
                qs[:, MX:MX + NS], qs[:, MN:MN + NS]
            RS_, NM_, DQ_, SC_ = qs[:, RS:RS + NS], qs[:, NMRS:NMRS + NS], \
                qs[:, DQ:DQ + NS], qs[:, SCR:SCR + NS]
            e.reciprocal(SC_, Q_)
            e.tensor_tensor(X_, X_, S_, ALU.subtract)
            e.tensor_tensor(N_, S_, N_, ALU.subtract)
            e.tensor_tensor(X_, X_, N_, ALU.max)
            e.tensor_tensor(X_, X_, SC_, ALU.mult)
            e.tensor_scalar(X_, X_, 1e-5, None, ALU.max)
            e.tensor_scalar(DQ_, X_, 1.0 / 127.0, None, ALU.mult)
            e.reciprocal(N_, X_)
            e.tensor_scalar(N_, N_, 127.0, None, ALU.mult)
            e.tensor_tensor(RS_, SC_, N_, ALU.mult)
            e.tensor_tensor(NM_, S_, RS_, ALU.mult)
            e.tensor_scalar(NM_, NM_, -1.0, None, ALU.mult)
        t_c3 = self.cstep("vector", [t_c2], colmath2, tg + "qc3")

        t_xq = []
        prevq = {}
        for t in range(NTT + 1):
            par = t % 2
            if t < NTT:
                src = self.stream[t][:]
                rows = 128
            else:
                src = halo[:]
                rows = 2

            def ap1(e, t=t, src=src, par=par, rows=rows):
                e.activation(i32t[par][0:rows, :], src, AF.Identity,
                             scale=qs[0:rows, RS + t:RS + t + 1],
                             bias=qs[0:rows, NMRS + t:NMRS + t + 1])
            ta = self.cstep("scalar", [t_c3, prevq.get(par)], ap1, tg + "qt1")

            def ap3(e, t=t, par=par, rows=rows):
                e.tensor_scalar(xqst[par][0:rows, :], i32t[par][0:rows, :],
                                qs[0:rows, DQ + t:DQ + t + 1], None, ALU.mult)
            tc = self.cstep("vector", [ta], ap3, tg + "qdq")

            def spl(e, dma, t=t, par=par):
                if t < NTT:
                    dma(out=self.xq_sp[128 * t + 1:128 * (t + 1) + 1, :],
                        in_=xqst[par][:])
                    dma(out=self.x_sp[128 * t:128 * (t + 1), :],
                        in_=self.stream[t][:])
                else:
                    dma(out=self.xq_sp[0:1, :], in_=xqst[par][0:1, :])
                    dma(out=self.xq_sp[2049:2050, :], in_=xqst[par][1:2, :])
                    dma(out=self.xq_sp[2050:2064, :], in_=self.zero16[0:14, :])
            td = self.dstep("sync", [tc] + ([self.t_const] if t == NTT else []),
                            spl, tg + "qspl")
            prevq[par] = td
            t_xq.append(td)
        self.free(*xqst, *i32t, sq, after=t_xq)
        self.free(*[self.stream[t] for t in range(NTT)], after=t_xq)
        for t in range(NTT):
            self.stream[t] = None

        dqh = self.sb(tg + "dqh", [128, 2 * NTT], F32)
        w1iT = self.sb(tg + "w1T", [128, 8 * DHID], BF16)

        def w1ld(e, dma):
            dma(out=w1iT[:].rearrange("p (c n) -> p c n", c=8),
                in_=self.w1i_sp, transpose=True)
        t_w1 = self.tstep(w1toks, w1ld, tg + "w1T")

        htm = [self.sb(tg + "htm", [128, DHID], BF16) for _ in range(2)]
        i32h = [self.sb(tg + "i32h", [128, DHID], I32) for _ in range(2)]
        hqst = [self.sb(tg + "hqst", [128, DHID], BF16) for _ in range(2)]
        xqg = self.sb(tg + "xqg", [128, 8 * 528], BF16)
        hb = [self.sb(tg + "hb", [128, 514], BF16) for _ in range(4)]
        cb1 = [self.sb(tg + "cb1", [128, 512], F32) for _ in range(4)]
        cb2 = [self.sb(tg + "cb2", [128, 512], F32) for _ in range(4)]
        cbo = [self.sb(tg + "cbo", [128, 512], F32) for _ in range(4)]
        hst = self.sb(tg + "hst", [128, 4 * 512], BF16)
        pm = [self.ps(tg + "pm", [128, 512]) for _ in range(4)]
        ptl = [self.ps(tg + "ptl", [128, 2]) for _ in range(4)]
        t_hspl = []
        t_hq = []
        t_bc_prev = {}
        t_xqld_prev = None
        t_spl4 = None
        for g in range(NG):
            g_spl = []
            t_xqld = self.tstep(t_xq + [t_xqld_prev], lambda e, dma, g=g: dma(
                out=xqg[:].rearrange("p (c n) -> p c n", c=8),
                in_=self.xq_sp[TG * g:TG * g + 528, :], transpose=True),
                tg + "xql")
            prev_ev = {}
            prev_sil = {}
            for hc in range(32):
                par = hc % 4

                def w1mm2(e, g=g, hc=hc, par=par):
                    last = None
                    for kc in range(8):
                        e.matmul(pm[par][:],
                                 w1iT[:, kc * DHID + 128 * hc:kc * DHID + 128 * (hc + 1)],
                                 xqg[:, kc * 528:kc * 528 + 512],
                                 start=(kc == 0), stop=(kc == 7))
                    for kc in range(8):
                        last = e.matmul(
                            ptl[par][:],
                            w1iT[:, kc * DHID + 128 * hc:kc * DHID + 128 * (hc + 1)],
                            xqg[:, kc * 528 + 512:kc * 528 + 514],
                            start=(kc == 0), stop=(kc == 7))
                    return last
                tm = self.cstep("tensor", [t_xqld, t_w1, prev_ev.get(par),
                                           prev_sil.get(par)], w1mm2, tg + "w1m")

                def hev(e, hc=hc, par=par):
                    e.tensor_copy(hb[par][:, 0:512], pm[par][:])
                    e.tensor_copy(hb[par][:, 512:514], ptl[par][:])
                    e.tensor_scalar(cb1[par][:], hb[par][:, 0:512],
                                    self.convc_s[:, 3 * hc:3 * hc + 1],
                                    None, ALU.mult)
                    e.scalar_tensor_tensor(cb2[par][:], hb[par][:, 1:513],
                                           self.convc_s[:, 3 * hc + 1:3 * hc + 2],
                                           cb1[par][:], ALU.mult, ALU.add)
                    e.scalar_tensor_tensor(cbo[par][:], hb[par][:, 2:514],
                                           self.convc_s[:, 3 * hc + 2:3 * hc + 3],
                                           cb2[par][:], ALU.mult, ALU.add)
                tc3 = self.cstep("vector", [tm, t_cf], hev, tg + "hev")
                prev_ev[par] = tc3

                def sil(e, hc=hc, par=par):
                    e.activation(hst[:, 512 * (hc % 4):512 * (hc % 4 + 1)],
                                 cbo[par][:], AF.Silu,
                                 bias=self.convb[:, hc:hc + 1])
                ts_ = self.cstep("scalar", [tc3, t_spl4], sil, tg + "sil")
                prev_sil[par] = ts_
                if hc % 4 == 3:
                    q4 = hc // 4

                    def spl4(e, dma, g=g, q4=q4):
                        dma(out=self.h_sp[512 * q4:512 * (q4 + 1),
                                          TG * g:TG * (g + 1)].rearrange(
                                              "(c p) n -> p c n", p=128),
                            in_=hst[:].rearrange("p (c n) -> p c n", c=4))
                    t_spl4 = self.dstep("sync", [ts_], spl4, tg + "hspl")
                    t_hspl.append(t_spl4)
                    g_spl.append(t_spl4)
            t_xqld_prev = tm
            for tt in range(4):
                t = 4 * g + tt
                par = t % 2
                t_hl = self.tstep(g_spl + [t_bc_prev.get(par)],
                                  lambda e, dma, t=t, par=par: dma(
                    out=htm[par][:], in_=self.h_sp[:, 128 * t:128 * (t + 1)],
                    transpose=True), tg + "htl")

                def amax(e, t=t, par=par):
                    e.tensor_reduce(dqh[:, NTT + t:NTT + t + 1], htm[par][:],
                                    AX.X, ALU.max, apply_absolute_value=True)
                    e.tensor_scalar(dqh[:, NTT + t:NTT + t + 1],
                                    dqh[:, NTT + t:NTT + t + 1], 1e-5, None,
                                    ALU.max)
                    e.tensor_scalar(dqh[:, t:t + 1],
                                    dqh[:, NTT + t:NTT + t + 1],
                                    1.0 / 127.0, None, ALU.mult)
                    e.reciprocal(dqh[:, NTT + t:NTT + t + 1],
                                 dqh[:, NTT + t:NTT + t + 1])
                    e.tensor_scalar(dqh[:, NTT + t:NTT + t + 1],
                                    dqh[:, NTT + t:NTT + t + 1], 127.0, None,
                                    ALU.mult)
                t_am = self.cstep("vector", [t_hl], amax, tg + "amax")

                def rh1(e, t=t, par=par):
                    e.activation(i32h[par][:], htm[par][:], AF.Identity,
                                 scale=dqh[:, NTT + t:NTT + t + 1])
                    e.copy(hqst[par][:], i32h[par][:])
                t_r1 = self.cstep("scalar", [t_am], rh1, tg + "rh1")
                t_sp = self.dstep("sync", [t_r1], lambda e, dma, t=t, par=par:
                                  dma(out=self.hq_sp[128 * t:128 * (t + 1), :],
                                      in_=hqst[par][:]), tg + "hqspl")
                t_hq.append(t_sp)
                t_bc_prev[par] = t_sp
        self.free(xqg, *hb, *cb1, *cb2, *cbo, hst, *pm, *ptl,
                  after=t_hspl + [prev_sil[0], prev_sil[1]])
        self.free(*hqst, *i32h, *htm, after=t_hq)
        self.free(w1iT, after=t_hspl)

        w2iT = self.sb(tg + "w2T", [128, 32 * D], BF16)

        def w2ld(e, dma):
            dma(out=w2iT[:].rearrange("p (c n) -> p c n", c=32),
                in_=self.w2i_sp, transpose=True)
        t_w2 = self.tstep(w2toks, w2ld, tg + "w2T")
        hqfm = self.sb(tg + "hqfm", [128, 32 * TG], BF16)
        hqv = hqfm[:].rearrange("p (c n) -> p c n", c=32)
        xld = [self.sb(tg + "xld", [128, D], F32) for _ in range(2)]
        dqq = self.sb(tg + "dqq", [128, NTT], F32)
        pd = [self.ps(tg + "pd", [128, 512]) for _ in range(4)]
        t_dqq = self.cstep("vector", [t_wcol, t_c3] + t_hq, lambda e:
                           e.tensor_scalar(dqq[:, 0:NTT], dqh[:, 0:NTT],
                                           self.wcol[:, 3:4], None, ALU.mult),
                           tg + "dqq")
        newts = {}
        prev_hql = None
        prev_evd = {}
        for g in range(NG):
            t_hqls = []
            for qq in range(4):
                t_hqls.append(self.tstep(
                    t_hq + [prev_hql],
                    lambda e, dma, g=g, qq=qq: dma(
                        out=hqv[:, 8 * qq:8 * (qq + 1), :],
                        in_=self.hq_sp[TG * g:TG * (g + 1),
                                       1024 * qq:1024 * (qq + 1)],
                        transpose=True), tg + "hql"))
            for tt in range(4):
                t = 4 * g + tt

                def xl(e, dma, t=t, par=t % 2):
                    dma(out=xld[par][:], in_=self.x_sp[128 * t:128 * (t + 1), :])
                t_xl = self.dstep("sync", [prev_evd.get(t % 2)], xl, tg + "dxl")
                newt = self.sb(tg + "nx", [128, D], F32, side="right")
                tevs = []
                for half in range(2):
                    par = (2 * t + half) % 4

                    tms = []
                    for qq in range(4):
                        def dmmq(e, tt=tt, half=half, par=par, qq=qq):
                            last = None
                            for kc in range(8 * qq, 8 * (qq + 1)):
                                last = e.matmul(
                                    pd[par][:],
                                    hqfm[:, kc * TG + 128 * tt:kc * TG + 128 * (tt + 1)],
                                    w2iT[:, kc * D + 512 * half:kc * D + 512 * (half + 1)],
                                    start=(kc == 8 * qq and qq == 0),
                                    stop=(kc == 8 * qq + 7 and qq == 3))
                            return last
                        tms.append(self.cstep(
                            "tensor", [t_hqls[qq], t_w2,
                                       prev_evd.get((t - 2, half))],
                            dmmq, tg + "w2m"))
                    tm = tms[-1]

                    def evd(e, t=t, half=half, par=par, newt=newt):
                        e.scalar_tensor_tensor(
                            newt[:, 512 * half:512 * (half + 1)], pd[par][:],
                            dqq[:, t:t + 1], xld[t % 2][:, 512 * half:512 * (half + 1)],
                            ALU.mult, ALU.add)
                    tev = self.cstep("vector", [tm, t_xl, t_dqq], evd, tg + "evd")
                    prev_evd[(t, half)] = tev
                    tevs.append(tev)
                prev_evd[t % 2] = tevs[-1]
                self.stream[t] = newt
                if not first and self.n_sub == 4:
                    tevs = [self.dstep("sync", tevs, lambda e, dma, t=t, newt=newt:
                                       dma(out=self.y_h[128 * t:128 * (t + 1), :],
                                           in_=newt[:]), tg + "ydma")]
                newts[t] = tevs
            prev_hql = prev_evd[(4 * g + 3, 1)]
        self.t_stream_done = [newts[t][-1] for t in range(NTT)]
        self.free(*pd, dqq, *xld, hqfm, after=self.t_stream_done)
        self.free(w2iT, after=self.t_stream_done)
        self.free(dqh, qs, halo, after=[t_dqq] + t_xq)

    def sl3(self):
        tg = "s3"
        def bcast(e, dma):
            dma(out=self.wbf["bWqkv"], in_=self.w["bWqkv"])
            dma(out=self.wbf["bWout"], in_=self.w["bWout"])
        t_bc = self.dstep("gpsimd", [], bcast, tg + "cast")

        t_ln3 = self.ln_pass(self.ln_sp, spill_x=False, tag="l3")

        WqkvT = self.sb(tg + "WqkvT", [128, 8 * 3 * D], BF16)
        WoutT = self.sb(tg + "WoutT", [128, 8 * D], BF16)

        def wld(e, dma):
            dma(out=WqkvT[:].rearrange("p (c n) -> p c n", c=8),
                in_=self.wbf["bWqkv"], transpose=True)
            dma(out=WoutT[:].rearrange("p (c n) -> p c n", c=8),
                in_=self.wbf["bWout"], transpose=True)
        t_w = self.tstep([t_bc], wld, tg + "wT")

        ln3g = self.sb(tg + "lng", [128, 8 * TG], BF16)
        qg = self.sb(tg + "qg", [128, 8 * TG], BF16)
        kg = self.sb(tg + "kg", [128, 8 * TG], BF16)
        vg = self.sb(tg + "vg", [128, 4 * D], BF16)
        esb = self.sb(tg + "es", [128, 256], F32)
        sm3 = self.sb(tg + "sm", [128, 8], F32)
        attb = self.sb(tg + "att", [128, 256], BF16)
        attT = self.sb(tg + "attT", [128, 256], BF16)
        og = self.sb(tg + "og", [128, 8 * TG], BF16)
        xld3 = [self.sb(tg + "xl", [128, D], F32) for _ in range(2)]
        pA = [self.ps(tg + "pA", [128, 512]) for _ in range(2)]
        psc = [self.ps(tg + "psc", [128, 512]) for _ in range(2)]
        ptE = self.ps(tg + "ptE", [128, 256], BF16)
        ptO = self.ps(tg + "ptO", [128, 256], BF16)
        po = [self.ps(tg + "po", [128, 512]) for _ in range(2)]

        prev_lnld = None
        prev_pA = {}
        prev_psc = {}
        prev_po = {}
        prev_pt = None
        prev_res = {}
        for g in range(NG):
            t_lng = self.tstep(t_ln3 + [prev_lnld], lambda e, dma, g=g: dma(
                out=ln3g[:].rearrange("p (c n) -> p c n", c=8),
                in_=self.ln_sp[TG * g:TG * (g + 1), :], transpose=True),
                tg + "lnl")
            t_qkv = []
            for kind in range(3):
                if kind < 2:
                    dst = qg if kind == 0 else kg
                    fo0 = D * kind
                    for oc in range(8):
                        par = oc % 2

                        def mm(e, fo0=fo0, oc=oc, par=par):
                            last = None
                            for kc in range(8):
                                last = e.matmul(
                                    pA[par][:],
                                    WqkvT[:, kc * 3 * D + fo0 + 128 * oc:
                                          kc * 3 * D + fo0 + 128 * (oc + 1)],
                                    ln3g[:, kc * TG:(kc + 1) * TG],
                                    start=(kc == 0), stop=(kc == 7))
                            return last
                        tm = self.cstep("tensor", [t_lng, t_w, prev_pA.get(par),
                                                   self.t_cols], mm, tg + "qkm")
                        bcol = self.cols[:, 16 + 8 * kind + oc:17 + 8 * kind + oc]

                        def ev(e, dst=dst, oc=oc, par=par, bcol=bcol):
                            e.tensor_scalar(dst[:, oc * TG:(oc + 1) * TG],
                                            pA[par][:], bcol, None, ALU.add)
                        tev = self.cstep("vector", [tm], ev, tg + "qke")
                        prev_pA[par] = tev
                        t_qkv.append(tev)
                else:
                    for tt in range(4):
                        for half in range(2):
                            par = (tt * 2 + half) % 2

                            def mmv(e, tt=tt, half=half, par=par):
                                for kc in range(8):
                                    e.matmul(
                                        pA[par][:],
                                        ln3g[:, kc * TG + 128 * tt:
                                             kc * TG + 128 * (tt + 1)],
                                        WqkvT[:, kc * 3 * D + 2 * D + 512 * half:
                                              kc * 3 * D + 2 * D + 512 * (half + 1)],
                                        start=(kc == 0), stop=False)
                                return e.matmul(
                                    pA[par][:], self.onesr_bf[0:1, :],
                                    self.rows2_bf[0:1, 2 * D + 512 * half:
                                                  2 * D + 512 * (half + 1)],
                                    start=False, stop=True)
                            tm = self.cstep("tensor",
                                            [t_lng, t_w, prev_pA.get(par),
                                             self.t_rowsbf, self.t_const],
                                            mmv, tg + "vm")

                            def evv(e, tt=tt, half=half, par=par):
                                e.copy(vg[:, tt * D + 512 * half:
                                          tt * D + 512 * (half + 1)], pA[par][:])
                            tev = self.cstep("scalar", [tm], evv, tg + "vev")
                            prev_pA[par] = tev
                            t_qkv.append(tev)

            t_oev = []
            for h in range(H):
                par = h % 2
                pb = 64 * (h % 2)
                fc = h // 2

                def smm(e, h=h, par=par, pb=pb, fc=fc):
                    last = None
                    for b in range(8):
                        last = e.matmul(
                            psc[par][64 * (b % 2):64 * (b % 2) + 64,
                                     64 * (b // 2):64 * (b // 2) + 64],
                            qg[pb:pb + 64, fc * TG + 64 * b:fc * TG + 64 * (b + 1)],
                            kg[pb:pb + 64, fc * TG + 64 * b:fc * TG + 64 * (b + 1)],
                            start=(b < 2), stop=(b >= 6))
                    return last
                def smm2(e, h=h, par=par, pb=pb, fc=fc):
                    z = self.zbig
                    e.matmul(psc[par][:], z[pb:pb + 64, 0:128], z[pb:pb + 64, 0:512],
                             start=True, stop=False)
                    for b in range(8):
                        e.matmul(
                            psc[par][64 * (b % 2):64 * (b % 2) + 64,
                                     64 * (b // 2):64 * (b // 2) + 64],
                            qg[pb:pb + 64, fc * TG + 64 * b:fc * TG + 64 * (b + 1)],
                            kg[pb:pb + 64, fc * TG + 64 * b:fc * TG + 64 * (b + 1)],
                            start=False, stop=False)
                    return e.matmul(psc[par][:], z[pb:pb + 64, 0:128],
                                    z[pb:pb + 64, 0:512],
                                    start=False, stop=True)
                tsc = self.cstep("tensor", [prev_psc.get(par), self.t_const]
                                 + t_qkv, smm2, tg + "sc")

                def vmax(e, par=par):
                    e.tensor_reduce(sm3[:, 0:1], psc[par][:, 0:256], AX.X,
                                    ALU.max)
                    e.tensor_scalar(sm3[:, 1:2], sm3[:, 0:1], -0.125, None,
                                    ALU.mult)
                tmx = self.cstep("vector", [tsc], vmax, tg + "mx")

                def vexp(e, par=par):
                    e.activation(esb[:], psc[par][:, 0:256], AF.Exp,
                                 scale=0.125, bias=sm3[:, 1:2])
                tex = self.cstep("scalar", [tmx], vexp, tg + "exp")

                def vsum(e):
                    e.tensor_reduce(sm3[:, 4:8], esb[:].rearrange(
                        "p (s k) -> p s k", k=64), AX.X, ALU.add)
                    e.reciprocal(sm3[:, 4:8], sm3[:, 4:8])
                    rv = sm3[:, 4:8].unsqueeze(2).broadcast_to((128, 4, 64))
                    e.tensor_tensor(attb[:].rearrange("p (s k) -> p s k", k=64),
                                    esb[:].rearrange("p (s k) -> p s k", k=64),
                                    rv, ALU.mult)
                tat = self.cstep("vector", [tex], vsum, tg + "sm")

                def trans(e, h=h):
                    last = None
                    for b in range(8):
                        pt = ptE if b % 2 == 0 else ptO
                        last = e.transpose(
                            pt[64 * (b % 2):64 * (b % 2) + 64,
                               64 * (b // 2):64 * (b // 2) + 64],
                            attb[64 * (b % 2):64 * (b % 2) + 64,
                                 64 * (b // 2):64 * (b // 2) + 64],
                            self.ident64[64 * (b % 2):64 * (b % 2) + 64, :])
                    return last
                ttr = self.cstep("tensor", [tat, prev_pt, self.t_id64], trans,
                                 tg + "tr")

                def evtr(e):
                    e.copy(attT[0:64, :], ptE[0:64, :])
                    e.copy(attT[64:128, :], ptO[64:128, :])
                t_aT = self.cstep("scalar", [ttr], evtr, tg + "evT")
                prev_pt = t_aT

                def omm(e, h=h, par=par, fc=fc):
                    z = self.zbig
                    e.matmul(po[par][:], z[:, 0:128], z[:, 0:512],
                             start=True, stop=False)
                    for j, b in enumerate([0, 2, 4, 6, 1, 3, 5, 7]):
                        if j == 4:
                            e.matmul(po[par][:, 0:1], z[:, 0:128], z[:, 0:1],
                                     start=False, stop=False)
                        pb_b = 64 * (b % 2)
                        e.matmul(
                            po[par][64 * (h % 2):64 * (h % 2) + 64,
                                    64 * b:64 * (b + 1)],
                            vg[pb_b:pb_b + 64,
                               (b // 2) * D + 64 * h:(b // 2) * D + 64 * (h + 1)],
                            attT[pb_b:pb_b + 64, 64 * (b // 2):64 * (b // 2) + 64],
                            start=False, stop=False)
                    return e.matmul(po[par][:, 0:1], z[:, 0:128], z[:, 0:1],
                                    start=False, stop=True)
                tom = self.cstep("tensor", [t_aT, prev_po.get(par)], omm,
                                 tg + "om")

                def evo(e, h=h, par=par, fc=fc):
                    e.tensor_copy(og[64 * (h % 2):64 * (h % 2) + 64,
                                     fc * TG:(fc + 1) * TG],
                                  po[par][64 * (h % 2):64 * (h % 2) + 64, :])
                teo = self.cstep("vector", [tom], evo, tg + "evo")
                prev_po[par] = teo
                prev_psc[par] = tat
                t_oev.append(teo)

            for tt in range(4):
                t = 4 * g + tt
                par = tt % 2

                def xcp(e, t=t, par=par):
                    e.tensor_copy(xld3[par][:], self.stream[t][:])
                t_xc = self.cstep("gpsimd", [prev_res.get(par)], xcp, tg + "xc")
                newt = self.stream[t]
                tevs = []
                for half in range(2):
                    parp = half

                    def pmm(e, tt=tt, half=half, parp=parp):
                        last = None
                        for kc in range(8):
                            last = e.matmul(
                                pA[parp][:],
                                og[:, kc * TG + 128 * tt:kc * TG + 128 * (tt + 1)],
                                WoutT[:, kc * D + 512 * half:kc * D + 512 * (half + 1)],
                                start=(kc == 0), stop=False)
                        return e.matmul(
                            pA[parp][:], self.onesr_bf[0:1, :],
                            self.rows_bf[0:1, 4 * D + 512 * half:
                                         4 * D + 512 * (half + 1)],
                            start=False, stop=True)
                    tm = self.cstep("tensor", [prev_pA.get(parp), self.t_rowsbf]
                                    + t_oev, pmm, tg + "pm")

                    def pev(e, t=t, half=half, parp=parp, par=par, newt=newt):
                        e.tensor_tensor(newt[:, 512 * half:512 * (half + 1)],
                                        pA[parp][:],
                                        xld3[par][:, 512 * half:512 * (half + 1)],
                                        ALU.add)
                    tev = self.cstep("vector", [tm, t_xc], pev, tg + "pev")
                    prev_pA[parp] = tev
                    tevs.append(tev)
                prev_res[par] = tevs[-1]
            prev_lnld = t_qkv[-1]
        self.t_stream_done = [prev_res[0], prev_res[1]]
        self.free(*po, ptO, ptE, *psc, *pA, *xld3, og, attT, attb, sm3, esb,
                  vg, kg, qg, ln3g, WoutT, WqkvT,
                  after=self.t_stream_done + list(prev_pA.values()))

    def finish(self):
        if self.n_sub == 4:
            w1 = self._resolve("sync", self.t_stream_done)
            self.rawstep("sync", lambda e: [e.wait_ge(s_.h, v) for s_, v in w1]
                         or e.nop(), "fin")
            return
        if self.stream[0] is None:
            zt = self.sb("zfin", [128, D], F32)
            tz = self.cstep("vector", [], lambda e: e.memset(zt[:], 0.0), "zfin")

            def yz(e, dma):
                for t in range(NTT):
                    dma(out=self.y_h[128 * t:128 * (t + 1), :], in_=zt[:])
            toks = [self.dstep("sync", [tz] + getattr(self, "_stage_toks", []),
                               yz, "yz")]

            w0 = self._resolve("sync", toks)
            self.rawstep("sync", lambda e: [e.wait_ge(s_.h, v) for s_, v in w0],
                         "fin0")
            return
        if True:
            def yspl(e, dma):
                for t in range(NTT):
                    dma(out=self.y_h[128 * t:128 * (t + 1), :], in_=self.stream[t][:])
            deps = getattr(self, "t_stream_done", []) + \
                getattr(self, "_stage_toks", [])
            toks = [self.dstep("sync", deps, yspl, "yspl")]

        w1 = self._resolve("sync", toks)
        self.rawstep("sync", lambda e: [e.wait_ge(s_.h, v) for s_, v in w1],
                     "fin")


def build(n_sub=4, stage=99):
    return KB(n_sub, stage).build()


_IN_MAP = {
    "aWq": "L0_attn_a_Wq", "abq": "L0_attn_a_bq", "aWk": "L0_attn_a_Wk",
    "abk": "L0_attn_a_bk", "aWv": "L0_attn_a_Wv", "abv": "L0_attn_a_bv",
    "aWo": "L0_attn_a_Wo", "abo": "L0_attn_a_bo",
    "bWqkv": "L0_attn_b_W_qkv", "bbqkv": "L0_attn_b_b_qkv",
    "bWout": "L0_attn_b_W_out", "bbout": "L0_attn_b_b_out",
    "fw1": "L0_ffn_w1", "fcb": "L0_ffn_conv_b", "fw2": "L0_ffn_w2",
}


def _run(inputs, n_sub=4, stage=99):
    x = np.ascontiguousarray(np.asarray(inputs["L0_attn_a_x"],
                                        dtype=np.float32).reshape(8 * T, D))
    nc = build(n_sub, stage)
    base = {k: np.ascontiguousarray(np.asarray(inputs[v], np.float32))
            for k, v in _IN_MAP.items()}
    base["fcw"] = np.ascontiguousarray(
        np.asarray(inputs["L0_ffn_conv_w"], np.float32).reshape(3 * DHID))
    for k in list(base):
        if base[k].ndim > 1 and k not in ("fcw",):
            base[k] = base[k].reshape(base[k].shape[-2], base[k].shape[-1]) \
                if base[k].ndim == 2 else base[k]
    in_maps = []
    for c in range(8):
        m = dict(base)
        m["x"] = x[T * c:T * (c + 1)]
        pm = np.zeros((128, 8), np.float32)
        pm[:, :c] = 1.0
        m["pmask"] = pm
        hs = np.zeros((16, 2), np.float32)
        if c > 0:
            hs[2 * (c - 1) + 1, 0] = 1.0
        if c < 7:
            hs[2 * (c + 1), 1] = 1.0
        m["hsel"] = hs
        in_maps.append(m)
    res = run_bass_kernel_spmd(nc, in_maps, core_ids=list(range(8)))
    y = np.concatenate([res.results[c]["y"] for c in range(8)], axis=0)
    return y.reshape(1, 8 * T, D).astype(np.float32)


def kernel(**inputs):
    return _run(inputs, n_sub=4)
